# version 23
# baseline (speedup 1.0000x reference)
import numpy as np
import ml_dtypes
from contextlib import ExitStack

import concourse.bass as bass
import concourse.mybir as mybir
from concourse.bass_utils import run_bass_kernel_spmd
from concourse import library_config

F32 = mybir.dt.float32
BF16 = mybir.dt.bfloat16
AF = mybir.ActivationFunctionType
ALU = mybir.AluOpType
AXX = mybir.AxisListType.X

T = 2048
DM = 1024
NT = T // 128
NC = DM // 128
H = 16
HD = 64
DI = 2048
DS = 16
DTR = 64
INNER = 4096
NEGBIG = -1.0e30
SIGBIG = 1.0e30

ENGS = ("sync", "tensor", "vector", "scalar", "gpsimd")

STOP_AFTER = "full"


class Sched:
    def __init__(self, nc, es):
        self.nc = nc
        self.es = es
        self.sems = {}
        self.done = {}
        self.q = {e: [] for e in ENGS}
        self.last_wait = {c: {} for c in ENGS}
        for e in ENGS:
            self._newsem(e)

    def _newsem(self, key):
        self.sems[key] = self.es.enter_context(self.nc.semaphore("sem_" + str(key)))
        self.done[key] = 0

    def emit(self, eng, fn, inc=1, key=None):
        key = key or eng
        if inc:
            if key not in self.sems:
                self._newsem(key)
            sem = self.sems[key]
            self.done[key] += inc
            def wrapped(h, fn=fn, sem=sem, inc=inc):
                fn(h).then_inc(sem, inc)
            self.q[eng].append(wrapped)
        else:
            self.q[eng].append(fn)
        return (key, self.done.get(key, 0))

    def dma(self, eng, slot, out_ap, in_ap):
        return self.emit(eng, lambda h, o=out_ap, i=in_ap: h.dma_start(out=o, in_=i),
                         inc=16, key=slot)

    def token(self, key):
        return (key, self.done.get(key, 0))

    def wait(self, consumer, tok):
        if tok is None:
            return
        key, val = tok
        if val <= 0:
            return
        if self.last_wait[consumer].get(key, 0) >= val:
            return
        self.last_wait[consumer][key] = val
        sem = self.sems[key]
        self.q[consumer].append(lambda h, sem=sem, val=val: h.wait_ge(sem, val))

    def bump(self, eng):
        sem = self.sems[eng]
        self.done[eng] += 1
        self.q[eng].append(lambda h, sem=sem: h.sem_inc(sem, 1))
        return (eng, self.done[eng])

    def barrier(self):
        toks = [(k, v) for k, v in self.done.items() if v > 0]
        for c in ENGS:
            for t in toks:
                self.wait(c, t)


def build_program(stop_after=STOP_AFTER):
    nc = bass.Bass("TRN2", target_bir_lowering=False)

    def din(name, shape, dtype=F32):
        return nc.dram_tensor(name, shape, dtype, kind="ExternalInput").ap()

    x0_h = din("x0", [T, DM])
    W_attn_h = din("W_attn", [DM, 3 * DM])
    b_attn_h = din("b_attn", [3 * DM])
    W_proj_h = din("W_proj", [DM, DM])
    b_proj_h = din("b_proj", [DM])
    Wg_h = din("Wg", [DM, 8])
    bg_h = din("bg", [8])
    W1_h = din("W1", [8, DM, DM])
    b1_h = din("b1", [8, DM])
    W2_h = din("W2", [8, DM, DM])
    b2_h = din("b2", [8, DM])
    W_in_h = din("W_in", [DM, 2 * DI])
    W_xproj_h = din("W_xproj", [DI, DTR + 2 * DS])
    W_dt_h = din("W_dt", [DTR, DI])
    b_dt_h = din("b_dt", [DI])
    A_log_h = din("A_log", [DI, DS])
    Dvec_h = din("Dvec", [DI])
    W_out_h = din("W_out", [DI, DM])
    w_gate_h = din("w_gate", [INNER, DM])
    w_up_h = din("w_up", [INNER, DM])
    w_down_h = din("w_down", [DM, INNER])

    ident_bf_h = din("ident_bf", [128, 128], BF16)
    ident_f_h = din("ident_f", [128, 128], F32)
    alibi_h = din("alibi_base", [128, T])
    ones_bf_h = din("ones_bf", [1, 128], BF16)
    ones_f_h = din("ones_f", [1, 128], F32)

    out_h = nc.dram_tensor("out", [T, DM], F32, kind="ExternalOutput").ap()

    x1_scr = nc.dram_tensor("x1_scr", [T, DM], F32, kind="Internal").ap()
    x2_scr = nc.dram_tensor("x2_scr", [T, DM], F32, kind="Internal").ap()
    x3_scr = nc.dram_tensor("x3_scr", [T, DM], F32, kind="Internal").ap()
    wout_scr = nc.dram_tensor("wout_scr", [DI, DM], BF16, kind="Internal").ap()
    gu_scr = nc.dram_tensor("gu_scr", [NT * 8 * 128, 512], BF16, kind="Internal").ap()

    es = ExitStack()
    SB = lambda nm, shp, dt=F32: es.enter_context(nc.sbuf_tensor(nm, shp, dt))
    PS = lambda nm, shp, dt=F32: es.enter_context(nc.psum_tensor(nm, shp, dt))
    s = Sched(nc, es)

    def dma(eng, out_ap, in_ap, slot="misc"):
        return s.dma(eng, slot, out_ap, in_ap)

    _constv_cache = {}
    _constv_toks = {}
    def constv(val):
        if val not in _constv_cache:
            t = SB("constv_%d" % len(_constv_cache), [128, 1], F32)
            _constv_toks[val] = s.emit("vector", lambda h, t=t, v=val: h.memset(t[:], v), inc=1)
            _constv_cache[val] = t
        return _constv_cache[val][:]

    for _v in (1.0e-5, 1.0):
        constv(_v)

    ident_bf = SB("c_ident_bf", [128, 128], BF16)
    ident_f = SB("c_ident_f", [128, 128], F32)
    ones_bf = SB("c_ones_bf", [1, 128], BF16)
    ones_f = SB("c_ones_f", [1, 128], F32)
    dma("sync", ones_f[:], ones_f_h, "consts")
    dma("sync", ident_bf[:], ident_bf_h, "consts")
    dma("sync", ident_f[:], ident_f_h, "consts")
    consts_tok = dma("sync", ones_bf[:], ones_bf_h, "consts")

    ln_m1 = SB("ln_m1", [128, 1]); ln_m2 = SB("ln_m2", [128, 1])
    ln_mean = SB("ln_mean", [128, 1]); ln_ex2 = SB("ln_ex2", [128, 1])
    ln_var = SB("ln_var", [128, 1]); ln_sd = SB("ln_sd", [128, 1])
    ln_rstd = SB("ln_rstd", [128, 1]); ln_nmr = SB("ln_nmr", [128, 1])
    ln_sq_holder = {}

    def layernorm(r_ap, out_f32_ap, out_bf_ap, width=DM):
        t0 = s.emit("vector", lambda h: h.reduce_sum(out=ln_m1[:], in_=r_ap, axis=AXX))
        s.wait("vector", t0)
        v1 = s.emit("vector", lambda h: h.tensor_scalar(
            out=ln_mean[:], in0=ln_m1[:], scalar1=1.0 / width, scalar2=None, op0=ALU.mult))
        ln_sq = ln_sq_holder["cur"]
        sqt = s.emit("scalar", lambda h, ln_sq=ln_sq: h.activation(
            out=ln_sq[:, :width], in_=r_ap, func=AF.Square, accum_out=ln_m2[:]))
        s.wait("vector", sqt)
        t1 = s.emit("vector", lambda h: h.tensor_scalar(
            out=ln_ex2[:], in0=ln_m2[:], scalar1=1.0 / width, scalar2=None, op0=ALU.mult))
        s.wait("vector", v1)
        t2 = s.emit("vector", lambda h: h.tensor_tensor(
            out=ln_var[:], in0=ln_mean[:], in1=ln_mean[:], op=ALU.mult))
        s.wait("vector", t1); s.wait("vector", t2)
        v2 = s.emit("vector", lambda h: h.tensor_tensor(
            out=ln_var[:], in0=ln_ex2[:], in1=ln_var[:], op=ALU.subtract))
        s.wait("scalar", v2)
        eps = constv(1.0e-5)
        s3 = s.emit("scalar", lambda h, eps=eps: h.activation(
            out=ln_sd[:], in_=ln_var[:], func=AF.Sqrt, bias=eps))
        s.wait("vector", s3)
        t3 = s.emit("vector", lambda h: h.reciprocal(out=ln_rstd[:], in_=ln_sd[:]))
        s.wait("vector", t3)
        v3 = s.emit("vector", lambda h: h.tensor_scalar(
            out=ln_nmr[:], in0=ln_mean[:], scalar1=ln_rstd[:], scalar2=-1.0,
            op0=ALU.mult, op1=ALU.mult))
        s.wait("vector", v3)
        sc_tok = None
        vec_tok = None
        if out_f32_ap is not None:
            sc_tok = s.emit("vector", lambda h: h.tensor_scalar(
                out=out_f32_ap, in0=r_ap, scalar1=ln_rstd[:], scalar2=ln_nmr[:],
                op0=ALU.mult, op1=ALU.add))
        if out_bf_ap is not None:
            vec_tok = s.emit("vector", lambda h: h.tensor_scalar(
                out=out_bf_ap, in0=r_ap, scalar1=ln_rstd[:], scalar2=ln_nmr[:],
                op0=ALU.mult, op1=ALU.add))
        return vec_tok, sc_tok

    es_ab = ExitStack()
    x1T = es_ab.enter_context(nc.sbuf_tensor("xT_hold", [128, NC * T], BF16))

    es_a = ExitStack()
    SBa = lambda nm, shp, dt=F32: es_a.enter_context(nc.sbuf_tensor(nm, shp, dt))
    PSa = lambda nm, shp, dt=F32: es_a.enter_context(nc.psum_tensor(nm, shp, dt))
    qkT = SBa("qkT", [128, 16 * T], BF16)
    VW = H * 65
    v_store = SBa("v_store", [128, NT * VW], BF16)

    es_a1 = ExitStack()
    SB1 = lambda nm, shp, dt=F32: es_a1.enter_context(nc.sbuf_tensor(nm, shp, dt))
    PS1 = lambda nm, shp, dt=F32: es_a1.enter_context(nc.psum_tensor(nm, shp, dt))
    x0T = SB1("x0T", [128, NC * T], BF16)
    wqk = SB1("wqk", [128, NC * DM], BF16)
    wv = wqk
    wstage = SB1("wstageA", [128, 4 * DM], F32)
    xnat = wstage
    battn_qk = SB1("battn_qk", [128, 24], F32)
    bvb = SB1("bvb", [128, DM], F32)
    brow = SB1("brow_misc", [16, 128], F32)
    p_tr = PS1("p_trA", [128, 512], F32)
    p_mm = PS1("p_mmA", [128, 1024], F32)
    p_small = PS1("p_smallA", [128, 128], F32)

    x0_tiled = x0_h.rearrange("(c p) d -> c p d", p=128)
    W_attn_t = W_attn_h.rearrange("(c p) d -> c p d", p=128)

    tr_use = [None, None, None, None]
    ev_tok = None
    for tc in range(NT):
        buf = xnat[:, (tc % 4) * DM:(tc % 4) * DM + DM]
        s.wait("sync", tr_use[tc % 4])
        ld = dma("sync", buf, x0_tiled[tc], "xnat%d" % (tc % 4))
        s.wait("tensor", ld)
        s.wait("tensor", consts_tok)
        for g in range(2):
            s.wait("tensor", ev_tok)
            for k in range(4):
                cc = g * 4 + k
                tt = s.emit("tensor", lambda h, b=buf, cc=cc, k=k: h.transpose(
                    out=p_tr[:, k * 128:(k + 1) * 128], in_=b[:, cc * 128:(cc + 1) * 128],
                    identity=ident_f[:]), inc=(1 if k == 3 else 0))
            s.wait("scalar", tt)
            dst = x0T[:].rearrange("p (c t) -> p c t", t=T)[:, g * 4:(g + 1) * 4,
                                                            tc * 128:(tc + 1) * 128]
            ev_tok = s.emit("scalar", lambda h, d=dst: h.activation(
                out=d, in_=p_tr[:].rearrange("p (c t) -> p c t", t=128), func=AF.Copy), inc=1)
        tr_use[tc % 4] = s.token("tensor")
    x0T_done = s.token("scalar")

    ld = dma("sync", brow[:, :], b_attn_h[0:2048].rearrange("(a p) -> a p", p=128), "b_brow")
    s.wait("tensor", ld)
    s.wait("tensor", x0T_done)
    tt = s.emit("tensor", lambda h: h.transpose(out=p_small[:, 0:16], in_=brow[:, :],
                                                identity=ident_f[0:16, 0:16]), inc=1)
    s.wait("scalar", tt)
    battn_tok = s.emit("scalar", lambda h: h.activation(
        out=battn_qk[:, 0:16], in_=p_small[:, 0:16], func=AF.Copy), inc=1)
    s.wait("vector", battn_tok)
    battn_tok = s.emit("vector", lambda h: h.tensor_scalar(
        out=battn_qk[:, 16:24], in0=battn_qk[:, 0:8], scalar1=0.125, scalar2=None,
        op0=ALU.mult))
    bvb_tok = dma("sync", bvb[:], b_attn_h[2048:3072].unsqueeze(0).partition_broadcast(128), "b_bv")
    cvd = [tr_use[0], tr_use[1], tr_use[2], tr_use[3]]
    for cc in range(NC):
        buf = wstage[:, (cc % 4) * DM:(cc % 4) * DM + DM]
        s.wait("sync", cvd[cc % 4])
        ldw = dma("sync", buf, W_attn_t[cc][:, 2048:3072], "wstg%d" % (cc % 4))
        s.wait("vector", ldw)
        cvd[cc % 4] = s.emit("vector", lambda h, b=buf, cc=cc: h.tensor_copy(
            out=wv[:, cc * DM:(cc + 1) * DM], in_=b), inc=1)
    ones_set = s.emit("vector", lambda h: h.memset(
        v_store[:].rearrange("p (t w) -> p t w", w=VW)[:, :, 64::65], 1.0))
    evv = [None, None]
    s.wait("vector", bvb_tok)
    s.wait("tensor", s.token("vector"))
    s.wait("tensor", x0T_done)
    for tch in range(NT):
        for db in range(2):
            r = (tch * 2 + db) % 2
            pz = p_mm[:, r * 512:(r + 1) * 512]
            s.wait("tensor", evv[r])
            for cc in range(NC):
                mt = s.emit("tensor", lambda h, pz=pz, cc=cc, tch=tch, db=db: h.matmul(
                    out=pz, lhsT=x0T[:, cc * T + tch * 128: cc * T + (tch + 1) * 128],
                    rhs=wv[:, cc * DM + db * 512: cc * DM + (db + 1) * 512],
                    start=(cc == 0), stop=(cc == NC - 1)), inc=(1 if cc == NC - 1 else 0))
            s.wait("vector", mt)
            base = tch * VW + db * 520
            dst = v_store[:, base:base + 520].rearrange("p (h e) -> p h e", e=65)[:, :, 0:64]
            evv[r] = s.emit("vector", lambda h, pz=pz, dst=dst, db=db: h.tensor_tensor(
                out=dst, in0=pz.rearrange("p (h e) -> p h e", e=64),
                in1=bvb[:, db * 512:(db + 1) * 512].rearrange("p (h e) -> p h e", e=64),
                op=ALU.add), inc=1)

    v_done = s.token("vector")

    ev = [evv[0], evv[1]]
    for half in range(2):
        tdone = s.token("tensor")
        cvd2 = [tdone, tdone, tdone, tdone]
        for cc in range(NC):
            buf = wstage[:, (cc % 4) * DM:(cc % 4) * DM + DM]
            s.wait("sync", cvd2[cc % 4])
            ldw = dma("sync", buf, W_attn_t[cc][:, half * 1024:(half + 1) * 1024], "wstg%d" % (cc % 4))
            s.wait("vector", ldw)
            s.wait("vector", tdone)
            cvd2[cc % 4] = s.emit("vector", lambda h, b=buf, cc=cc: h.tensor_copy(
                out=wqk[:, cc * DM:(cc + 1) * DM], in_=b), inc=1)
        wconv = s.token("vector")
        s.wait("tensor", wconv)
        s.wait("scalar", battn_tok)
        for cp8 in range(8):
            cp = half * 8 + cp8
            for tb in range(4):
                r = (cp8 * 4 + tb) % 2
                pz = p_mm[:, r * 512:(r + 1) * 512]
                s.wait("tensor", ev[r])
                for cc in range(NC):
                    mt = s.emit("tensor", lambda h, pz=pz, cc=cc, cp8=cp8, tb=tb: h.matmul(
                        out=pz, lhsT=wqk[:, cc * DM + cp8 * 128: cc * DM + (cp8 + 1) * 128],
                        rhs=x0T[:, cc * T + tb * 512: cc * T + (tb + 1) * 512],
                        start=(cc == 0), stop=(cc == NC - 1)), inc=(1 if cc == NC - 1 else 0))
                s.wait("vector", mt)
                s.wait("vector", battn_tok)
                if half == 0:
                    ev[r] = s.emit("vector", lambda h, pz=pz, cp=cp, tb=tb: h.tensor_scalar(
                        out=qkT[:, cp * T + tb * 512: cp * T + (tb + 1) * 512], in0=pz,
                        scalar1=0.125, scalar2=battn_qk[:, 16 + cp:17 + cp],
                        op0=ALU.mult, op1=ALU.add))
                else:
                    ev[r] = s.emit("vector", lambda h, pz=pz, cp=cp, tb=tb: h.tensor_scalar(
                        out=qkT[:, cp * T + tb * 512: cp * T + (tb + 1) * 512], in0=pz,
                        scalar1=battn_qk[:, cp:cp + 1], scalar2=None, op0=ALU.add))

    s.barrier()
    es_a1.close()

    es_a2a = ExitStack()
    attnoT = es_a2a.enter_context(nc.sbuf_tensor("attnoT", [128, NC * T], BF16))
    es_a2 = ExitStack()
    SB2 = lambda nm, shp, dt=F32: es_a2.enter_context(nc.sbuf_tensor(nm, shp, dt))
    PS2 = lambda nm, shp, dt=F32: es_a2.enter_context(nc.psum_tensor(nm, shp, dt))
    alibi = SB2("c_alibi", [128, T], F32)
    alibi_tok = dma("sync", alibi[:], alibi_h, "alibi")
    zbuf = SB2("zbuf", [128, 2 * T], F32)
    attbuf = SB2("attbuf", [128, 2 * T], BF16)
    rb64 = SB2("rb64", [64, T], F32)
    rrow = rb64[0:1, :]
    p_z = PS2("p_z", [128, T], F32)
    p_av = PS2("p_av", [65, T], F32)

    s.wait("tensor", v_done)
    s.wait("tensor", ones_set)
    zf_buf = [None, None]
    exp_sc = {}
    att_par = [None, None]
    avdone_tok = None
    for h_ in range(H):
        slope = 1.0 / (2.0 ** (h_ + 1))
        s.wait("vector", alibi_tok)
        s.wait("tensor", avdone_tok)
        row0 = 64 * (h_ % 2)
        qc = (h_ // 2) * T
        kc = (8 + h_ // 2) * T

        def emit_attv(sc):
            par = sc % 2
            tlen = T - sc * 128
            ab = attbuf[:, par * T: par * T + tlen]
            s.wait("tensor", exp_sc[sc])
            last_tok = None
            for tb in range(4):
                lo = max(tb * 512, sc * 128)
                hi = (tb + 1) * 512
                if lo >= hi:
                    continue
                last = (sc == NT - 1 and tb == 3)
                last_tok = s.emit("tensor", lambda h, sc=sc, h_=h_, lo=lo, hi=hi, ab=ab: h.matmul(
                    out=p_av[:, lo:hi],
                    lhsT=v_store[:, sc * VW + h_ * 65: sc * VW + h_ * 65 + 65],
                    rhs=ab[:, lo - sc * 128: hi - sc * 128],
                    start=(sc == 0), stop=(sc == min(4 * tb + 3, NT - 1)),
                    skip_group_check=True), inc=(1 if last else 0))
            att_par[par] = last_tok

        for sc in range(NT):
            tlen = T - sc * 128
            par = sc % 2
            zb = zbuf[:, par * T: par * T + tlen]
            ab = attbuf[:, par * T: par * T + tlen]
            nch = (tlen + 1023) // 1024
            ztoks = []
            for ch in range(nch):
                off = ch * 1024
                cw = min(1024, tlen - off)
                s.wait("tensor", zf_buf[ch])
                for tb in range(2):
                    cols = min(512, cw - tb * 512)
                    if cols <= 0:
                        continue
                    zt = s.emit("tensor", lambda h, row0=row0, kc=kc, qc=qc, sc=sc,
                                off=off, tb=tb, cols=cols: h.matmul(
                        out=p_z[:, off + tb * 512: off + tb * 512 + cols],
                        lhsT=qkT[row0:row0 + 64, kc + sc * 128: kc + (sc + 1) * 128],
                        rhs=qkT[row0:row0 + 64, qc + sc * 128 + off + tb * 512:
                                qc + sc * 128 + off + tb * 512 + cols],
                        start=True, stop=True, skip_group_check=True))
                ztoks.append(zt)
            if sc > 0:
                emit_attv(sc - 1)
            elif h_ > 0:
                pass
            zf_prev = None
            for ch in range(nch):
                off = ch * 1024
                cw = min(1024, tlen - off)
                s.wait("vector", ztoks[ch])
                s.wait("vector", exp_sc.get(sc - 2))
                zf_buf[ch] = s.emit("vector", lambda h, zb=zb, sl=slope, off=off, cw=cw: h.scalar_tensor_tensor(
                    out=zb[:, off:off + cw], in0=alibi[:, off:off + cw], scalar=sl,
                    in1=p_z[:, off:off + cw], op0=ALU.mult, op1=ALU.add))
                zf_prev = zf_buf[ch]
            s.wait("scalar", zf_prev)
            s.wait("scalar", att_par[par])
            exp_sc[sc] = s.emit("scalar", lambda h, zb=zb, ab=ab: h.activation(
                out=ab, in_=zb, func=AF.Exp))
        emit_attv(NT - 1)
        att_done = att_par[(NT - 1) % 2]
        s.wait("vector", att_done)
        rc = s.emit("vector", lambda h: h.reciprocal(out=rrow, in_=p_av[64:65, :]))
        s.wait("tensor", rc)
        s.wait("tensor", zf_buf[0]); s.wait("tensor", zf_buf[1])
        for tb in range(4):
            rbmm = s.emit("tensor", lambda h, tb=tb: h.matmul(
                out=p_z[0:64, tb * 512:(tb + 1) * 512], lhsT=ones_f[0:1, 0:64],
                rhs=rrow[0:1, tb * 512:(tb + 1) * 512], start=True, stop=True,
                skip_group_check=True), inc=(1 if tb == 3 else 0))
        s.wait("scalar", rbmm)
        rb = s.emit("scalar", lambda h: h.activation(out=rb64[:], in_=p_z[0:64, :],
                                                     func=AF.Copy))
        s.wait("vector", rb)
        dst = attnoT[row0:row0 + 64, qc: qc + T]
        avdone_tok = s.emit("vector", lambda h, dst=dst: h.tensor_tensor(
            out=dst, in0=p_av[0:64, :], in1=rb64[:], op=ALU.mult))
        zf_buf = [rb, rb]
    attno_done = s.token("vector")

    s.barrier()
    es_a2.close()

    es_a3 = ExitStack()
    SB3 = lambda nm, shp, dt=F32: es_a3.enter_context(nc.sbuf_tensor(nm, shp, dt))
    PS3 = lambda nm, shp, dt=F32: es_a3.enter_context(nc.psum_tensor(nm, shp, dt))
    wproj = SB3("wproj", [128, NC * DM], BF16)
    x1f = SB3("x1f", [128, DM], F32)
    r1buf = SB3("r1buf", [128, DM], F32)
    wstage3 = SB3("wstage3", [128, 2 * DM], F32)
    xnat3 = wstage3
    bpb = SB3("bpb", [128, DM], F32)
    ln_sq_holder["cur"] = SB3("ln_sqA", [128, DM], F32)
    p_trB = PS3("p_trB", [128, 512], BF16)
    p_p = PS3("p_p", [128, 2048], F32)
    bpb_tok = dma("sync", bpb[:], b_proj_h.unsqueeze(0).partition_broadcast(128), "b_bp")

    W_proj_t = W_proj_h.rearrange("(c p) d -> c p d", p=128)
    cvd = [None, None]
    for cc in range(NC):
        buf = wstage3[:, (cc % 2) * DM:(cc % 2) * DM + DM]
        s.wait("sync", cvd[cc % 2])
        ldw = dma("sync", buf, W_proj_t[cc], "wstg%d" % (cc % 2))
        s.wait("vector", ldw)
        cvd[cc % 2] = s.emit("vector", lambda h, b=buf, cc=cc: h.tensor_copy(
            out=wproj[:, cc * DM:(cc + 1) * DM], in_=b), inc=1)
    wproj_done = s.token("vector")

    x1_scr_t = x1_scr.rearrange("(c p) d -> c p d", p=128)
    out_t = out_h.rearrange("(c p) d -> c p d", p=128)

    s.wait("tensor", wproj_done)
    s.wait("tensor", attno_done)
    ev_p = None
    x0c_use = [wproj_done, wproj_done]
    x1guards = []
    ev_x1t = None
    prev_stok = None
    prev_trT = None
    x1bf2 = SB3("x1bf2", [128, 2 * DM], BF16)
    pend_tr = None
    prev_vtok = None
    ev_p_par = [None, None]
    trT_par = [None, None]
    for tch in range(NT):
        pp = p_p[:, (tch % 2) * 1024:(tch % 2) * 1024 + 1024]
        x0c = xnat3[:, (tch % 2) * DM:(tch % 2) * DM + DM]
        s.wait("sync", x0c_use[tch % 2])
        ldx = dma("sync", x0c, x0_tiled[tch], "x0c%d" % (tch % 2))
        s.wait("tensor", ev_p_par[tch % 2])
        for cc in range(NC):
            for db in range(2):
                mt = s.emit("tensor", lambda h, cc=cc, db=db, tch=tch, pp=pp: h.matmul(
                    out=pp[:, db * 512:(db + 1) * 512],
                    lhsT=attnoT[:, cc * T + tch * 128: cc * T + (tch + 1) * 128],
                    rhs=wproj[:, cc * DM + db * 512: cc * DM + (db + 1) * 512],
                    start=(cc == 0), stop=(cc == NC - 1)),
                    inc=(1 if cc == NC - 1 and db == 1 else 0))
        if pend_tr is not None:
            pend_tr()
        s.wait("vector", mt)
        s.wait("vector", bpb_tok)
        s.wait("vector", ldx)
        s.wait("vector", prev_stok)
        s.wait("vector", prev_vtok)
        rz = s.emit("vector", lambda h, pp=pp: h.tensor_tensor(
            out=r1buf[:], in0=pp, in1=bpb[:], op=ALU.add))
        ev_p_par[tch % 2] = rz
        s.wait("vector", rz)
        r1v = s.emit("vector", lambda h, x0c=x0c: h.tensor_tensor(
            out=r1buf[:], in0=r1buf[:], in1=x0c, op=ALU.add))
        x0c_use[tch % 2] = r1v
        s.wait("scalar", r1v)
        s.wait("vector", r1v)
        for g_ in (x1guards if tch > 0 else []):
            s.wait("scalar", g_); s.wait("vector", g_)
        xb = x1bf2[:, (tch % 2) * DM:(tch % 2) * DM + DM]
        s.wait("vector", trT_par[tch % 2])
        vtok, stok = layernorm(r1buf[:], x1f[:], xb)
        prev_stok = stok
        prev_vtok = vtok
        s.wait("sync", stok)
        w1 = dma("sync", x1_scr_t[tch], x1f[:], "x1w")
        x1guards = [w1]
        if stop_after == "x1":
            x1guards.append(dma("sync", out_t[tch], x1f[:], "outw"))

        def tr_tail(tch=tch, vtok=vtok, xb=xb):
            nonlocal ev_x1t, prev_trT
            s.wait("tensor", vtok)
            trT_par[tch % 2] = None
            for g in range(2):
                s.wait("tensor", ev_x1t)
                for k in range(4):
                    cc = g * 4 + k
                    tt = s.emit("tensor", lambda h, cc=cc, k=k, xb=xb: h.transpose(
                        out=p_trB[:, k * 128:(k + 1) * 128], in_=xb[:, cc * 128:(cc + 1) * 128],
                        identity=ident_bf[:]), inc=(1 if k == 3 else 0))
                s.wait("scalar", tt)
                dst = x1T[:].rearrange("p (c t) -> p c t", t=T)[:, g * 4:(g + 1) * 4,
                                                                tch * 128:(tch + 1) * 128]
                ev_x1t = s.emit("scalar", lambda h, d=dst: h.activation(
                    out=d, in_=p_trB[:].rearrange("p (c t) -> p c t", t=128), func=AF.Copy), inc=1)
            prev_trT = s.token("tensor")
            trT_par[tch % 2] = prev_trT

        pend_tr = tr_tail
    pend_tr()

    s.barrier()
    es_a3.close()
    es_a2a.close()
    es_a.close()

    if stop_after != "x1":
        es_b = ExitStack()
        SBb = lambda nm, shp, dt=F32: es_b.enter_context(nc.sbuf_tensor(nm, shp, dt))
        PSb = lambda nm, shp, dt=F32: es_b.enter_context(nc.psum_tensor(nm, shp, dt))
        acc = SBb("moe_acc", [128, NT * DM], F32)
        hTe = SBb("hTe", [128, 8 * T], BF16)
        w1b = SBb("w1b", [128, NC * DM], BF16)
        w2b = SBb("w2b", [128, NC * DM], BF16)
        wstgB = SBb("wstgB", [128, 5 * DM], F32)
        wg_bf = SBb("wg_bf", [128, NC * 8], BF16)
        bgb = SBb("bgb", [128, 8], F32)
        lg_all = SBb("lg_all", [128, NT * 8], F32)
        gw_all = SBb("gw_all", [128, NT * 8], F32)
        g_sc1 = SBb("g_sc1", [128, NT * 8], F32)
        g_sc2 = SBb("g_sc2", [128, NT], F32)
        g_sc3 = SBb("g_sc3", [128, NT], F32)
        g_sc4 = SBb("g_sc4", [128, NT], F32)
        b1s = SBb("b1s", [128, 8], F32)
        b1row = SBb("b1row", [8, 128], F32)
        b2bf = SBb("b2bf", [1, DM], BF16)
        b2row = SBb("b2row", [1, DM], F32)
        x1c = SBb("x1c", [128, 2 * DM], F32)
        x2f = SBb("x2f", [128, DM], F32)
        ln_sq_holder["cur"] = SBb("ln_sqB", [128, DM], F32)
        p_h = PSb("p_hB", [128, 1024], F32)
        p_eo = PSb("p_eoB", [128, 2048], F32)
        p_sm = PSb("p_smB", [128, 128], F32)

        wgld = dma("sync", wstgB[:, 0:64], Wg_h.rearrange("(c p) e -> p c e", p=128), "wstg0")
        s.wait("vector", wgld)
        wgcv = s.emit("vector", lambda h: h.tensor_copy(out=wg_bf[:], in_=wstgB[:, 0:64]))
        bgt = dma("sync", bgb[:], bg_h.unsqueeze(0).partition_broadcast(128), "b_bg")
        s.wait("tensor", wgcv)
        ev_g = None
        for tch in range(NT):
            for cc in range(NC):
                gt = s.emit("tensor", lambda h, cc=cc, tch=tch: h.matmul(
                    out=p_sm[:, 0:8], lhsT=x1T[:, cc * T + tch * 128: cc * T + (tch + 1) * 128],
                    rhs=wg_bf[:, cc * 8:(cc + 1) * 8],
                    start=(cc == 0), stop=(cc == NC - 1)), inc=(1 if cc == NC - 1 else 0))
            s.wait("vector", gt)
            s.wait("vector", bgt)
            ev_g = s.emit("vector", lambda h, tch=tch: h.tensor_tensor(
                out=lg_all[:, tch * 8:(tch + 1) * 8], in0=p_sm[:, 0:8], in1=bgb[:], op=ALU.add))
            s.wait("tensor", ev_g)
        lg3 = lg_all[:].rearrange("p (t e) -> p t e", e=8)
        s.wait("vector", ev_g)
        gmax = s.emit("vector", lambda h: h.reduce_max(out=g_sc2[:], in_=lg3, axis=AXX))
        s.wait("scalar", ev_g)
        gexp = s.emit("scalar", lambda h: h.activation(out=g_sc1[:], in_=lg_all[:], func=AF.Exp))
        s.wait("vector", gexp)
        gsum = s.emit("vector", lambda h: h.reduce_sum(
            out=g_sc3[:], in_=g_sc1[:].rearrange("p (t e) -> p t e", e=8), axis=AXX))
        s.wait("scalar", gmax)
        gelm = s.emit("scalar", lambda h: h.activation(out=g_sc4[:], in_=g_sc2[:], func=AF.Exp))
        s.wait("vector", gsum)
        grs = s.emit("vector", lambda h: h.reciprocal(out=g_sc3[:], in_=g_sc3[:]))
        s.wait("vector", gelm); s.wait("vector", grs)
        gw = s.emit("vector", lambda h: h.tensor_tensor(
            out=g_sc4[:], in0=g_sc4[:], in1=g_sc3[:], op=ALU.mult))
        s.wait("vector", gw)
        gmask = s.emit("vector", lambda h: h.tensor_tensor(
            out=gw_all[:].rearrange("p (t e) -> p t e", e=8), in0=lg3,
            in1=g_sc2[:].unsqueeze(2).broadcast_to([128, NT, 8]), op=ALU.is_ge))
        s.wait("vector", gmask)
        gwa = s.emit("vector", lambda h: h.tensor_tensor(
            out=gw_all[:].rearrange("p (t e) -> p t e", e=8),
            in0=gw_all[:].rearrange("p (t e) -> p t e", e=8),
            in1=g_sc4[:].unsqueeze(2).broadcast_to([128, NT, 8]), op=ALU.mult))

        W1_t = W1_h.rearrange("e (c p) d -> e c p d", p=128)
        W2_t = W2_h.rearrange("e (c p) d -> e c p d", p=128)
        acc_tok = [None] * NT

        stream_prev = {"cvt": wgcv, "ctr": 0}

        def stream_w(e_, dstbuf, Wt, base_tok, tag):
            cvt = [stream_prev["cvt"]] * 5
            first = True
            for cc in range(NC):
                r = (stream_prev["ctr"] + cc) % 5
                buf = wstgB[:, r * DM:(r + 1) * DM]
                s.wait("sync", cvt[r])
                ldt = dma("sync", buf, Wt[e_][cc], "wstg%d" % r)
                s.wait("vector", ldt)
                if first:
                    s.wait("vector", base_tok)
                    first = False
                cvt[r] = s.emit("vector", lambda h, b=buf, cc=cc, d=dstbuf: h.tensor_copy(
                    out=d[:, cc * DM:(cc + 1) * DM], in_=b))
            stream_prev["ctr"] = (stream_prev["ctr"] + NC) % 5
            stream_prev["cvt"] = s.token("vector")
            return stream_prev["cvt"]

        w1_tok = stream_w(0, w1b, W1_t, s.token("vector"), "w1")
        x1_rd = [None, None]
        x2guards = []
        prev_ln = None
        last_h_mm = None
        last_eo_mm = None
        ev_h = [None, None]
        ep_guard = [None, None]
        for e_ in range(8):
            b1ld = dma("sync", b1row[:], b1_h[e_].rearrange("(a p) -> a p", p=128), "b_b1")
            s.wait("tensor", b1ld)
            s.wait("tensor", s.token("scalar"))
            b1t = s.emit("tensor", lambda h: h.transpose(
                out=p_sm[:, 0:8], in_=b1row[:], identity=ident_f[0:8, 0:8]))
            s.wait("scalar", b1t)
            b1ev = s.emit("scalar", lambda h: h.activation(out=b1s[:], in_=p_sm[:, 0:8],
                                                           func=AF.Copy))
            w2_tok = stream_w(e_, w2b, W2_t, last_eo_mm, "w2")
            s.wait("tensor", w1_tok)
            for hc in range(NC):
                for tb in range(4):
                    r = (hc * 4 + tb) % 2
                    pz = p_h[:, r * 512:(r + 1) * 512]
                    s.wait("tensor", ev_h[r])
                    for cc in range(NC):
                        mt = s.emit("tensor", lambda h, pz=pz, cc=cc, hc=hc, tb=tb: h.matmul(
                            out=pz, lhsT=w1b[:, cc * DM + hc * 128: cc * DM + (hc + 1) * 128],
                            rhs=x1T[:, cc * T + tb * 512: cc * T + (tb + 1) * 512],
                            start=(cc == 0), stop=(cc == NC - 1)), inc=(1 if cc == NC - 1 else 0))
                    s.wait("scalar", mt)
                    s.wait("scalar", b1ev)
                    ev_h[r] = s.emit("scalar", lambda h, pz=pz, hc=hc, tb=tb: h.activation(
                        out=hTe[:, hc * T + tb * 512: hc * T + (tb + 1) * 512], in_=pz,
                        func=AF.Gelu, bias=b1s[:, hc:hc + 1]))
            h_done = s.token("scalar")
            last_h_mm = s.token("tensor")
            b2ld = dma("sync", b2row[:], b2_h[e_].unsqueeze(0), "b_b2")
            s.wait("vector", b2ld)
            b2cv = s.emit("vector", lambda h: h.tensor_copy(out=b2bf[:], in_=b2row[:]))
            if e_ < 7:
                w1_tok = stream_w(e_ + 1, w1b, W1_t, last_h_mm, "w1")
            s.wait("tensor", w2_tok)
            s.wait("tensor", h_done)
            s.wait("tensor", b2cv)
            s.wait("tensor", gwa)
            ep_par = [ep_guard[0], ep_guard[1]]
            for tch in range(NT):
                par = tch % 2
                peo = p_eo[:, par * 1024:(par + 1) * 1024]
                s.wait("tensor", ep_par[par])
                for db in range(2):
                    s.emit("tensor", lambda h, db=db, peo=peo: h.matmul(
                        out=peo[:, db * 512:(db + 1) * 512], lhsT=ones_bf[0:1, :],
                        rhs=b2bf[0:1, db * 512:(db + 1) * 512],
                        start=True, stop=False))
                    for hc in range(NC):
                        mt = s.emit("tensor", lambda h, hc=hc, db=db, tch=tch, peo=peo: h.matmul(
                            out=peo[:, db * 512:(db + 1) * 512],
                            lhsT=hTe[:, hc * T + tch * 128: hc * T + (tch + 1) * 128],
                            rhs=w2b[:, hc * DM + db * 512: hc * DM + (db + 1) * 512],
                            start=False, stop=(hc == NC - 1)),
                            inc=(1 if hc == NC - 1 and db == 1 else 0))
                s.wait("vector", mt)
                s.wait("vector", acc_tok[tch])
                gwcol = gw_all[:, tch * 8 + e_: tch * 8 + e_ + 1]
                if e_ == 0:
                    acc_tok[tch] = s.emit("vector", lambda h, tch=tch, gwcol=gwcol, peo=peo: h.tensor_scalar(
                        out=acc[:, tch * DM:(tch + 1) * DM], in0=peo,
                        scalar1=gwcol, scalar2=None, op0=ALU.mult))
                else:
                    acc_tok[tch] = s.emit("vector", lambda h, tch=tch, gwcol=gwcol, peo=peo: h.scalar_tensor_tensor(
                        out=acc[:, tch * DM:(tch + 1) * DM], in0=peo, scalar=gwcol,
                        in1=acc[:, tch * DM:(tch + 1) * DM], op0=ALU.mult, op1=ALU.add))
                ep_par[par] = acc_tok[tch]
                if e_ == 7:
                    x1cb = x1c[:, (tch % 2) * DM:(tch % 2) * DM + DM]
                    s.wait("sync", x1_rd[tch % 2])
                    s.wait("sync", s.token("x1w"))
                    ldx = dma("sync", x1cb, x1_scr_t[tch], "x1r%d" % (tch % 2))
                    s.wait("vector", ldx)
                    s.wait("vector", acc_tok[tch])
                    for g_ in x2guards:
                        s.wait("vector", g_); s.wait("scalar", g_)
                    s.wait("vector", prev_ln)
                    r2v = s.emit("vector", lambda h, tch=tch, x1cb=x1cb: h.tensor_tensor(
                        out=acc[:, tch * DM:(tch + 1) * DM], in0=acc[:, tch * DM:(tch + 1) * DM],
                        in1=x1cb, op=ALU.add))
                    x1_rd[tch % 2] = r2v
                    ep_par[par] = r2v
                    s.wait("scalar", r2v)
                    s.wait("vector", r2v)
                    vtok, stok = layernorm(acc[:, tch * DM:(tch + 1) * DM], x2f[:], None)
                    prev_ln = stok
                    s.wait("sync", stok)
                    w2_ = dma("sync", x2_scr.rearrange("(c p) d -> c p d", p=128)[tch],
                              x2f[:], "x2w")
                    x2guards = [w2_]
                    if stop_after == "x2":
                        x2guards.append(dma("sync", out_t[tch], x2f[:], "outw"))
            ep_guard = ep_par
            last_eo_mm = s.token("tensor")

        s.barrier()
        es_b.close()
        es_ab.close()

    if stop_after not in ("x1", "x2"):
        TC = 1024
        NK = T // TC
        NDH = 2
        NQ = 8
        NSUB = TC // 128
        es_c = ExitStack()
        SBc = lambda nm, shp, dt=F32: es_c.enter_context(nc.sbuf_tensor(nm, shp, dt))
        PSc = lambda nm, shp, dt=F32: es_c.enter_context(nc.psum_tensor(nm, shp, dt))
        bc_scr = nc.dram_tensor("bc_scr", [32, T], F32, kind="Internal").ap()
        win_scr = nc.dram_tensor("win_scr", [DM, 2 * DI], BF16, kind="Internal").ap()
        w_eff = SBc("w_eff", [128, NC * 96], BF16)
        w_dt = SBc("w_dt", [64, DI], BF16)
        dltT = SBc("dltT", [64, T], BF16)
        A_all = SBc("A_all", [128, 16 * DS], F32)
        D_all = SBc("D_all", [128, 16], F32)
        bdt_all = SBc("bdt_all", [128, 16], F32)
        h_state = SBc("h_state", [128, 16 * DS], F32)
        wo_stg = SBc("wo_stg", [128, 2 * DM], F32)
        wo_bf = SBc("wo_bf", [128, 2 * DM], BF16)
        ln_sq_holder["cur"] = SBc("ln_sqC", [128, DM], F32)

        W_in_t = W_in_h.rearrange("(c p) d -> c p d", p=128)
        win_scr_t = win_scr.rearrange("(c p) d -> c p d", p=128)
        W_out_t = W_out_h.rearrange("(c p) d -> c p d", p=128)
        wout_scr_t = wout_scr.rearrange("(c p) d -> c p d", p=128)

        es_c1 = ExitStack()
        SC1 = lambda nm, shp, dt=F32: es_c1.enter_context(nc.sbuf_tensor(nm, shp, dt))
        PC1 = lambda nm, shp, dt=F32: es_c1.enter_context(nc.psum_tensor(nm, shp, dt))
        winT = SC1("winT", [128, 16 * DM], BF16)
        wpc = SC1("wpcC", [128, 2 * 4096], BF16)
        wstgC = SC1("wstgC", [128, 4 * 2048], F32)
        wxp_bf = SC1("wxp_bf", [128, 16 * 96], BF16)
        smallr = SC1("smallr", [16, 128], F32)
        p_trC = PC1("p_trC", [128, 512], BF16)
        p_we = PC1("p_weC", [128, 96], F32)

        tv0 = s.token("vector")
        cvtC = [tv0, tv0, tv0, tv0]
        wst = [None, None]
        rdC = [None, None]
        ev_wt = None
        for cc in range(NC):
            cp = cc % 2
            for hf in range(2):
                r = (cc * 2 + hf) % 4
                buf = wstgC[:, r * 2048:(r + 1) * 2048]
                s.wait("sync", cvtC[r])
                ldt = dma("sync", buf, W_in_t[cc][:, hf * 2048:(hf + 1) * 2048],
                          "wstg%d" % r)
                s.wait("vector", ldt)
                s.wait("vector", wst[cp])
                s.wait("vector", rdC[cp])
                cvtC[r] = s.emit("vector", lambda h, b=buf, hf=hf, cp=cp: h.tensor_copy(
                    out=wpc[:, cp * 4096 + hf * 2048: cp * 4096 + (hf + 1) * 2048],
                    in_=b))
            r0 = (cc * 2) % 4
            r1 = (cc * 2 + 1) % 4
            s.wait("sync", cvtC[r0]); s.wait("sync", cvtC[r1])
            wst[cp] = dma("sync", win_scr_t[cc],
                          wpc[:, cp * 4096:(cp + 1) * 4096], "winw%d" % cp)
            s.wait("tensor", cvtC[r0])
            for dc in range(16):
                k4 = dc % 4
                if k4 == 0:
                    s.wait("tensor", ev_wt)
                tt = s.emit("tensor", lambda h, dc=dc, k4=k4, cp=cp: h.transpose(
                    out=p_trC[:, k4 * 128:(k4 + 1) * 128],
                    in_=wpc[:, cp * 4096 + dc * 128: cp * 4096 + (dc + 1) * 128],
                    identity=ident_bf[:]), inc=(1 if k4 == 3 else 0))
                if k4 == 3:
                    s.wait("scalar", tt)
                    dc0 = dc - 3
                    dst = winT[:].rearrange("p (d c) -> p d c", c=DM)[
                        :, dc0:dc0 + 4, cc * 128:(cc + 1) * 128]
                    ev_wt = s.emit("scalar", lambda h, d=dst: h.activation(
                        out=d, in_=p_trC[:].rearrange("p (d c) -> p d c", c=128),
                        func=AF.Copy))
            rdC[cp] = s.token("tensor")
        winT_done = ev_wt

        s.wait("sync", cvtC[0])
        s.wait("sync", rdC[0]); s.wait("sync", rdC[1])
        wxld = dma("sync", wstgC[:, 0:16 * 96].rearrange("p (c e) -> p c e", e=96),
                   W_xproj_h.rearrange("(c p) e -> p c e", p=128), "wstg0")
        s.wait("vector", wxld)
        wxcv = s.emit("vector", lambda h: h.tensor_copy(out=wxp_bf[:], in_=wstgC[:, 0:16 * 96]))

        s.wait("tensor", wxcv)
        s.wait("tensor", winT_done)
        ev_we = None
        for cc in range(NC):
            s.wait("tensor", ev_we)
            for dc in range(16):
                mt = s.emit("tensor", lambda h, cc=cc, dc=dc: h.matmul(
                    out=p_we[:], lhsT=winT[:, dc * DM + cc * 128: dc * DM + (cc + 1) * 128],
                    rhs=wxp_bf[:, dc * 96:(dc + 1) * 96],
                    start=(dc == 0), stop=(dc == 15)), inc=(1 if dc == 15 else 0))
            s.wait("scalar", mt)
            s.wait("scalar", ev_we)
            ev_we = s.emit("scalar", lambda h, cc=cc: h.activation(
                out=w_eff[:, cc * 96:(cc + 1) * 96], in_=p_we[:], func=AF.Copy))
        weff_done = ev_we

        s.wait("sync", wxcv)
        wdld = dma("sync", wstgC[0:64, 0:2048], W_dt_h, "wstg1")
        s.wait("vector", wdld)
        wdcv = s.emit("vector", lambda h: h.tensor_copy(out=w_dt[:], in_=wstgC[0:64, 0:2048]))

        smallr_guard = None
        for (name, hsrc, dstt) in (("bdt", b_dt_h, bdt_all), ("dvec", Dvec_h, D_all)):
            s.wait("sync", smallr_guard)
            ldr = dma("sync", smallr[:], hsrc.rearrange("(a p) -> a p", p=128), "b_" + name)
            s.wait("tensor", ldr)
            s.wait("tensor", s.token("scalar"))
            tt = s.emit("tensor", lambda h: h.transpose(
                out=p_we[:, 0:16], in_=smallr[:], identity=ident_f[0:16, 0:16]))
            s.wait("scalar", tt)
            s.emit("scalar", lambda h, d=dstt: h.activation(out=d[:], in_=p_we[:, 0:16],
                                                            func=AF.Copy))
            smallr_guard = s.token("tensor")

        s.wait("sync", wdcv)
        ald = dma("sync", wstgC[:, 0:256].rearrange("p (c s) -> p c s", s=DS),
                  A_log_h.rearrange("(c p) s -> p c s", p=128), "wstg0")
        s.wait("scalar", ald)
        ae = s.emit("scalar", lambda h: h.activation(out=A_all[:], in_=wstgC[:, 0:256],
                                                     func=AF.Exp))
        s.wait("vector", ae)
        a_done = s.emit("vector", lambda h: h.tensor_scalar(
            out=A_all[:], in0=A_all[:], scalar1=-1.0, scalar2=None, op0=ALU.mult))

        hz = s.emit("vector", lambda h: h.memset(h_state[:], 0.0))

        s.barrier()
        es_c1.close()

        es_c2 = ExitStack()
        SC2 = lambda nm, shp, dt=F32: es_c2.enter_context(nc.sbuf_tensor(nm, shp, dt))
        PC2 = lambda nm, shp, dt=F32: es_c2.enter_context(nc.psum_tensor(nm, shp, dt))
        x2r = SC2("x2r", [128, 2 * DM], F32)
        x2Tc = SC2("x2Tc", [128, NC * TC], BF16)
        dt_ch = SC2("dt_ch", [128, 2 * NDH * TC], F32)
        xp_ch = SC2("xp_ch", [128, NDH * TC], F32)
        dxp_ch = SC2("dxp_ch", [128, NDH * TC], F32)
        y_ch = SC2("y_ch", [128, NDH * TC], F32)
        a_sl = SC2("a_sl", [128, NDH * TC], F32)
        zs_ch = SC2("zs_ch", [128, 2 * NDH * TC], F32)
        b_sl0 = SC2("b_sl0", [128, NDH * TC], F32)
        b_sl1 = SC2("b_sl1", [128, NDH * TC], F32)
        ao_ch = SC2("ao_ch", [128, (T // 128 // 2) * DM], F32)
        ybf = SC2("ybf", [128, NDH * TC], BF16)
        bcb = SC2("bcb", [128, 4 * TC], F32)
        wipz = SC2("wipz", [128, NC * NDH * 128], BF16)
        wob4 = SC2("wob4", [128, NDH * DM], BF16)
        brow_c = SC2("brow_c", [32, TC], F32)
        x3f = SC2("x3f", [128, DM], F32)
        p_big = PC2("p_bigC", [128, 2048], F32)
        p_xd = p_big[0:96, 1024:2048]
        p_ao2 = PC2("p_aoC", [128, 2 * DM], F32)

        x2_scr_t = x2_scr.rearrange("(c p) d -> c p d", p=128)
        x3_scr_t = x3_scr.rearrange("(c p) d -> c p d", p=128)

        wout_wr = [None] * 16

        def wout_pieces():
            prev_cv = [None, None]
            prev_wr = [None, None]
            for dc in range(16):
                r = dc % 2
                buf = wo_stg[:, r * DM:(r + 1) * DM]
                s.wait("sync", prev_cv[r])
                ldt = dma("sync", buf, W_out_t[dc], "wold%d" % r)
                s.wait("vector", ldt)
                s.wait("vector", prev_wr[r])
                cv = s.emit("vector", lambda h, b=buf, r=r: h.tensor_copy(
                    out=wo_bf[:, r * DM:(r + 1) * DM], in_=b))
                prev_cv[r] = cv
                s.wait("sync", cv)
                wout_wr[dc] = dma("sync", wout_scr_t[dc], wo_bf[:, r * DM:(r + 1) * DM],
                                  "wow%d" % r)
                prev_wr[r] = wout_wr[dc]
                yield

        wo_gen = wout_pieces()

        WPW = NDH * 128
        quarters = [(k, dh) for k in range(NK) for dh in range(NQ)]
        NQT = len(quarters)
        G = dict(x2r_guard=[None, None], x2T_guard=None, dlt_guard=None,
                 bc_rd=[None, None, None, None], bcw_tok=None, pbig_guard=None,
                 xd_guard=None, wip_guard=None, wo_guard=None,
                 ao_guard=None, ao_par=[None, None], x3guards=[], prev_ln3=None,
                 dtp_rd_s=[None, None], dtp_rd_v=[None, None],
                 zs_rd=[None, None], y_guard=None, dxp_guard=None, xp_rd=None,
                 pbig_free=None, ab0=None, ab1=[None, None], hs_guard=None,
                 dltev={}, xdbl={})
        PQ = {}

        def dt2(qp):
            return dt_ch[:, qp * NDH * TC:(qp + 1) * NDH * TC]

        def zs2(qp):
            return zs_ch[:, qp * NDH * TC:(qp + 1) * NDH * TC]

        def chunk_stage(k, pbf):
            G['x2T_guard'] = s.token("tensor")
            s.wait("tensor", pbf)
            ev_t2 = None
            for sub in range(NSUB):
                r = sub % 2
                s.wait("sync", G['x2r_guard'][r])
                s.wait("sync", s.token("x2w"))
                ldx2 = dma("sync", x2r[:, r * DM:(r + 1) * DM], x2_scr_t[k * NSUB + sub],
                           "x2r%d" % r)
                s.wait("tensor", ldx2)
                for cc in range(NC):
                    k4 = cc % 4
                    if k4 == 0:
                        s.wait("tensor", ev_t2 or G['pbig_guard'])
                    tt = s.emit("tensor", lambda h, r=r, cc=cc, k4=k4: h.transpose(
                        out=p_big[:, k4 * 128:(k4 + 1) * 128],
                        in_=x2r[:, r * DM + cc * 128: r * DM + (cc + 1) * 128],
                        identity=ident_f[:]), inc=(1 if k4 == 3 else 0))
                    if k4 == 3:
                        s.wait("scalar", tt)
                        s.wait("scalar", G['x2T_guard'])
                        cc0 = cc - 3
                        dst = x2Tc[:].rearrange("p (c t) -> p c t", t=TC)[
                            :, cc0:cc0 + 4, sub * 128:(sub + 1) * 128]
                        ev_t2 = s.emit("scalar", lambda h, d=dst: h.activation(
                            out=d, in_=p_big[:, 0:512].rearrange("p (c t) -> p c t", t=128),
                            func=AF.Copy))
                G['x2r_guard'][r] = s.token("tensor")
            G['pbig_guard'] = ev_t2
            s.wait("tensor", ev_t2)
            s.wait("tensor", weff_done)
            s.wait("tensor", G['xd_guard'])
            for hv in range(TC // 512):
                for cc in range(NC):
                    mt = s.emit("tensor", lambda h, cc=cc, hv=hv: h.matmul(
                        out=p_xd[:, hv * 512:(hv + 1) * 512],
                        lhsT=w_eff[:, cc * 96:(cc + 1) * 96],
                        rhs=x2Tc[:, cc * TC + hv * 512: cc * TC + hv * 512 + 512],
                        start=(cc == 0), stop=(cc == NC - 1)),
                        inc=(1 if cc == NC - 1 and hv == TC // 512 - 1 else 0))
            s.wait("scalar", mt)
            s.wait("scalar", G['dlt_guard'])
            dltev = s.emit("scalar", lambda h, k=k: h.activation(
                out=dltT[:, k * TC:(k + 1) * TC], in_=p_xd[0:64, :], func=AF.Copy))
            s.wait("scalar", G['bcw_tok'])
            bcev = s.emit("scalar", lambda h: h.activation(
                out=brow_c[:], in_=p_xd[64:96, :], func=AF.Copy))
            G['dlt_guard'] = bcev
            G['xd_guard'] = bcev
            G['dltev'][k] = dltev
            G['xdbl'][k] = bcev
            s.wait("sync", bcev)
            G['bcw_tok'] = dma("sync", bc_scr[:, k * TC:(k + 1) * TC], brow_c[:], "bcw")

        def produce_prescan(qi):
            k, dh = quarters[qi]
            PQ[qi] = {}
            s.wait("sync", G['wip_guard'])
            s.wait("sync", s.token("winw0"))
            s.wait("sync", s.token("winw1"))
            wipld = [dma("sync", wipz[:, cc * WPW:(cc + 1) * WPW],
                         win_scr_t[cc][:, dh * WPW:(dh + 1) * WPW], "wip%d" % cc)
                     for cc in range(NC)]
            PQ[qi]['wipld'] = wipld
            s.wait("tensor", wdcv)
            s.wait("tensor", G['dltev'][k])
            s.wait("tensor", G['xdbl'][k])
            s.wait("tensor", G['pbig_guard'])
            s.wait("tensor", G['pbig_free'])
            for j4 in range(NDH):
                dc = dh * NDH + j4
                for hv in range(TC // 512):
                    mt = s.emit("tensor", lambda h, dc=dc, j4=j4, k=k, hv=hv: h.matmul(
                        out=p_big[:, j4 * TC + hv * 512: j4 * TC + hv * 512 + 512],
                        lhsT=w_dt[:, dc * 128:(dc + 1) * 128],
                        rhs=dltT[:, k * TC + hv * 512: k * TC + hv * 512 + 512],
                        start=True, stop=True),
                        inc=(1 if j4 == NDH - 1 and hv == TC // 512 - 1 else 0))
            PQ[qi]['mt_delta'] = mt

        def emit_exp(qi):
            k, dh = quarters[qi]
            qp = qi % 2
            s.wait("scalar", PQ[qi]['mt_delta'])
            s.wait("scalar", G['dtp_rd_s'][qp])
            tok = None
            for j4 in range(NDH):
                dc = dh * NDH + j4
                tok = s.emit("scalar", lambda h, j4=j4, dc=dc, d=dt2(qp): h.activation(
                    out=d[:, j4 * TC:(j4 + 1) * TC], in_=p_big[:, j4 * TC:(j4 + 1) * TC],
                    func=AF.Exp, bias=bdt_all[:, dc:dc + 1]))
            PQ[qi]['exp'] = tok

        def emit_clip(qi):
            qp = qi % 2
            s.wait("vector", PQ[qi]['exp'])
            s.wait("vector", G['dtp_rd_v'][qp])
            PQ[qi]['clip'] = s.emit("vector", lambda h, d=dt2(qp): h.tensor_scalar(
                out=d, in0=d, scalar1=0.10517091807564763, scalar2=0.0010005001667083846,
                op0=ALU.min, op1=ALU.max))

        def emit_ln(qi):
            qp = qi % 2
            s.wait("scalar", PQ[qi]['clip'])
            one_c = constv(1.0)
            PQ[qi]['dt_done'] = s.emit("scalar", lambda h, d=dt2(qp), one_c=one_c: h.activation(
                out=d, in_=d, func=AF.Ln, bias=one_c))

        def emit_xpmms(qi):
            s.wait("tensor", PQ[qi]['exp'])
            for cc in range(NC):
                s.wait("tensor", PQ[qi]['wipld'][cc])
            for j4 in range(NDH):
                for hv in range(TC // 512):
                    for cc in range(NC):
                        mt = s.emit("tensor", lambda h, j4=j4, cc=cc, hv=hv: h.matmul(
                            out=p_big[:, j4 * TC + hv * 512: j4 * TC + hv * 512 + 512],
                            lhsT=wipz[:, cc * WPW + j4 * 128: cc * WPW + (j4 + 1) * 128],
                            rhs=x2Tc[:, cc * TC + hv * 512: cc * TC + hv * 512 + 512],
                            start=(cc == 0), stop=(cc == NC - 1)),
                            inc=(1 if cc == NC - 1 and j4 == NDH - 1 and hv == TC // 512 - 1 else 0))
            PQ[qi]['mt_xp'] = mt
            PQ[qi]['wip_xp'] = s.token("tensor")

        def emit_xpev(qi):
            s.wait("scalar", PQ[qi]['mt_xp'])
            s.wait("scalar", G['xp_rd'])
            PQ[qi]['xp_done'] = s.emit("scalar", lambda h: h.activation(
                out=xp_ch[:], in_=p_big[:], func=AF.Copy))

        def emit_zmms(qi):
            k, dh = quarters[qi]
            s.wait("sync", PQ[qi]['wip_xp'])
            wipldz = [dma("sync", wipz[:, cc * WPW:(cc + 1) * WPW],
                          win_scr_t[cc][:, 2048 + dh * WPW: 2048 + (dh + 1) * WPW],
                          "wip%d" % cc) for cc in range(NC)]
            s.wait("tensor", PQ[qi]['xp_done'])
            for cc in range(NC):
                s.wait("tensor", wipldz[cc])
            for j4 in range(NDH):
                for hv in range(TC // 512):
                    for cc in range(NC):
                        mt = s.emit("tensor", lambda h, j4=j4, cc=cc, hv=hv: h.matmul(
                            out=p_big[:, j4 * TC + hv * 512: j4 * TC + hv * 512 + 512],
                            lhsT=wipz[:, cc * WPW + j4 * 128: cc * WPW + (j4 + 1) * 128],
                            rhs=x2Tc[:, cc * TC + hv * 512: cc * TC + hv * 512 + 512],
                            start=(cc == 0), stop=(cc == NC - 1)),
                            inc=(1 if cc == NC - 1 and j4 == NDH - 1 and hv == TC // 512 - 1 else 0))
            PQ[qi]['mt_z'] = mt
            G['wip_guard'] = s.token("tensor")

        def emit_zev(qi):
            qp = qi % 2
            s.wait("scalar", PQ[qi]['mt_z'])
            s.wait("scalar", G['zs_rd'][qp])
            PQ[qi]['zev'] = s.emit("scalar", lambda h, d=zs2(qp): h.activation(
                out=d, in_=p_big[:], func=AF.Silu))
            G['pbig_free'] = PQ[qi]['zev']

        HOOKS = {0: emit_exp, 1: emit_clip, 2: emit_ln, 3: emit_xpmms,
                 5: emit_xpev, 6: emit_zmms, 8: emit_zev}

        def consume(qi, nxt):
            k, dh = quarters[qi]
            qp = qi % 2
            if k == 0:
                for _ in range(2):
                    next(wo_gen, None)
            s.wait("vector", PQ[qi]['xp_done'])
            s.wait("vector", PQ[qi]['dt_done'])
            s.wait("vector", G['y_guard'])
            yv = None
            for j in range(NDH):
                dc = dh * NDH + j
                yv = s.emit("vector", lambda h, j=j, dc=dc: h.tensor_scalar(
                    out=y_ch[:, j * TC:(j + 1) * TC], in0=xp_ch[:, j * TC:(j + 1) * TC],
                    scalar1=D_all[:, dc:dc + 1], scalar2=None, op0=ALU.mult))
            s.wait("vector", G['dxp_guard'])
            dxpv = s.emit("vector", lambda h, d=dt2(qp): h.tensor_tensor(
                out=dxp_ch[:], in0=d, in1=xp_ch[:], op=ALU.mult))
            G['xp_rd'] = dxpv
            G['dtp_rd_v'][qp] = dxpv

            s.wait("vector", a_done)
            s.wait("vector", hz)
            acc_prev = yv
            bcast_tok = [None, None, None, None]

            def issue_bcast(s_):
                slot = s_ % 2
                s.wait("sync", G['bc_rd'][slot])
                s.wait("sync", G['bc_rd'][2 + slot])
                s.wait("sync", G['bcw_tok'])
                bcast_tok[slot] = dma(
                    "sync", bcb[:, slot * TC:(slot + 1) * TC],
                    bc_scr[s_:s_ + 1, k * TC:(k + 1) * TC].partition_broadcast(128),
                    "bcr%d" % slot)
                bcast_tok[2 + slot] = dma(
                    "sync", bcb[:, (2 + slot) * TC:(3 + slot) * TC],
                    bc_scr[16 + s_:17 + s_, k * TC:(k + 1) * TC].partition_broadcast(128),
                    "bcc%d" % slot)

            issue_bcast(0)
            amk = [None]
            for s_ in range(16):
                slot = s_ % 2
                bld = bcast_tok[slot]
                cld = bcast_tok[2 + slot]
                if s_ < 15:
                    issue_bcast(s_ + 1)
                s.wait("scalar", G['ab0'])
                s.wait("scalar", PQ[qi]['dt_done'])
                s.wait("scalar", a_done)
                amk = []
                for j in range(NDH):
                    dc = dh * NDH + j
                    amk.append(s.emit("scalar", lambda h, j=j, dc=dc, s_=s_, d=dt2(qp): h.activation(
                        out=a_sl[:, j * TC:(j + 1) * TC], in_=d[:, j * TC:(j + 1) * TC],
                        func=AF.Copy, scale=A_all[:, dc * DS + s_: dc * DS + s_ + 1],
                        bias=1.0)))
                bs = b_sl0 if (s_ % 2 == 0) else b_sl1
                s.wait("vector", bld)
                s.wait("vector", dxpv)
                s.wait("vector", G['ab1'][s_ % 2])
                bmk = s.emit("vector", lambda h, slot=slot, bs=bs: h.tensor_tensor(
                    out=bs[:].rearrange("p (j t) -> p j t", t=TC),
                    in0=dxp_ch[:].rearrange("p (j t) -> p j t", t=TC),
                    in1=bcb[:, slot * TC:(slot + 1) * TC].unsqueeze(1)
                        .broadcast_to([128, NDH, TC]), op=ALU.mult))
                G['bc_rd'][slot] = bmk
                s.wait("vector", G['hs_guard'])
                s.wait("vector", amk[-1])
                s.wait("vector", bmk)
                scn = []
                for j in range(NDH):
                    dc = dh * NDH + j
                    scn.append(s.emit("vector", lambda h, j=j, dc=dc, s_=s_, bs=bs: h.tensor_tensor_scan(
                        out=bs[:, j * TC:(j + 1) * TC], data0=a_sl[:, j * TC:(j + 1) * TC],
                        data1=bs[:, j * TC:(j + 1) * TC],
                        initial=h_state[:, dc * DS + s_: dc * DS + s_ + 1],
                        op0=ALU.mult, op1=ALU.add)))
                G['ab0'] = scn[-1]
                s.wait("vector", scn[-1])
                G['hs_guard'] = s.emit("vector", lambda h, dh=dh, s_=s_, bs=bs: h.tensor_copy(
                    out=h_state[:].rearrange("p (d s) -> p d s", s=DS)[
                        :, dh * NDH:(dh + 1) * NDH, s_:s_ + 1],
                    in_=bs[:].rearrange("p (j t) -> p j t", t=TC)[:, :, TC - 1:TC]))
                s.wait("vector", cld)
                s.wait("vector", G['hs_guard'])
                cm = s.emit("vector", lambda h, slot=slot, bs=bs: h.tensor_tensor(
                    out=bs[:].rearrange("p (j t) -> p j t", t=TC),
                    in0=bs[:].rearrange("p (j t) -> p j t", t=TC),
                    in1=bcb[:, (2 + slot) * TC:(3 + slot) * TC].unsqueeze(1)
                        .broadcast_to([128, NDH, TC]), op=ALU.mult))
                s.wait("vector", acc_prev)
                s.wait("vector", cm)
                acc_prev = s.emit("vector", lambda h, bs=bs: h.tensor_tensor(
                    out=y_ch[:], in0=y_ch[:], in1=bs[:], op=ALU.add))
                G['bc_rd'][2 + slot] = acc_prev
                G['ab1'][s_ % 2] = acc_prev
                if nxt is not None and s_ in HOOKS:
                    HOOKS[s_](nxt)
            y_done = acc_prev
            G['dtp_rd_s'][qp] = amk[-1]

            s.wait("vector", PQ[qi]['zev'])
            s.wait("vector", y_done)
            ymul = s.emit("vector", lambda h, d=zs2(qp): h.tensor_tensor(
                out=y_ch[:], in0=y_ch[:], in1=d, op=ALU.mult))
            G['zs_rd'][qp] = ymul
            s.wait("vector", ymul)
            ycv = s.emit("vector", lambda h: h.tensor_copy(out=ybf[:], in_=y_ch[:]))
            G['y_guard'] = ycv
            G['dxp_guard'] = ycv

            s.wait("tensor", ycv)
            for j in range(NDH):
                s.wait("sync", wout_wr[dh * NDH + j])
            s.wait("sync", G['wo_guard'])
            wld4 = [dma("sync", wob4[:, j * DM:(j + 1) * DM],
                        wout_scr_t[dh * NDH + j], "wo%d" % j) for j in range(NDH)]
            for sub in range(NSUB):
                pa = p_ao2[:, (sub % 2) * DM:(sub % 2 + 1) * DM]
                s.wait("tensor", G['ao_par'][sub % 2])
                for j in range(NDH):
                    s.wait("tensor", wld4[j])
                    for db in range(2):
                        mt = s.emit("tensor", lambda h, j=j, sub=sub, db=db, pa=pa: h.matmul(
                            out=pa[:, db * 512:(db + 1) * 512],
                            lhsT=ybf[:, j * TC + sub * 128: j * TC + (sub + 1) * 128],
                            rhs=wob4[:, j * DM + db * 512: j * DM + (db + 1) * 512],
                            start=(j == 0), stop=(j == NDH - 1)),
                            inc=(1 if j == NDH - 1 and db == 1 else 0))
                if dh == 0:
                    s.wait("scalar", mt)
                    s.wait("scalar", G['ao_guard'])
                    G['ao_guard'] = s.emit("scalar", lambda h, sub=sub, pa=pa: h.activation(
                        out=ao_ch[:, sub * DM:(sub + 1) * DM], in_=pa[:], func=AF.Copy))
                    G['ao_par'][sub % 2] = G['ao_guard']
                elif dh < NQ - 1:
                    s.wait("vector", mt)
                    s.wait("vector", G['ao_guard'])
                    G['ao_guard'] = s.emit("vector", lambda h, sub=sub, pa=pa: h.tensor_tensor(
                        out=ao_ch[:, sub * DM:(sub + 1) * DM], in0=pa[:],
                        in1=ao_ch[:, sub * DM:(sub + 1) * DM], op=ALU.add))
                    G['ao_par'][sub % 2] = G['ao_guard']
                else:
                    s.wait("vector", mt)
                    s.wait("vector", G['ao_guard'])
                    aadd = s.emit("vector", lambda h, sub=sub, pa=pa: h.tensor_tensor(
                        out=ao_ch[:, sub * DM:(sub + 1) * DM], in0=pa[:],
                        in1=ao_ch[:, sub * DM:(sub + 1) * DM], op=ALU.add))
                    G['ao_par'][sub % 2] = aadd
                    r = sub % 2
                    s.wait("sync", G['x2r_guard'][r])
                    ldr2 = dma("sync", x2r[:, r * DM:(r + 1) * DM],
                               x2_scr_t[k * NSUB + sub], "x2b%d" % r)
                    s.wait("vector", ldr2)
                    s.wait("vector", aadd)
                    radd = s.emit("vector", lambda h, sub=sub, r=r: h.tensor_tensor(
                        out=ao_ch[:, sub * DM:(sub + 1) * DM],
                        in0=ao_ch[:, sub * DM:(sub + 1) * DM],
                        in1=x2r[:, r * DM:(r + 1) * DM], op=ALU.add))
                    G['x2r_guard'][r] = radd
                    s.wait("scalar", radd)
                    s.wait("vector", radd)
                    for g_ in G['x3guards']:
                        s.wait("scalar", g_); s.wait("vector", g_)
                    s.wait("vector", G['prev_ln3'])
                    vt3, st3 = layernorm(ao_ch[:, sub * DM:(sub + 1) * DM], x3f[:], None)
                    G['prev_ln3'] = st3
                    G['ao_guard'] = st3
                    s.wait("sync", st3)
                    w3 = dma("sync", x3_scr_t[k * NSUB + sub], x3f[:], "x3w")
                    G['x3guards'] = [w3]
                    if stop_after == "x3":
                        G['x3guards'].append(dma("sync", out_t[k * NSUB + sub], x3f[:], "outw"))
            G['wo_guard'] = s.token("tensor")

        chunk_stage(0, None)
        produce_prescan(0)
        for fn in (emit_exp, emit_clip, emit_ln, emit_xpmms, emit_xpev,
                   emit_zmms, emit_zev):
            fn(0)
        for qi in range(NQT):
            if qi + 1 < NQT:
                if quarters[qi + 1][1] == 0:
                    chunk_stage(quarters[qi + 1][0], G['pbig_free'])
                produce_prescan(qi + 1)
                consume(qi, qi + 1)
            else:
                consume(qi, None)

        s.barrier()
        es_c2.close()
        es_c.close()

    if stop_after == "full":
        es_d = ExitStack()
        SBd = lambda nm, shp, dt=F32: es_d.enter_context(nc.sbuf_tensor(nm, shp, dt))
        PSd = lambda nm, shp, dt=F32: es_d.enter_context(nc.psum_tensor(nm, shp, dt))
        LADDER_KS = [64, 32, 16, 8, 4, 2, 1]
        for k_ in LADDER_KS:
            constv(-SIGBIG * (k_ - 0.5))
        fx_all = SBd("fx_all", [128, NT], F32)
        gmax_all = SBd("gmax_all", [128, NT], F32)
        fg_all = SBd("fg_all", [128, NT], F32)
        fu_all = SBd("fu_all", [128, NT], F32)
        f2_all = SBd("f2_all", [128, NT], F32)
        wsc = SBd("wsc", [128, 9], F32)
        wsc2 = SBd("wsc2", [128, 4], F32)
        totsb = SBd("totsb", [1, 4], F32)
        ln_sq_holder["cur"] = SBd("ln_sqD", [128, DM], F32)
        wdT = SBd("wdT", [128, 32 * DM], BF16)
        wstgD2 = SBd("wstgD2", [128, 2 * 2048], F32)
        wqb2 = SBd("wqb2", [128, 2048], BF16)
        ternB = SBd("ternB", [128, 2048], F32)
        es_dx = ExitStack()
        xqT = es_dx.enter_context(nc.sbuf_tensor("xqT", [128, NC * T], BF16))
        gu_scr_t = gu_scr.rearrange("(n p) w -> n p w", p=128)

        es_d0 = ExitStack()
        SD0 = lambda nm, shp, dt=F32: es_d0.enter_context(nc.sbuf_tensor(nm, shp, dt))
        PD0 = lambda nm, shp, dt=F32: es_d0.enter_context(nc.psum_tensor(nm, shp, dt))
        wrow = SD0("wrowD", [128, 2 * 2048], F32)
        wpart = SD0("wpartD", [128, 2], F32)
        p_d0 = PD0("p_d0", [128, 8], F32)

        onec = constv(1.0)
        w_gate_t0 = w_gate_h.rearrange("(a p) i -> a p i", p=128)
        w_up_t0 = w_up_h.rearrange("(a p) i -> a p i", p=128)
        w_down_t0 = w_down_h.rearrange("(a p) i -> a p i", p=128)
        w_srcs = [
            ("g", lambda pc: w_gate_t0[pc], 32, DM),
            ("u", lambda pc: w_up_t0[pc], 32, DM),
            ("d", lambda pc: w_down_t0[pc // 2][:, (pc % 2) * 2048:(pc % 2 + 1) * 2048],
             16, 2048)]
        red_tok_box = [None]
        ld_guard = [None, None]

        def d0_pieces():
            red_tok = None
            for wi, (tag, wsrc, npieces, wid) in enumerate(w_srcs):
                acc_t = red_tok
                for pc in range(npieces):
                    r = pc % 2
                    buf = wrow[:, r * 2048: r * 2048 + wid]
                    s.wait("sync", ld_guard[r])
                    ldw = dma("sync", buf, wsrc(pc), "w0D%d" % r)
                    s.wait("vector", ldw)
                    s.wait("vector", acc_t)
                    rt = s.emit("vector", lambda h, b=buf: h.tensor_reduce(
                        out=wpart[:, 0:1], in_=b, axis=AXX, op=ALU.add,
                        apply_absolute_value=True))
                    ld_guard[r] = rt
                    s.wait("vector", rt)
                    if pc == 0:
                        s.wait("vector", red_tok)
                        acc_t = s.emit("vector", lambda h: h.tensor_copy(
                            out=wpart[:, 1:2], in_=wpart[:, 0:1]))
                    else:
                        acc_t = s.emit("vector", lambda h: h.tensor_tensor(
                            out=wpart[:, 1:2], in0=wpart[:, 1:2], in1=wpart[:, 0:1],
                            op=ALU.add))
                    yield
                s.wait("tensor", acc_t)
                s.wait("tensor", red_tok)
                pt = s.emit("tensor", lambda h, wi=wi: h.matmul(
                    out=p_d0[0:1, wi:wi + 1], lhsT=wpart[:, 1:2], rhs=onec,
                    start=True, stop=True))
                s.wait("scalar", pt)
                red_tok = s.emit("scalar", lambda h, wi=wi: h.activation(
                    out=totsb[0:1, wi:wi + 1], in_=p_d0[0:1, wi:wi + 1], func=AF.Copy))
                red_tok_box[0] = red_tok

        d0_gen = d0_pieces()
        def ternary(dst_ap, src_ap, wi, dep_toks, scr=None):
            tau = wsc[:, wi * 3 + 1: wi * 3 + 2]
            ntau = wsc[:, wi * 3 + 2: wi * 3 + 3]
            scr = scr if scr is not None else tern_scr
            for t_ in dep_toks:
                s.wait("vector", t_)
            r1 = s.emit("vector", lambda h, scr=scr: h.tensor_scalar(
                out=scr[:, 0:src_ap.shape[-1]], in0=src_ap, scalar1=ntau,
                scalar2=-1.0, op0=ALU.is_ge, op1=ALU.add))
            s.wait("vector", r1)
            return s.emit("vector", lambda h, scr=scr: h.scalar_tensor_tensor(
                out=dst_ap, in0=src_ap, scalar=tau,
                in1=scr[:, 0:src_ap.shape[-1]], op0=ALU.is_gt, op1=ALU.add))

        def ladder(u_abs_ap, d_ap, stp_ap, width, dep):
            s.wait("vector", dep)
            t = dep
            for li, k_ in enumerate(LADDER_KS):
                srcd = u_abs_ap if li == 0 else d_ap
                s.wait("vector", t)
                st = s.emit("vector", lambda h, srcd=srcd, k_=k_: h.tensor_scalar(
                    out=stp_ap, in0=srcd, scalar1=float(k_) - 0.5, scalar2=float(k_),
                    op0=ALU.is_ge, op1=ALU.mult))
                s.wait("vector", st)
                t = s.emit("vector", lambda h, srcd=srcd, k_=k_: h.scalar_tensor_tensor(
                    out=d_ap, in0=stp_ap, scalar=-1.0, in1=srcd,
                    op0=ALU.mult, op1=ALU.add))
            s.wait("vector", t)
            return s.emit("vector", lambda h: h.tensor_tensor(
                out=u_abs_ap, in0=u_abs_ap, in1=d_ap, op=ALU.subtract))

        es_d2 = ExitStack()
        SD2 = lambda nm, shp, dt=F32: es_d2.enter_context(nc.sbuf_tensor(nm, shp, dt))
        PD2 = lambda nm, shp, dt=F32: es_d2.enter_context(nc.psum_tensor(nm, shp, dt))
        x3c = SD2("x3cD", [128, 4 * DM], F32)
        qa = SD2("qaD", [128, DM], F32)
        qd = SD2("qdD", [128, DM], F32)
        qs = SD2("qsD", [128, DM], F32)
        qbf = SD2("qbfD", [128, DM], BF16)
        qcol = SD2("qcolD", [128, 4], F32)
        p_trD = PD2("p_trD", [128, 512], BF16)

        x3ld_g = [None, None, None, None]
        lad_prev = None
        tr_prev = None
        for tch in range(NT):
            r = tch % 4
            buf = x3c[:, r * DM:(r + 1) * DM]
            s.wait("sync", x3ld_g[r])
            s.wait("sync", s.token("x3w"))
            ldx3 = dma("sync", buf, x3_scr_t[tch], "x3r%d" % r)
            s.wait("vector", ldx3)
            s.wait("vector", lad_prev)
            mx = s.emit("vector", lambda h, b=buf: h.reduce_max(
                out=qcol[:, 0:1], in_=b, axis=AXX, apply_absolute_value=True))
            s.wait("vector", mx)
            fxv = s.emit("vector", lambda h, tch=tch: h.tensor_scalar(
                out=fx_all[:, tch:tch + 1], in0=qcol[:, 0:1],
                scalar1=1.0e-5, scalar2=1.0 / 127.0, op0=ALU.max, op1=ALU.mult))
            s.wait("vector", fxv)
            sxv = s.emit("vector", lambda h, tch=tch: h.reciprocal(
                out=qcol[:, 1:2], in_=fx_all[:, tch:tch + 1]))
            s.wait("scalar", sxv)
            ab = s.emit("scalar", lambda h, b=buf: h.activation(
                out=qa[:], in_=b, func=AF.Abs, scale=qcol[:, 1:2]))
            sg = s.emit("scalar", lambda h, b=buf: h.activation(
                out=qs[:], in_=b, func=AF.Sign))
            lt = ladder(qa[:], qd[:], ln_sq_holder["cur"][:, 0:DM], DM, ab)
            s.wait("vector", lt)
            s.wait("vector", sg)
            s.wait("vector", tr_prev)
            qv = s.emit("vector", lambda h: h.tensor_tensor(
                out=qbf[:], in0=qa[:], in1=qs[:], op=ALU.mult))
            lad_prev = qv
            x3ld_g[r] = qv
            s.wait("tensor", qv)
            ev_q = None
            for g in range(2):
                s.wait("tensor", ev_q or tr_prev)
                for k4 in range(4):
                    cc = g * 4 + k4
                    tt = s.emit("tensor", lambda h, cc=cc, k4=k4: h.transpose(
                        out=p_trD[:, k4 * 128:(k4 + 1) * 128],
                        in_=qbf[:, cc * 128:(cc + 1) * 128],
                        identity=ident_bf[:]), inc=(1 if k4 == 3 else 0))
                s.wait("scalar", tt)
                s.wait("scalar", tr_prev)
                dst = xqT[:].rearrange("p (c t) -> p c t", t=T)[:, g * 4:(g + 1) * 4,
                                                               tch * 128:(tch + 1) * 128]
                ev_q = s.emit("scalar", lambda h, d=dst: h.activation(
                    out=d, in_=p_trD[:].rearrange("p (c t) -> p c t", t=128), func=AF.Copy))
            tr_prev = s.token("tensor")
            for _ in range(5):
                if next(d0_gen, "end") == "end":
                    break
        xqT_done = s.token("scalar")
        s.barrier()
        es_d2.close()

        for _ in d0_gen:
            pass
        red_tok = red_tok_box[0]
        s.wait("tensor", red_tok)
        bt = s.emit("tensor", lambda h: h.matmul(
            out=p_d0[:, 0:3], lhsT=ones_f[0:1, :], rhs=totsb[0:1, 0:3],
            start=True, stop=True))
        s.wait("vector", bt)
        totb = s.emit("vector", lambda h: h.tensor_copy(out=wsc2[:, 0:3], in_=p_d0[:, 0:3]))
        mc_toks = []
        for wi, nelem in ((0, INNER * DM), (1, INNER * DM), (2, DM * INNER)):
            s.wait("vector", totb if wi == 0 else mc_toks[-1])
            t_a = s.emit("vector", lambda h, wi=wi, nelem=nelem: h.tensor_scalar(
                out=wsc[:, wi * 3: wi * 3 + 1], in0=wsc2[:, wi:wi + 1],
                scalar1=1.0 / nelem, scalar2=1.0e-5, op0=ALU.mult, op1=ALU.max))
            s.wait("vector", t_a)
            t_b = s.emit("vector", lambda h, wi=wi: h.tensor_scalar(
                out=wsc[:, wi * 3 + 1: wi * 3 + 2], in0=wsc[:, wi * 3: wi * 3 + 1],
                scalar1=0.5, scalar2=None, op0=ALU.mult))
            s.wait("vector", t_b)
            mc_toks.append(s.emit("vector", lambda h, wi=wi: h.tensor_scalar(
                out=wsc[:, wi * 3 + 2: wi * 3 + 3], in0=wsc[:, wi * 3 + 1: wi * 3 + 2],
                scalar1=-1.0, scalar2=None, op0=ALU.mult)))
        wsc_done = mc_toks[-1]
        w_down_t = w_down_h.rearrange("(a p) i -> a p i", p=128)

        def wd_pieces():
            ldg = [None, None]
            wq2_guard = None
            ev_wd = None
            for pc in range(8):
                for hv in range(2):
                    r2 = (pc * 2 + hv) % 2
                    buf = wstgD2[:, r2 * 2048:(r2 + 1) * 2048]
                    s.wait("sync", ldg[r2])
                    ldw = dma("sync", buf, w_down_t[pc][:, hv * 2048:(hv + 1) * 2048],
                              "wdd%d" % r2)
                    s.wait("vector", ldw)
                    tq = ternary(wqb2[:], buf, 2, [wq2_guard], scr=ternB)
                    ldg[r2] = tq
                    s.wait("tensor", tq)
                    for g in range(4):
                        s.wait("tensor", ev_wd)
                        for k4 in range(4):
                            jj = g * 4 + k4
                            tt = s.emit("tensor", lambda h, jj=jj, k4=k4: h.transpose(
                                out=p_trB2w[:, k4 * 128:(k4 + 1) * 128],
                                in_=wqb2[:, jj * 128:(jj + 1) * 128],
                                identity=ident_bf[:]), inc=(1 if k4 == 3 else 0))
                        s.wait("scalar", tt)
                        s.wait("scalar", ev_wd)
                        ic0 = hv * 16 + g * 4
                        dst = wdT[:].rearrange("p (c d) -> p c d", d=DM)[
                            :, ic0:ic0 + 4, pc * 128:(pc + 1) * 128]
                        ev_wd = s.emit("scalar", lambda h, d=dst: h.activation(
                            out=d, in_=p_trB2w[:].rearrange("p (c d) -> p c d", d=128),
                            func=AF.Copy))
                    wq2_guard = ev_wd
                    yield

        wd_gen = wd_pieces()

        es_da = ExitStack()
        SDa = lambda nm, shp, dt=F32: es_da.enter_context(nc.sbuf_tensor(nm, shp, dt))
        PDa = lambda nm, shp, dt=F32: es_da.enter_context(nc.psum_tensor(nm, shp, dt))
        wstgD = SDa("wstgD", [128, 4 * DM], F32)
        tern_scr = SDa("ternA", [128, DM], F32)
        wqb = SDa("wqbA", [128, DM], BF16)
        wgT2 = SDa("wgT", [128, 2 * NC * 512], BF16)
        wuT2 = SDa("wuT", [128, 2 * NC * 512], BF16)
        sgb = SDa("sgbA", [128, 2 * 512], F32)
        gub = SDa("gubA", [128, 2 * 512], BF16)
        qcolD = SDa("qcolA", [128, 4], F32)
        p_gA = PDa("p_gA", [128, 2 * 512], F32)
        p_uA = PDa("p_uA", [128, 2 * 512], F32)
        p_trA2 = PDa("p_trA2", [128, 512], BF16)
        p_trB2w = PDa("p_trB2w", [128, 512], BF16)

        w_gate_t = w_gate_h.rearrange("(a p) i -> a p i", p=128)
        w_up_t = w_up_h.rearrange("(a p) i -> a p i", p=128)

        s.wait("vector", wsc_done)
        s.wait("vector", xqT_done)
        fgt = s.emit("vector", lambda h: h.tensor_tensor(
            out=fg_all[:], in0=fx_all[:], in1=wsc[:, 0:1].broadcast_to([128, NT]),
            op=ALU.mult))
        fut = s.emit("vector", lambda h: h.tensor_tensor(
            out=fu_all[:], in0=fx_all[:], in1=wsc[:, 3:4].broadcast_to([128, NT]),
            op=ALU.mult))

        gm_init = [None] * NT
        ld_g = [None] * 4
        ld_ctr = [0]
        wq_guard = None
        gu_par = [None, None]
        st_par = [None, None]
        mmA_par = [None, None]
        for ob in range(8):
            obp = ob % 2
            wgT = wgT2[:, obp * NC * 512:(obp + 1) * NC * 512]
            wuT = wuT2[:, obp * NC * 512:(obp + 1) * NC * 512]
            mmA_guard = mmA_par[obp]
            for (wt, wdst, wi) in ((w_gate_t, wgT, 0), (w_up_t, wuT, 1)):
                ev_wq = None
                for osub in range(4):
                    pc = ob * 4 + osub
                    r = ld_ctr[0] % 4
                    ld_ctr[0] += 1
                    buf = wstgD[:, r * DM:(r + 1) * DM]
                    s.wait("sync", ld_g[r])
                    ldw = dma("sync", buf, wt[pc], "wD%d" % r)
                    s.wait("vector", ldw)
                    tq = ternary(wqb[:], buf, wi, [wq_guard])
                    ld_g[r] = tq
                    wq_guard = tq
                    s.wait("tensor", tq)
                    for g in range(2):
                        s.wait("tensor", ev_wq or mmA_guard)
                        for k4 in range(4):
                            ic = g * 4 + k4
                            tt = s.emit("tensor", lambda h, ic=ic, k4=k4: h.transpose(
                                out=p_trA2[:, k4 * 128:(k4 + 1) * 128],
                                in_=wqb[:, ic * 128:(ic + 1) * 128],
                                identity=ident_bf[:]), inc=(1 if k4 == 3 else 0))
                        s.wait("scalar", tt)
                        s.wait("scalar", ev_wq)
                        dst = wdst[:].rearrange("p (c o) -> p c o", o=512)[
                            :, g * 4:(g + 1) * 4, osub * 128:(osub + 1) * 128]
                        ev_wq = s.emit("scalar", lambda h, d=dst: h.activation(
                            out=d, in_=p_trA2[:].rearrange("p (c o) -> p c o", o=128),
                            func=AF.Copy))
                    wq_guard = ev_wq
            wT_done = s.token("scalar")
            for _ in range(2):
                next(wd_gen, None)

            s.wait("tensor", wT_done)
            s.wait("tensor", fgt); s.wait("tensor", fut)
            for tch in range(NT):
                par = tch % 2
                pg = p_gA[:, par * 512:(par + 1) * 512]
                pu = p_uA[:, par * 512:(par + 1) * 512]
                sg_ = sgb[:, par * 512:(par + 1) * 512]
                gu_ = gub[:, par * 512:(par + 1) * 512]
                s.wait("tensor", gu_par[par])
                for ic in range(NC):
                    s.emit("tensor", lambda h, ic=ic, tch=tch, pg=pg, wgT=wgT: h.matmul(
                        out=pg, lhsT=xqT[:, ic * T + tch * 128: ic * T + (tch + 1) * 128],
                        rhs=wgT[:, ic * 512:(ic + 1) * 512],
                        start=(ic == 0), stop=(ic == NC - 1)))
                for ic in range(NC):
                    mt = s.emit("tensor", lambda h, ic=ic, tch=tch, pu=pu, wuT=wuT: h.matmul(
                        out=pu, lhsT=xqT[:, ic * T + tch * 128: ic * T + (tch + 1) * 128],
                        rhs=wuT[:, ic * 512:(ic + 1) * 512],
                        start=(ic == 0), stop=(ic == NC - 1)), inc=(1 if ic == NC - 1 else 0))
                s.wait("scalar", mt)
                s.wait("scalar", gu_par[par])
                sgev = s.emit("scalar", lambda h, tch=tch, pg=pg, sg_=sg_: h.activation(
                    out=sg_, in_=pg, func=AF.Silu, scale=fg_all[:, tch:tch + 1]))
                s.wait("vector", sgev)
                s.wait("vector", st_par[par])
                guv = s.emit("vector", lambda h, tch=tch, pu=pu, sg_=sg_, gu_=gu_: h.scalar_tensor_tensor(
                    out=gu_, in0=pu, scalar=fu_all[:, tch:tch + 1], in1=sg_,
                    op0=ALU.mult, op1=ALU.mult))
                gu_par[par] = guv
                s.wait("vector", guv)
                rm = s.emit("vector", lambda h, par=par, gu_=gu_: h.reduce_max(
                    out=qcolD[:, par:par + 1], in_=gu_, axis=AXX, apply_absolute_value=True))
                s.wait("vector", rm)
                if gm_init[tch] is None:
                    gm_init[tch] = s.emit("vector", lambda h, tch=tch, par=par: h.tensor_copy(
                        out=gmax_all[:, tch:tch + 1], in_=qcolD[:, par:par + 1]))
                else:
                    gm_init[tch] = s.emit("vector", lambda h, tch=tch, par=par: h.tensor_tensor(
                        out=gmax_all[:, tch:tch + 1], in0=gmax_all[:, tch:tch + 1],
                        in1=qcolD[:, par:par + 1], op=ALU.max))
                s.wait("sync", guv)
                st_par[par] = dma("sync", gu_scr_t[tch * 8 + ob], gu_, "guw%d" % par)
            mmA_guard = s.token("tensor")
        gmax_done = s.token("vector")
        s.barrier()
        es_da.close()
        es_d0.close()
        es_dx.close()

        es_db = ExitStack()
        SDb = lambda nm, shp, dt=F32: es_db.enter_context(nc.sbuf_tensor(nm, shp, dt))
        PDb = lambda nm, shp, dt=F32: es_db.enter_context(nc.psum_tensor(nm, shp, dt))
        tern_scr2 = SDb("ternB2", [128, 2048], F32)
        guh = SDb("guhB", [128, 2 * 2048], BF16)
        qa2 = SDb("qa2B", [128, 2 * 2048], F32)
        qd2 = SDb("qd2B", [128, 2048], F32)
        qs2 = SDb("qs2B", [128, 2048], BF16)
        qbf2 = SDb("qbf2B", [128, 2 * 4096], BF16)
        xq2T = SDb("xq2T", [128, 2 * 32 * 128], BF16)
        x3c2 = SDb("x3c2", [128, 2 * DM], F32)
        outf = SDb("outfD", [128, 2 * DM], F32)
        qcolD2 = SDb("qcolD2", [128, 4], F32)
        ln_sq_holder["cur"] = SDb("ln_sqDB", [128, DM], F32)
        p_dn = PDb("p_dnB", [128, 2 * DM], F32)
        p_trB2 = PDb("p_trB2", [128, 512], BF16)

        tern_scr = tern_scr2
        for _ in wd_gen:
            pass
        wdT_done = s.token("scalar")
        tr_ps_guard = wdT_done
        s.barrier()

        s.wait("vector", gmax_done)
        f2a = s.emit("vector", lambda h: h.tensor_scalar(
            out=f2_all[:], in0=gmax_all[:], scalar1=1.0e-5, scalar2=1.0 / 127.0,
            op0=ALU.max, op1=ALU.mult))
        s.wait("vector", f2a)
        f2b = s.emit("vector", lambda h: h.tensor_tensor(
            out=f2_all[:], in0=f2_all[:], in1=wsc[:, 6:7].broadcast_to([128, NT]),
            op=ALU.mult))

        guh_par = [None, None]
        qs_guard = None
        lad2_prev = [None, None]
        tr2_prev = [None, None]
        x3ld2_g = [None, None]
        out_guards = {0: [], 1: []}
        prev_ln4 = None
        dn_guard = [None, None]
        pend_tail = None
        for tch in range(NT):
            par = tch % 2
            qa2_ = qa2[:, par * 2048:(par + 1) * 2048]
            qd2_ = qd2[:]
            qs2_ = qs2[:]
            qbf2_ = qbf2[:, par * 4096:(par + 1) * 4096]
            xq2T_ = xq2T[:, par * 4096:(par + 1) * 4096]
            x3c2_ = x3c2[:, par * DM:(par + 1) * DM]
            outf_ = outf[:, par * DM:(par + 1) * DM]
            pdn_ = p_dn[:, par * DM:(par + 1) * DM]
            stp_ = tern_scr2[:]
            s.wait("vector", lad2_prev[par])
            s.wait("vector", lad2_prev[1 - par])
            s.wait("vector", gmax_done)
            s2v = s.emit("vector", lambda h, tch=tch, par=par: h.tensor_scalar(
                out=qcolD2[:, 2 * par: 2 * par + 1], in0=gmax_all[:, tch:tch + 1],
                scalar1=1.0e-5, scalar2=1.0 / 127.0, op0=ALU.max, op1=ALU.mult))
            s.wait("vector", s2v)
            s2r = s.emit("vector", lambda h, par=par: h.reciprocal(
                out=qcolD2[:, 2 * par + 1: 2 * par + 2], in_=qcolD2[:, 2 * par: 2 * par + 1]))
            for hf in range(2):
                gh = guh[:, hf * 2048:(hf + 1) * 2048]
                s.wait("sync", guh_par[hf])
                s.wait("sync", s.token("guw0"))
                s.wait("sync", s.token("guw1"))
                glds = [dma("sync", gh[:, b4 * 512:(b4 + 1) * 512],
                            gu_scr_t[tch * 8 + hf * 4 + b4], "gur%d" % (hf * 4 + b4))
                        for b4 in range(4)]
                for g_ in glds:
                    s.wait("scalar", g_)
                s.wait("scalar", s2r)
                s.wait("scalar", lad2_prev[par])
                ab2 = s.emit("scalar", lambda h, gh=gh, par=par, qa2_=qa2_: h.activation(
                    out=qa2_, in_=gh, func=AF.Abs, scale=qcolD2[:, 2 * par + 1: 2 * par + 2]))
                s.wait("scalar", qs_guard)
                sg2 = s.emit("scalar", lambda h, gh=gh, qs2_=qs2_: h.activation(
                    out=qs2_, in_=gh, func=AF.Sign))
                guh_par[hf] = sg2
                lt2 = ladder(qa2_, qd2_, stp_, 2048, ab2)
                s.wait("vector", lt2)
                s.wait("vector", sg2)
                s.wait("vector", tr2_prev[par])
                qv2 = s.emit("vector", lambda h, hf=hf, qbf2_=qbf2_, qa2_=qa2_, qs2_=qs2_: h.tensor_tensor(
                    out=qbf2_[:, hf * 2048:(hf + 1) * 2048], in0=qa2_, in1=qs2_,
                    op=ALU.mult))
                lad2_prev[par] = qv2
                qs_guard = qv2
            s.wait("tensor", lad2_prev[par])
            ev_q2 = None
            for g in range(8):
                s.wait("tensor", ev_q2 or tr2_prev[par])
                s.wait("tensor", tr_ps_guard)
                for k4 in range(4):
                    oc = g * 4 + k4
                    tt = s.emit("tensor", lambda h, oc=oc, k4=k4, qbf2_=qbf2_: h.transpose(
                        out=p_trB2[:, k4 * 128:(k4 + 1) * 128],
                        in_=qbf2_[:, oc * 128:(oc + 1) * 128],
                        identity=ident_bf[:]), inc=(1 if k4 == 3 else 0))
                s.wait("scalar", tt)
                s.wait("scalar", ev_q2)
                dst = xq2T_.rearrange("p (c t) -> p c t", t=128)[:, g * 4:(g + 1) * 4, :]
                ev_q2 = s.emit("scalar", lambda h, d=dst: h.activation(
                    out=d, in_=p_trB2[:].rearrange("p (c t) -> p c t", t=128), func=AF.Copy))
                tr_ps_guard = ev_q2
            tr2_prev[par] = s.token("tensor")
            xq2_done = ev_q2

            s.wait("tensor", xq2_done)
            s.wait("tensor", wdT_done)
            s.wait("tensor", dn_guard[par])
            for oc in range(32):
                for db in range(2):
                    mt = s.emit("tensor", lambda h, oc=oc, db=db, xq2T_=xq2T_, pdn_=pdn_: h.matmul(
                        out=pdn_[:, db * 512:(db + 1) * 512],
                        lhsT=xq2T_[:, oc * 128:(oc + 1) * 128],
                        rhs=wdT[:, oc * DM + db * 512: oc * DM + (db + 1) * 512],
                        start=(oc == 0), stop=(oc == 31)),
                        inc=(1 if oc == 31 and db == 1 else 0))
            s.wait("sync", x3ld2_g[par])
            s.wait("sync", s.token("x3w"))
            ldx3b = dma("sync", x3c2_, x3_scr_t[tch], "x3b%d" % par)

            def dn_tail(tch=tch, par=par, mt=mt, ldx3b=ldx3b, pdn_=pdn_,
                        outf_=outf_, x3c2_=x3c2_):
                nonlocal prev_ln4
                s.wait("vector", mt)
                s.wait("vector", f2b)
                s.wait("vector", prev_ln4)
                for g_ in out_guards[par]:
                    s.wait("vector", g_); s.wait("scalar", g_)
                dq = s.emit("vector", lambda h: h.tensor_scalar(
                    out=outf_, in0=pdn_, scalar1=f2_all[:, tch:tch + 1], scalar2=None,
                    op0=ALU.mult))
                dn_guard[par] = dq
                s.wait("vector", ldx3b)
                s.wait("vector", dq)
                radd = s.emit("vector", lambda h: h.tensor_tensor(
                    out=outf_, in0=outf_, in1=x3c2_, op=ALU.add))
                x3ld2_g[par] = radd
                s.wait("scalar", radd)
                s.wait("vector", radd)
                vt4, st4 = layernorm(outf_, outf_, None)
                prev_ln4 = st4
                s.wait("sync", st4)
                out_guards[par] = [dma("sync", out_t[tch], outf_, "outw%d" % par)]

            if pend_tail is not None:
                pend_tail()
            pend_tail = dn_tail
        pend_tail()

        s.barrier()
        es_db.close()
        es_d.close()

    block = es.enter_context(nc.Block())
    for e, attr in (("sync", block.sync), ("tensor", block.tensor), ("vector", block.vector),
                    ("scalar", block.scalar), ("gpsimd", block.gpsimd)):
        def make_body(ops):
            def body(hh):
                for f in ops:
                    f(hh)
            return body
        attr(make_body(s.q[e]))
    es.close()
    mybir.codegen_inst_isa_subclasses(nc)
    return nc


_NC_CACHE = {}


def _get_nc(stop_after):
    if stop_after not in _NC_CACHE:
        _NC_CACHE[stop_after] = build_program(stop_after)
    return _NC_CACHE[stop_after]


def kernel(**inputs):
    nc = _get_nc(STOP_AFTER)

    f32 = lambda a: np.ascontiguousarray(np.asarray(a, dtype=np.float32))
    common = {
        "W_attn": f32(inputs["L0_attn_W_attn"]),
        "b_attn": f32(inputs["L0_attn_b_attn"]),
        "W_proj": f32(inputs["L0_attn_W_proj"]),
        "b_proj": f32(inputs["L0_attn_b_proj"]),
        "Wg": f32(inputs["L0_ffn_Wg"]),
        "bg": f32(inputs["L0_ffn_bg"]),
        "W1": f32(inputs["L0_ffn_W1"]),
        "b1": f32(inputs["L0_ffn_b1"]),
        "W2": f32(inputs["L0_ffn_W2"]),
        "b2": f32(inputs["L0_ffn_b2"]),
        "W_in": f32(inputs["L1_attn_W_in"]),
        "W_xproj": f32(inputs["L1_attn_W_xproj"]),
        "W_dt": f32(inputs["L1_attn_W_dt"]),
        "b_dt": f32(inputs["L1_attn_b_dt"]),
        "A_log": f32(inputs["L1_attn_A_log"]),
        "Dvec": f32(inputs["L1_attn_D"]),
        "W_out": f32(inputs["L1_attn_W_out"]),
        "w_gate": f32(inputs["L1_ffn_w_gate"]),
        "w_up": f32(inputs["L1_ffn_w_up"]),
        "w_down": f32(inputs["L1_ffn_w_down"]),
    }
    p = np.arange(128).reshape(128, 1)
    jj = np.arange(T).reshape(1, T)
    alibi_base = np.where(p <= jj, (p - jj).astype(np.float32), np.float32(NEGBIG))
    common["alibi_base"] = alibi_base.astype(np.float32)
    common["ident_bf"] = np.eye(128, dtype=ml_dtypes.bfloat16)
    common["ident_f"] = np.eye(128, dtype=np.float32)
    common["ones_bf"] = np.ones((1, 128), dtype=ml_dtypes.bfloat16)
    common["ones_f"] = np.ones((1, 128), dtype=np.float32)

    x_full = f32(inputs["L0_attn_x"])
    in_maps = []
    for b in range(8):
        m = dict(common)
        m["x0"] = np.ascontiguousarray(x_full[b])
        in_maps.append(m)

    res = run_bass_kernel_spmd(nc, in_maps, core_ids=list(range(8)))
    out = np.stack([np.asarray(r["out"], dtype=np.float32) for r in res.results], axis=0)
    return out


# version 24
# speedup vs baseline: 1.0040x; 1.0040x over previous
import numpy as np
import ml_dtypes
from contextlib import ExitStack

import concourse.bass as bass
import concourse.mybir as mybir
from concourse.bass_utils import run_bass_kernel_spmd
from concourse import library_config

F32 = mybir.dt.float32
BF16 = mybir.dt.bfloat16
AF = mybir.ActivationFunctionType
ALU = mybir.AluOpType
AXX = mybir.AxisListType.X

T = 2048
DM = 1024
NT = T // 128
NC = DM // 128
H = 16
HD = 64
DI = 2048
DS = 16
DTR = 64
INNER = 4096
NEGBIG = -1.0e30
SIGBIG = 1.0e30

ENGS = ("sync", "tensor", "vector", "scalar", "gpsimd")

STOP_AFTER = "full"


class Sched:
    def __init__(self, nc, es):
        self.nc = nc
        self.es = es
        self.sems = {}
        self.done = {}
        self.q = {e: [] for e in ENGS}
        self.last_wait = {c: {} for c in ENGS}
        for e in ENGS:
            self._newsem(e)

    def _newsem(self, key):
        self.sems[key] = self.es.enter_context(self.nc.semaphore("sem_" + str(key)))
        self.done[key] = 0

    def emit(self, eng, fn, inc=1, key=None):
        key = key or eng
        if inc:
            if key not in self.sems:
                self._newsem(key)
            sem = self.sems[key]
            self.done[key] += inc
            def wrapped(h, fn=fn, sem=sem, inc=inc):
                fn(h).then_inc(sem, inc)
            self.q[eng].append(wrapped)
        else:
            self.q[eng].append(fn)
        return (key, self.done.get(key, 0))

    def dma(self, eng, slot, out_ap, in_ap):
        return self.emit(eng, lambda h, o=out_ap, i=in_ap: h.dma_start(out=o, in_=i),
                         inc=16, key=slot)

    def token(self, key):
        return (key, self.done.get(key, 0))

    def wait(self, consumer, tok):
        if tok is None:
            return
        key, val = tok
        if val <= 0:
            return
        if self.last_wait[consumer].get(key, 0) >= val:
            return
        self.last_wait[consumer][key] = val
        sem = self.sems[key]
        self.q[consumer].append(lambda h, sem=sem, val=val: h.wait_ge(sem, val))

    def bump(self, eng):
        sem = self.sems[eng]
        self.done[eng] += 1
        self.q[eng].append(lambda h, sem=sem: h.sem_inc(sem, 1))
        return (eng, self.done[eng])

    def barrier(self):
        toks = [(k, v) for k, v in self.done.items() if v > 0]
        for c in ENGS:
            for t in toks:
                self.wait(c, t)


def build_program(stop_after=STOP_AFTER):
    nc = bass.Bass("TRN2", target_bir_lowering=False)

    def din(name, shape, dtype=F32):
        return nc.dram_tensor(name, shape, dtype, kind="ExternalInput").ap()

    x0_h = din("x0", [T, DM])
    W_attn_h = din("W_attn", [DM, 3 * DM])
    b_attn_h = din("b_attn", [3 * DM])
    W_proj_h = din("W_proj", [DM, DM])
    b_proj_h = din("b_proj", [DM])
    Wg_h = din("Wg", [DM, 8])
    bg_h = din("bg", [8])
    W1_h = din("W1", [8, DM, DM])
    b1_h = din("b1", [8, DM])
    W2_h = din("W2", [8, DM, DM])
    b2_h = din("b2", [8, DM])
    W_in_h = din("W_in", [DM, 2 * DI])
    W_xproj_h = din("W_xproj", [DI, DTR + 2 * DS])
    W_dt_h = din("W_dt", [DTR, DI])
    b_dt_h = din("b_dt", [DI])
    A_log_h = din("A_log", [DI, DS])
    Dvec_h = din("Dvec", [DI])
    W_out_h = din("W_out", [DI, DM])
    w_gate_h = din("w_gate", [INNER, DM])
    w_up_h = din("w_up", [INNER, DM])
    w_down_h = din("w_down", [DM, INNER])

    ident_bf_h = din("ident_bf", [128, 128], BF16)
    ident_f_h = din("ident_f", [128, 128], F32)
    alibi_h = din("alibi_base", [128, T])
    ones_bf_h = din("ones_bf", [1, 128], BF16)
    ones_f_h = din("ones_f", [1, 128], F32)

    out_h = nc.dram_tensor("out", [T, DM], F32, kind="ExternalOutput").ap()

    x1_scr = nc.dram_tensor("x1_scr", [T, DM], F32, kind="Internal").ap()
    x2_scr = nc.dram_tensor("x2_scr", [T, DM], F32, kind="Internal").ap()
    x3_scr = nc.dram_tensor("x3_scr", [T, DM], F32, kind="Internal").ap()
    wout_scr = nc.dram_tensor("wout_scr", [DI, DM], BF16, kind="Internal").ap()
    gu_scr = nc.dram_tensor("gu_scr", [NT * 8 * 128, 512], BF16, kind="Internal").ap()

    es = ExitStack()
    SB = lambda nm, shp, dt=F32: es.enter_context(nc.sbuf_tensor(nm, shp, dt))
    PS = lambda nm, shp, dt=F32: es.enter_context(nc.psum_tensor(nm, shp, dt))
    s = Sched(nc, es)

    def dma(eng, out_ap, in_ap, slot="misc"):
        return s.dma(eng, slot, out_ap, in_ap)

    _constv_cache = {}
    _constv_toks = {}
    def constv(val):
        if val not in _constv_cache:
            t = SB("constv_%d" % len(_constv_cache), [128, 1], F32)
            _constv_toks[val] = s.emit("vector", lambda h, t=t, v=val: h.memset(t[:], v), inc=1)
            _constv_cache[val] = t
        return _constv_cache[val][:]

    for _v in (1.0e-5, 1.0):
        constv(_v)

    ident_bf = SB("c_ident_bf", [128, 128], BF16)
    ident_f = SB("c_ident_f", [128, 128], F32)
    ones_bf = SB("c_ones_bf", [1, 128], BF16)
    ones_f = SB("c_ones_f", [1, 128], F32)
    dma("sync", ones_f[:], ones_f_h, "consts")
    dma("sync", ident_bf[:], ident_bf_h, "consts")
    dma("sync", ident_f[:], ident_f_h, "consts")
    consts_tok = dma("sync", ones_bf[:], ones_bf_h, "consts")

    ln_m1 = SB("ln_m1", [128, 1]); ln_m2 = SB("ln_m2", [128, 1])
    ln_mean = SB("ln_mean", [128, 1]); ln_ex2 = SB("ln_ex2", [128, 1])
    ln_var = SB("ln_var", [128, 1]); ln_sd = SB("ln_sd", [128, 1])
    ln_rstd = SB("ln_rstd", [128, 1]); ln_nmr = SB("ln_nmr", [128, 1])
    ln_sq_holder = {}

    def layernorm(r_ap, out_f32_ap, out_bf_ap, width=DM):
        t0 = s.emit("vector", lambda h: h.reduce_sum(out=ln_m1[:], in_=r_ap, axis=AXX))
        s.wait("vector", t0)
        v1 = s.emit("vector", lambda h: h.tensor_scalar(
            out=ln_mean[:], in0=ln_m1[:], scalar1=1.0 / width, scalar2=None, op0=ALU.mult))
        ln_sq = ln_sq_holder["cur"]
        sqt = s.emit("scalar", lambda h, ln_sq=ln_sq: h.activation(
            out=ln_sq[:, :width], in_=r_ap, func=AF.Square, accum_out=ln_m2[:]))
        s.wait("vector", sqt)
        t1 = s.emit("vector", lambda h: h.tensor_scalar(
            out=ln_ex2[:], in0=ln_m2[:], scalar1=1.0 / width, scalar2=None, op0=ALU.mult))
        s.wait("vector", v1)
        t2 = s.emit("vector", lambda h: h.tensor_tensor(
            out=ln_var[:], in0=ln_mean[:], in1=ln_mean[:], op=ALU.mult))
        s.wait("vector", t1); s.wait("vector", t2)
        v2 = s.emit("vector", lambda h: h.tensor_tensor(
            out=ln_var[:], in0=ln_ex2[:], in1=ln_var[:], op=ALU.subtract))
        s.wait("scalar", v2)
        eps = constv(1.0e-5)
        s3 = s.emit("scalar", lambda h, eps=eps: h.activation(
            out=ln_sd[:], in_=ln_var[:], func=AF.Sqrt, bias=eps))
        s.wait("vector", s3)
        t3 = s.emit("vector", lambda h: h.reciprocal(out=ln_rstd[:], in_=ln_sd[:]))
        s.wait("vector", t3)
        v3 = s.emit("vector", lambda h: h.tensor_scalar(
            out=ln_nmr[:], in0=ln_mean[:], scalar1=ln_rstd[:], scalar2=-1.0,
            op0=ALU.mult, op1=ALU.mult))
        s.wait("vector", v3)
        sc_tok = None
        vec_tok = None
        if out_f32_ap is not None:
            sc_tok = s.emit("vector", lambda h: h.tensor_scalar(
                out=out_f32_ap, in0=r_ap, scalar1=ln_rstd[:], scalar2=ln_nmr[:],
                op0=ALU.mult, op1=ALU.add))
        if out_bf_ap is not None:
            vec_tok = s.emit("vector", lambda h: h.tensor_scalar(
                out=out_bf_ap, in0=r_ap, scalar1=ln_rstd[:], scalar2=ln_nmr[:],
                op0=ALU.mult, op1=ALU.add))
        return vec_tok, sc_tok

    es_ab = ExitStack()
    x1T = es_ab.enter_context(nc.sbuf_tensor("xT_hold", [128, NC * T], BF16))

    es_a = ExitStack()
    SBa = lambda nm, shp, dt=F32: es_a.enter_context(nc.sbuf_tensor(nm, shp, dt))
    PSa = lambda nm, shp, dt=F32: es_a.enter_context(nc.psum_tensor(nm, shp, dt))
    qkT = SBa("qkT", [128, 16 * T], BF16)
    VW = H * 65
    v_store = SBa("v_store", [128, NT * VW], BF16)

    es_a1 = ExitStack()
    SB1 = lambda nm, shp, dt=F32: es_a1.enter_context(nc.sbuf_tensor(nm, shp, dt))
    PS1 = lambda nm, shp, dt=F32: es_a1.enter_context(nc.psum_tensor(nm, shp, dt))
    x0T = SB1("x0T", [128, NC * T], BF16)
    wqk = SB1("wqk", [128, NC * DM], BF16)
    wv = wqk
    wstage = SB1("wstageA", [128, 4 * DM], F32)
    xnat = wstage
    battn_qk = SB1("battn_qk", [128, 24], F32)
    bvb = SB1("bvb", [128, DM], F32)
    brow = SB1("brow_misc", [16, 128], F32)
    p_tr = PS1("p_trA", [128, 512], F32)
    p_mm = PS1("p_mmA", [128, 1024], F32)
    p_small = PS1("p_smallA", [128, 128], F32)

    x0_tiled = x0_h.rearrange("(c p) d -> c p d", p=128)
    W_attn_t = W_attn_h.rearrange("(c p) d -> c p d", p=128)

    tr_use = [None, None, None, None]
    ev_tok = None
    for tc in range(NT):
        buf = xnat[:, (tc % 4) * DM:(tc % 4) * DM + DM]
        s.wait("sync", tr_use[tc % 4])
        ld = dma("sync", buf, x0_tiled[tc], "xnat%d" % (tc % 4))
        s.wait("tensor", ld)
        s.wait("tensor", consts_tok)
        for g in range(2):
            s.wait("tensor", ev_tok)
            for k in range(4):
                cc = g * 4 + k
                tt = s.emit("tensor", lambda h, b=buf, cc=cc, k=k: h.transpose(
                    out=p_tr[:, k * 128:(k + 1) * 128], in_=b[:, cc * 128:(cc + 1) * 128],
                    identity=ident_f[:]), inc=(1 if k == 3 else 0))
            s.wait("scalar", tt)
            dst = x0T[:].rearrange("p (c t) -> p c t", t=T)[:, g * 4:(g + 1) * 4,
                                                            tc * 128:(tc + 1) * 128]
            ev_tok = s.emit("scalar", lambda h, d=dst: h.activation(
                out=d, in_=p_tr[:].rearrange("p (c t) -> p c t", t=128), func=AF.Copy), inc=1)
        tr_use[tc % 4] = s.token("tensor")
    x0T_done = s.token("scalar")

    ld = dma("sync", brow[:, :], b_attn_h[0:2048].rearrange("(a p) -> a p", p=128), "b_brow")
    s.wait("tensor", ld)
    s.wait("tensor", x0T_done)
    tt = s.emit("tensor", lambda h: h.transpose(out=p_small[:, 0:16], in_=brow[:, :],
                                                identity=ident_f[0:16, 0:16]), inc=1)
    s.wait("scalar", tt)
    battn_tok = s.emit("scalar", lambda h: h.activation(
        out=battn_qk[:, 0:16], in_=p_small[:, 0:16], func=AF.Copy), inc=1)
    s.wait("vector", battn_tok)
    battn_tok = s.emit("vector", lambda h: h.tensor_scalar(
        out=battn_qk[:, 16:24], in0=battn_qk[:, 0:8], scalar1=0.125, scalar2=None,
        op0=ALU.mult))
    bvb_tok = dma("sync", bvb[:], b_attn_h[2048:3072].unsqueeze(0).partition_broadcast(128), "b_bv")
    cvd = [tr_use[0], tr_use[1], tr_use[2], tr_use[3]]
    for cc in range(NC):
        buf = wstage[:, (cc % 4) * DM:(cc % 4) * DM + DM]
        s.wait("sync", cvd[cc % 4])
        ldw = dma("sync", buf, W_attn_t[cc][:, 2048:3072], "wstg%d" % (cc % 4))
        s.wait("vector", ldw)
        cvd[cc % 4] = s.emit("vector", lambda h, b=buf, cc=cc: h.tensor_copy(
            out=wv[:, cc * DM:(cc + 1) * DM], in_=b), inc=1)
    ones_set = s.emit("vector", lambda h: h.memset(
        v_store[:].rearrange("p (t w) -> p t w", w=VW)[:, :, 64::65], 1.0))
    evv = [None, None]
    s.wait("vector", bvb_tok)
    s.wait("tensor", s.token("vector"))
    s.wait("tensor", x0T_done)
    for tch in range(NT):
        for db in range(2):
            r = (tch * 2 + db) % 2
            pz = p_mm[:, r * 512:(r + 1) * 512]
            s.wait("tensor", evv[r])
            for cc in range(NC):
                mt = s.emit("tensor", lambda h, pz=pz, cc=cc, tch=tch, db=db: h.matmul(
                    out=pz, lhsT=x0T[:, cc * T + tch * 128: cc * T + (tch + 1) * 128],
                    rhs=wv[:, cc * DM + db * 512: cc * DM + (db + 1) * 512],
                    start=(cc == 0), stop=(cc == NC - 1)), inc=(1 if cc == NC - 1 else 0))
            s.wait("vector", mt)
            base = tch * VW + db * 520
            dst = v_store[:, base:base + 520].rearrange("p (h e) -> p h e", e=65)[:, :, 0:64]
            evv[r] = s.emit("vector", lambda h, pz=pz, dst=dst, db=db: h.tensor_tensor(
                out=dst, in0=pz.rearrange("p (h e) -> p h e", e=64),
                in1=bvb[:, db * 512:(db + 1) * 512].rearrange("p (h e) -> p h e", e=64),
                op=ALU.add), inc=1)

    v_done = s.token("vector")

    ev = [evv[0], evv[1]]
    for half in range(2):
        tdone = s.token("tensor")
        cvd2 = [tdone, tdone, tdone, tdone]
        for cc in range(NC):
            buf = wstage[:, (cc % 4) * DM:(cc % 4) * DM + DM]
            s.wait("sync", cvd2[cc % 4])
            ldw = dma("sync", buf, W_attn_t[cc][:, half * 1024:(half + 1) * 1024], "wstg%d" % (cc % 4))
            s.wait("vector", ldw)
            s.wait("vector", tdone)
            cvd2[cc % 4] = s.emit("vector", lambda h, b=buf, cc=cc: h.tensor_copy(
                out=wqk[:, cc * DM:(cc + 1) * DM], in_=b), inc=1)
        wconv = s.token("vector")
        s.wait("tensor", wconv)
        s.wait("scalar", battn_tok)
        for cp8 in range(8):
            cp = half * 8 + cp8
            for tb in range(4):
                r = (cp8 * 4 + tb) % 2
                pz = p_mm[:, r * 512:(r + 1) * 512]
                s.wait("tensor", ev[r])
                for cc in range(NC):
                    mt = s.emit("tensor", lambda h, pz=pz, cc=cc, cp8=cp8, tb=tb: h.matmul(
                        out=pz, lhsT=wqk[:, cc * DM + cp8 * 128: cc * DM + (cp8 + 1) * 128],
                        rhs=x0T[:, cc * T + tb * 512: cc * T + (tb + 1) * 512],
                        start=(cc == 0), stop=(cc == NC - 1)), inc=(1 if cc == NC - 1 else 0))
                s.wait("vector", mt)
                s.wait("vector", battn_tok)
                if half == 0:
                    ev[r] = s.emit("vector", lambda h, pz=pz, cp=cp, tb=tb: h.tensor_scalar(
                        out=qkT[:, cp * T + tb * 512: cp * T + (tb + 1) * 512], in0=pz,
                        scalar1=0.125, scalar2=battn_qk[:, 16 + cp:17 + cp],
                        op0=ALU.mult, op1=ALU.add))
                else:
                    ev[r] = s.emit("vector", lambda h, pz=pz, cp=cp, tb=tb: h.tensor_scalar(
                        out=qkT[:, cp * T + tb * 512: cp * T + (tb + 1) * 512], in0=pz,
                        scalar1=battn_qk[:, cp:cp + 1], scalar2=None, op0=ALU.add))

    s.barrier()
    es_a1.close()

    es_a2a = ExitStack()
    attnoT = es_a2a.enter_context(nc.sbuf_tensor("attnoT", [128, NC * T], BF16))
    es_a2 = ExitStack()
    SB2 = lambda nm, shp, dt=F32: es_a2.enter_context(nc.sbuf_tensor(nm, shp, dt))
    PS2 = lambda nm, shp, dt=F32: es_a2.enter_context(nc.psum_tensor(nm, shp, dt))
    alibi = SB2("c_alibi", [128, T], F32)
    alibi_tok = dma("sync", alibi[:], alibi_h, "alibi")
    zbuf = SB2("zbuf", [128, 2 * T], F32)
    attbuf = SB2("attbuf", [128, 2 * T], BF16)
    rb64 = SB2("rb64", [64, T], F32)
    rrow = rb64[0:1, :]
    p_z = PS2("p_z", [128, T], F32)
    p_av = PS2("p_av", [65, T], F32)

    s.wait("tensor", v_done)
    s.wait("tensor", ones_set)
    zf_buf = [None, None]
    exp_sc = {}
    att_par = [None, None]
    avdone_tok = None
    for h_ in range(H):
        slope = 1.0 / (2.0 ** (h_ + 1))
        s.wait("vector", alibi_tok)
        s.wait("tensor", avdone_tok)
        row0 = 64 * (h_ % 2)
        qc = (h_ // 2) * T
        kc = (8 + h_ // 2) * T

        def emit_attv(sc):
            par = sc % 2
            tlen = T - sc * 128
            ab = attbuf[:, par * T: par * T + tlen]
            s.wait("tensor", exp_sc[sc])
            last_tok = None
            for tb in range(4):
                lo = max(tb * 512, sc * 128)
                hi = (tb + 1) * 512
                if lo >= hi:
                    continue
                last = (sc == NT - 1 and tb == 3)
                last_tok = s.emit("tensor", lambda h, sc=sc, h_=h_, lo=lo, hi=hi, ab=ab: h.matmul(
                    out=p_av[:, lo:hi],
                    lhsT=v_store[:, sc * VW + h_ * 65: sc * VW + h_ * 65 + 65],
                    rhs=ab[:, lo - sc * 128: hi - sc * 128],
                    start=(sc == 0), stop=(sc == min(4 * tb + 3, NT - 1)),
                    skip_group_check=True), inc=(1 if last else 0))
            att_par[par] = last_tok

        for sc in range(NT):
            tlen = T - sc * 128
            par = sc % 2
            zb = zbuf[:, par * T: par * T + tlen]
            ab = attbuf[:, par * T: par * T + tlen]
            nch = (tlen + 1023) // 1024
            ztoks = []
            for ch in range(nch):
                off = ch * 1024
                cw = min(1024, tlen - off)
                s.wait("tensor", zf_buf[ch])
                for tb in range(2):
                    cols = min(512, cw - tb * 512)
                    if cols <= 0:
                        continue
                    zt = s.emit("tensor", lambda h, row0=row0, kc=kc, qc=qc, sc=sc,
                                off=off, tb=tb, cols=cols: h.matmul(
                        out=p_z[:, off + tb * 512: off + tb * 512 + cols],
                        lhsT=qkT[row0:row0 + 64, kc + sc * 128: kc + (sc + 1) * 128],
                        rhs=qkT[row0:row0 + 64, qc + sc * 128 + off + tb * 512:
                                qc + sc * 128 + off + tb * 512 + cols],
                        start=True, stop=True, skip_group_check=True))
                ztoks.append(zt)
            if sc > 0:
                emit_attv(sc - 1)
            elif h_ > 0:
                pass
            zf_prev = None
            for ch in range(nch):
                off = ch * 1024
                cw = min(1024, tlen - off)
                s.wait("vector", ztoks[ch])
                s.wait("vector", exp_sc.get(sc - 2))
                zf_buf[ch] = s.emit("vector", lambda h, zb=zb, sl=slope, off=off, cw=cw: h.scalar_tensor_tensor(
                    out=zb[:, off:off + cw], in0=alibi[:, off:off + cw], scalar=sl,
                    in1=p_z[:, off:off + cw], op0=ALU.mult, op1=ALU.add))
                zf_prev = zf_buf[ch]
            s.wait("scalar", zf_prev)
            s.wait("scalar", att_par[par])
            exp_sc[sc] = s.emit("scalar", lambda h, zb=zb, ab=ab: h.activation(
                out=ab, in_=zb, func=AF.Exp))
        emit_attv(NT - 1)
        att_done = att_par[(NT - 1) % 2]
        s.wait("vector", att_done)
        rc = s.emit("vector", lambda h: h.reciprocal(out=rrow, in_=p_av[64:65, :]))
        s.wait("tensor", rc)
        s.wait("tensor", zf_buf[0]); s.wait("tensor", zf_buf[1])
        for tb in range(4):
            rbmm = s.emit("tensor", lambda h, tb=tb: h.matmul(
                out=p_z[0:64, tb * 512:(tb + 1) * 512], lhsT=ones_f[0:1, 0:64],
                rhs=rrow[0:1, tb * 512:(tb + 1) * 512], start=True, stop=True,
                skip_group_check=True), inc=(1 if tb == 3 else 0))
        s.wait("scalar", rbmm)
        rb = s.emit("scalar", lambda h: h.activation(out=rb64[:], in_=p_z[0:64, :],
                                                     func=AF.Copy))
        s.wait("vector", rb)
        dst = attnoT[row0:row0 + 64, qc: qc + T]
        avdone_tok = s.emit("vector", lambda h, dst=dst: h.tensor_tensor(
            out=dst, in0=p_av[0:64, :], in1=rb64[:], op=ALU.mult))
        zf_buf = [rb, rb]
    attno_done = s.token("vector")

    s.barrier()
    es_a2.close()

    es_a3 = ExitStack()
    SB3 = lambda nm, shp, dt=F32: es_a3.enter_context(nc.sbuf_tensor(nm, shp, dt))
    PS3 = lambda nm, shp, dt=F32: es_a3.enter_context(nc.psum_tensor(nm, shp, dt))
    wproj = SB3("wproj", [128, NC * DM], BF16)
    x1f = SB3("x1f", [128, DM], F32)
    r1buf = SB3("r1buf", [128, DM], F32)
    wstage3 = SB3("wstage3", [128, 2 * DM], F32)
    xnat3 = wstage3
    bpb = SB3("bpb", [128, DM], F32)
    ln_sq_holder["cur"] = SB3("ln_sqA", [128, DM], F32)
    p_trB = PS3("p_trB", [128, 512], BF16)
    p_p = PS3("p_p", [128, 2048], F32)
    bpb_tok = dma("sync", bpb[:], b_proj_h.unsqueeze(0).partition_broadcast(128), "b_bp")

    W_proj_t = W_proj_h.rearrange("(c p) d -> c p d", p=128)
    cvd = [None, None]
    for cc in range(NC):
        buf = wstage3[:, (cc % 2) * DM:(cc % 2) * DM + DM]
        s.wait("sync", cvd[cc % 2])
        ldw = dma("sync", buf, W_proj_t[cc], "wstg%d" % (cc % 2))
        s.wait("vector", ldw)
        cvd[cc % 2] = s.emit("vector", lambda h, b=buf, cc=cc: h.tensor_copy(
            out=wproj[:, cc * DM:(cc + 1) * DM], in_=b), inc=1)
    wproj_done = s.token("vector")

    x1_scr_t = x1_scr.rearrange("(c p) d -> c p d", p=128)
    out_t = out_h.rearrange("(c p) d -> c p d", p=128)

    s.wait("tensor", wproj_done)
    s.wait("tensor", attno_done)
    ev_p = None
    x0c_use = [wproj_done, wproj_done]
    x1guards = []
    ev_x1t = None
    prev_stok = None
    prev_trT = None
    x1bf2 = SB3("x1bf2", [128, 2 * DM], BF16)
    pend_tr = None
    prev_vtok = None
    ev_p_par = [None, None]
    trT_par = [None, None]
    for tch in range(NT):
        pp = p_p[:, (tch % 2) * 1024:(tch % 2) * 1024 + 1024]
        x0c = xnat3[:, (tch % 2) * DM:(tch % 2) * DM + DM]
        s.wait("sync", x0c_use[tch % 2])
        ldx = dma("sync", x0c, x0_tiled[tch], "x0c%d" % (tch % 2))
        s.wait("tensor", ev_p_par[tch % 2])
        for cc in range(NC):
            for db in range(2):
                mt = s.emit("tensor", lambda h, cc=cc, db=db, tch=tch, pp=pp: h.matmul(
                    out=pp[:, db * 512:(db + 1) * 512],
                    lhsT=attnoT[:, cc * T + tch * 128: cc * T + (tch + 1) * 128],
                    rhs=wproj[:, cc * DM + db * 512: cc * DM + (db + 1) * 512],
                    start=(cc == 0), stop=(cc == NC - 1)),
                    inc=(1 if cc == NC - 1 and db == 1 else 0))
        if pend_tr is not None:
            pend_tr()
        s.wait("vector", mt)
        s.wait("vector", bpb_tok)
        s.wait("vector", ldx)
        s.wait("vector", prev_stok)
        s.wait("vector", prev_vtok)
        rz = s.emit("vector", lambda h, pp=pp: h.tensor_tensor(
            out=r1buf[:], in0=pp, in1=bpb[:], op=ALU.add))
        ev_p_par[tch % 2] = rz
        s.wait("vector", rz)
        r1v = s.emit("vector", lambda h, x0c=x0c: h.tensor_tensor(
            out=r1buf[:], in0=r1buf[:], in1=x0c, op=ALU.add))
        x0c_use[tch % 2] = r1v
        s.wait("scalar", r1v)
        s.wait("vector", r1v)
        for g_ in (x1guards if tch > 0 else []):
            s.wait("scalar", g_); s.wait("vector", g_)
        xb = x1bf2[:, (tch % 2) * DM:(tch % 2) * DM + DM]
        s.wait("vector", trT_par[tch % 2])
        vtok, stok = layernorm(r1buf[:], x1f[:], xb)
        prev_stok = stok
        prev_vtok = vtok
        s.wait("sync", stok)
        w1 = dma("sync", x1_scr_t[tch], x1f[:], "x1w")
        x1guards = [w1]
        if stop_after == "x1":
            x1guards.append(dma("sync", out_t[tch], x1f[:], "outw"))

        def tr_tail(tch=tch, vtok=vtok, xb=xb):
            nonlocal ev_x1t, prev_trT
            s.wait("tensor", vtok)
            trT_par[tch % 2] = None
            for g in range(2):
                s.wait("tensor", ev_x1t)
                for k in range(4):
                    cc = g * 4 + k
                    tt = s.emit("tensor", lambda h, cc=cc, k=k, xb=xb: h.transpose(
                        out=p_trB[:, k * 128:(k + 1) * 128], in_=xb[:, cc * 128:(cc + 1) * 128],
                        identity=ident_bf[:]), inc=(1 if k == 3 else 0))
                s.wait("scalar", tt)
                dst = x1T[:].rearrange("p (c t) -> p c t", t=T)[:, g * 4:(g + 1) * 4,
                                                                tch * 128:(tch + 1) * 128]
                ev_x1t = s.emit("scalar", lambda h, d=dst: h.activation(
                    out=d, in_=p_trB[:].rearrange("p (c t) -> p c t", t=128), func=AF.Copy), inc=1)
            prev_trT = s.token("tensor")
            trT_par[tch % 2] = prev_trT

        pend_tr = tr_tail
    pend_tr()

    s.barrier()
    es_a3.close()
    es_a2a.close()
    es_a.close()

    if stop_after != "x1":
        es_b = ExitStack()
        SBb = lambda nm, shp, dt=F32: es_b.enter_context(nc.sbuf_tensor(nm, shp, dt))
        PSb = lambda nm, shp, dt=F32: es_b.enter_context(nc.psum_tensor(nm, shp, dt))
        acc = SBb("moe_acc", [128, NT * DM], F32)
        hTe = SBb("hTe", [128, 8 * T], BF16)
        w1b = SBb("w1b", [128, NC * DM], BF16)
        w2b = SBb("w2b", [128, NC * DM], BF16)
        wstgB = SBb("wstgB", [128, 5 * DM], F32)
        wg_bf = SBb("wg_bf", [128, NC * 8], BF16)
        bgb = SBb("bgb", [128, 8], F32)
        lg_all = SBb("lg_all", [128, NT * 8], F32)
        gw_all = SBb("gw_all", [128, NT * 8], F32)
        g_sc1 = SBb("g_sc1", [128, NT * 8], F32)
        g_sc2 = SBb("g_sc2", [128, NT], F32)
        g_sc3 = SBb("g_sc3", [128, NT], F32)
        g_sc4 = SBb("g_sc4", [128, NT], F32)
        b1s = SBb("b1s", [128, 8], F32)
        b1row = SBb("b1row", [8, 128], F32)
        b2bf = SBb("b2bf", [1, DM], BF16)
        b2row = SBb("b2row", [1, DM], F32)
        x1c = SBb("x1c", [128, 2 * DM], F32)
        x2f = SBb("x2f", [128, DM], F32)
        ln_sq_holder["cur"] = SBb("ln_sqB", [128, DM], F32)
        p_h = PSb("p_hB", [128, 1024], F32)
        p_eo = PSb("p_eoB", [128, 2048], F32)
        p_sm = PSb("p_smB", [128, 128], F32)

        wgld = dma("sync", wstgB[:, 0:64], Wg_h.rearrange("(c p) e -> p c e", p=128), "wstg0")
        s.wait("vector", wgld)
        wgcv = s.emit("vector", lambda h: h.tensor_copy(out=wg_bf[:], in_=wstgB[:, 0:64]))
        bgt = dma("sync", bgb[:], bg_h.unsqueeze(0).partition_broadcast(128), "b_bg")
        s.wait("tensor", wgcv)
        ev_g = None
        for tch in range(NT):
            for cc in range(NC):
                gt = s.emit("tensor", lambda h, cc=cc, tch=tch: h.matmul(
                    out=p_sm[:, 0:8], lhsT=x1T[:, cc * T + tch * 128: cc * T + (tch + 1) * 128],
                    rhs=wg_bf[:, cc * 8:(cc + 1) * 8],
                    start=(cc == 0), stop=(cc == NC - 1)), inc=(1 if cc == NC - 1 else 0))
            s.wait("vector", gt)
            s.wait("vector", bgt)
            ev_g = s.emit("vector", lambda h, tch=tch: h.tensor_tensor(
                out=lg_all[:, tch * 8:(tch + 1) * 8], in0=p_sm[:, 0:8], in1=bgb[:], op=ALU.add))
            s.wait("tensor", ev_g)
        lg3 = lg_all[:].rearrange("p (t e) -> p t e", e=8)
        s.wait("vector", ev_g)
        gmax = s.emit("vector", lambda h: h.reduce_max(out=g_sc2[:], in_=lg3, axis=AXX))
        s.wait("scalar", ev_g)
        gexp = s.emit("scalar", lambda h: h.activation(out=g_sc1[:], in_=lg_all[:], func=AF.Exp))
        s.wait("vector", gexp)
        gsum = s.emit("vector", lambda h: h.reduce_sum(
            out=g_sc3[:], in_=g_sc1[:].rearrange("p (t e) -> p t e", e=8), axis=AXX))
        s.wait("scalar", gmax)
        gelm = s.emit("scalar", lambda h: h.activation(out=g_sc4[:], in_=g_sc2[:], func=AF.Exp))
        s.wait("vector", gsum)
        grs = s.emit("vector", lambda h: h.reciprocal(out=g_sc3[:], in_=g_sc3[:]))
        s.wait("vector", gelm); s.wait("vector", grs)
        gw = s.emit("vector", lambda h: h.tensor_tensor(
            out=g_sc4[:], in0=g_sc4[:], in1=g_sc3[:], op=ALU.mult))
        s.wait("vector", gw)
        gmask = s.emit("vector", lambda h: h.tensor_tensor(
            out=gw_all[:].rearrange("p (t e) -> p t e", e=8), in0=lg3,
            in1=g_sc2[:].unsqueeze(2).broadcast_to([128, NT, 8]), op=ALU.is_ge))
        s.wait("vector", gmask)
        gwa = s.emit("vector", lambda h: h.tensor_tensor(
            out=gw_all[:].rearrange("p (t e) -> p t e", e=8),
            in0=gw_all[:].rearrange("p (t e) -> p t e", e=8),
            in1=g_sc4[:].unsqueeze(2).broadcast_to([128, NT, 8]), op=ALU.mult))

        W1_t = W1_h.rearrange("e (c p) d -> e c p d", p=128)
        W2_t = W2_h.rearrange("e (c p) d -> e c p d", p=128)
        acc_tok = [None] * NT

        stream_prev = {"cvt": wgcv, "ctr": 0}

        def stream_w(e_, dstbuf, Wt, base_tok, tag):
            cvt = [stream_prev["cvt"]] * 5
            first = True
            for cc in range(NC):
                r = (stream_prev["ctr"] + cc) % 5
                buf = wstgB[:, r * DM:(r + 1) * DM]
                s.wait("sync", cvt[r])
                ldt = dma("sync", buf, Wt[e_][cc], "wstg%d" % r)
                s.wait("vector", ldt)
                if first:
                    s.wait("vector", base_tok)
                    first = False
                cvt[r] = s.emit("vector", lambda h, b=buf, cc=cc, d=dstbuf: h.tensor_copy(
                    out=d[:, cc * DM:(cc + 1) * DM], in_=b))
            stream_prev["ctr"] = (stream_prev["ctr"] + NC) % 5
            stream_prev["cvt"] = s.token("vector")
            return stream_prev["cvt"]

        w1_tok = stream_w(0, w1b, W1_t, s.token("vector"), "w1")
        x1_rd = [None, None]
        x2guards = []
        prev_ln = None
        last_h_mm = None
        last_eo_mm = None
        ev_h = [None, None]
        ep_guard = [None, None]
        for e_ in range(8):
            b1ld = dma("sync", b1row[:], b1_h[e_].rearrange("(a p) -> a p", p=128), "b_b1")
            s.wait("tensor", b1ld)
            s.wait("tensor", s.token("scalar"))
            b1t = s.emit("tensor", lambda h: h.transpose(
                out=p_sm[:, 0:8], in_=b1row[:], identity=ident_f[0:8, 0:8]))
            s.wait("scalar", b1t)
            b1ev = s.emit("scalar", lambda h: h.activation(out=b1s[:], in_=p_sm[:, 0:8],
                                                           func=AF.Copy))
            w2_tok = stream_w(e_, w2b, W2_t, last_eo_mm, "w2")
            s.wait("tensor", w1_tok)
            for hc in range(NC):
                for tb in range(4):
                    r = (hc * 4 + tb) % 2
                    pz = p_h[:, r * 512:(r + 1) * 512]
                    s.wait("tensor", ev_h[r])
                    for cc in range(NC):
                        mt = s.emit("tensor", lambda h, pz=pz, cc=cc, hc=hc, tb=tb: h.matmul(
                            out=pz, lhsT=w1b[:, cc * DM + hc * 128: cc * DM + (hc + 1) * 128],
                            rhs=x1T[:, cc * T + tb * 512: cc * T + (tb + 1) * 512],
                            start=(cc == 0), stop=(cc == NC - 1)), inc=(1 if cc == NC - 1 else 0))
                    s.wait("scalar", mt)
                    s.wait("scalar", b1ev)
                    ev_h[r] = s.emit("scalar", lambda h, pz=pz, hc=hc, tb=tb: h.activation(
                        out=hTe[:, hc * T + tb * 512: hc * T + (tb + 1) * 512], in_=pz,
                        func=AF.Gelu, bias=b1s[:, hc:hc + 1]))
            h_done = s.token("scalar")
            last_h_mm = s.token("tensor")
            b2ld = dma("sync", b2row[:], b2_h[e_].unsqueeze(0), "b_b2")
            s.wait("vector", b2ld)
            b2cv = s.emit("vector", lambda h: h.tensor_copy(out=b2bf[:], in_=b2row[:]))
            if e_ < 7:
                w1_tok = stream_w(e_ + 1, w1b, W1_t, last_h_mm, "w1")
            s.wait("tensor", w2_tok)
            s.wait("tensor", h_done)
            s.wait("tensor", b2cv)
            s.wait("tensor", gwa)
            ep_par = [ep_guard[0], ep_guard[1]]
            for tch in range(NT):
                par = tch % 2
                peo = p_eo[:, par * 1024:(par + 1) * 1024]
                s.wait("tensor", ep_par[par])
                for db in range(2):
                    s.emit("tensor", lambda h, db=db, peo=peo: h.matmul(
                        out=peo[:, db * 512:(db + 1) * 512], lhsT=ones_bf[0:1, :],
                        rhs=b2bf[0:1, db * 512:(db + 1) * 512],
                        start=True, stop=False))
                    for hc in range(NC):
                        mt = s.emit("tensor", lambda h, hc=hc, db=db, tch=tch, peo=peo: h.matmul(
                            out=peo[:, db * 512:(db + 1) * 512],
                            lhsT=hTe[:, hc * T + tch * 128: hc * T + (tch + 1) * 128],
                            rhs=w2b[:, hc * DM + db * 512: hc * DM + (db + 1) * 512],
                            start=False, stop=(hc == NC - 1)),
                            inc=(1 if hc == NC - 1 and db == 1 else 0))
                s.wait("vector", mt)
                s.wait("vector", acc_tok[tch])
                gwcol = gw_all[:, tch * 8 + e_: tch * 8 + e_ + 1]
                if e_ == 0:
                    acc_tok[tch] = s.emit("vector", lambda h, tch=tch, gwcol=gwcol, peo=peo: h.tensor_scalar(
                        out=acc[:, tch * DM:(tch + 1) * DM], in0=peo,
                        scalar1=gwcol, scalar2=None, op0=ALU.mult))
                else:
                    acc_tok[tch] = s.emit("vector", lambda h, tch=tch, gwcol=gwcol, peo=peo: h.scalar_tensor_tensor(
                        out=acc[:, tch * DM:(tch + 1) * DM], in0=peo, scalar=gwcol,
                        in1=acc[:, tch * DM:(tch + 1) * DM], op0=ALU.mult, op1=ALU.add))
                ep_par[par] = acc_tok[tch]
                if e_ == 7:
                    x1cb = x1c[:, (tch % 2) * DM:(tch % 2) * DM + DM]
                    s.wait("sync", x1_rd[tch % 2])
                    s.wait("sync", s.token("x1w"))
                    ldx = dma("sync", x1cb, x1_scr_t[tch], "x1r%d" % (tch % 2))
                    s.wait("vector", ldx)
                    s.wait("vector", acc_tok[tch])
                    for g_ in x2guards:
                        s.wait("vector", g_); s.wait("scalar", g_)
                    s.wait("vector", prev_ln)
                    r2v = s.emit("vector", lambda h, tch=tch, x1cb=x1cb: h.tensor_tensor(
                        out=acc[:, tch * DM:(tch + 1) * DM], in0=acc[:, tch * DM:(tch + 1) * DM],
                        in1=x1cb, op=ALU.add))
                    x1_rd[tch % 2] = r2v
                    ep_par[par] = r2v
                    s.wait("scalar", r2v)
                    s.wait("vector", r2v)
                    vtok, stok = layernorm(acc[:, tch * DM:(tch + 1) * DM], x2f[:], None)
                    prev_ln = stok
                    s.wait("sync", stok)
                    w2_ = dma("sync", x2_scr.rearrange("(c p) d -> c p d", p=128)[tch],
                              x2f[:], "x2w")
                    x2guards = [w2_]
                    if stop_after == "x2":
                        x2guards.append(dma("sync", out_t[tch], x2f[:], "outw"))
            ep_guard = ep_par
            last_eo_mm = s.token("tensor")

        s.barrier()
        es_b.close()
        es_ab.close()

    if stop_after not in ("x1", "x2"):
        TC = 1024
        NK = T // TC
        NDH = 2
        NQ = 8
        NSUB = TC // 128
        es_c = ExitStack()
        SBc = lambda nm, shp, dt=F32: es_c.enter_context(nc.sbuf_tensor(nm, shp, dt))
        PSc = lambda nm, shp, dt=F32: es_c.enter_context(nc.psum_tensor(nm, shp, dt))
        bc_scr = nc.dram_tensor("bc_scr", [32, T], F32, kind="Internal").ap()
        win_scr = nc.dram_tensor("win_scr", [DM, 2 * DI], BF16, kind="Internal").ap()
        w_eff = SBc("w_eff", [128, NC * 96], BF16)
        w_dt = SBc("w_dt", [64, DI], BF16)
        dltT = SBc("dltT", [64, T], BF16)
        A_all = SBc("A_all", [128, 16 * DS], F32)
        D_all = SBc("D_all", [128, 16], F32)
        bdt_all = SBc("bdt_all", [128, 16], F32)
        h_state = SBc("h_state", [128, 16 * DS], F32)
        wo_stg = SBc("wo_stg", [128, 2 * DM], F32)
        wo_bf = SBc("wo_bf", [128, 2 * DM], BF16)
        ln_sq_holder["cur"] = SBc("ln_sqC", [128, DM], F32)

        W_in_t = W_in_h.rearrange("(c p) d -> c p d", p=128)
        win_scr_t = win_scr.rearrange("(c p) d -> c p d", p=128)
        W_out_t = W_out_h.rearrange("(c p) d -> c p d", p=128)
        wout_scr_t = wout_scr.rearrange("(c p) d -> c p d", p=128)

        es_c1 = ExitStack()
        SC1 = lambda nm, shp, dt=F32: es_c1.enter_context(nc.sbuf_tensor(nm, shp, dt))
        PC1 = lambda nm, shp, dt=F32: es_c1.enter_context(nc.psum_tensor(nm, shp, dt))
        winT = SC1("winT", [128, 16 * DM], BF16)
        wpc = SC1("wpcC", [128, 2 * 4096], BF16)
        wstgC = SC1("wstgC", [128, 4 * 2048], F32)
        wxp_bf = SC1("wxp_bf", [128, 16 * 96], BF16)
        smallr = SC1("smallr", [16, 128], F32)
        p_trC = PC1("p_trC", [128, 512], BF16)
        p_we = PC1("p_weC", [128, 96], F32)

        tv0 = s.token("vector")
        cvtC = [tv0, tv0, tv0, tv0]
        wst = [None, None]
        rdC = [None, None]
        ev_wt = None
        for cc in range(NC):
            cp = cc % 2
            for hf in range(2):
                r = (cc * 2 + hf) % 4
                buf = wstgC[:, r * 2048:(r + 1) * 2048]
                s.wait("sync", cvtC[r])
                ldt = dma("sync", buf, W_in_t[cc][:, hf * 2048:(hf + 1) * 2048],
                          "wstg%d" % r)
                s.wait("vector", ldt)
                s.wait("vector", wst[cp])
                s.wait("vector", rdC[cp])
                cvtC[r] = s.emit("vector", lambda h, b=buf, hf=hf, cp=cp: h.tensor_copy(
                    out=wpc[:, cp * 4096 + hf * 2048: cp * 4096 + (hf + 1) * 2048],
                    in_=b))
            r0 = (cc * 2) % 4
            r1 = (cc * 2 + 1) % 4
            s.wait("sync", cvtC[r0]); s.wait("sync", cvtC[r1])
            wst[cp] = dma("sync", win_scr_t[cc],
                          wpc[:, cp * 4096:(cp + 1) * 4096], "winw%d" % cp)
            s.wait("tensor", cvtC[r0])
            for dc in range(16):
                k4 = dc % 4
                if k4 == 0:
                    s.wait("tensor", ev_wt)
                tt = s.emit("tensor", lambda h, dc=dc, k4=k4, cp=cp: h.transpose(
                    out=p_trC[:, k4 * 128:(k4 + 1) * 128],
                    in_=wpc[:, cp * 4096 + dc * 128: cp * 4096 + (dc + 1) * 128],
                    identity=ident_bf[:]), inc=(1 if k4 == 3 else 0))
                if k4 == 3:
                    s.wait("scalar", tt)
                    dc0 = dc - 3
                    dst = winT[:].rearrange("p (d c) -> p d c", c=DM)[
                        :, dc0:dc0 + 4, cc * 128:(cc + 1) * 128]
                    ev_wt = s.emit("scalar", lambda h, d=dst: h.activation(
                        out=d, in_=p_trC[:].rearrange("p (d c) -> p d c", c=128),
                        func=AF.Copy))
            rdC[cp] = s.token("tensor")
        winT_done = ev_wt

        s.wait("sync", cvtC[0])
        s.wait("sync", rdC[0]); s.wait("sync", rdC[1])
        wxld = dma("sync", wstgC[:, 0:16 * 96].rearrange("p (c e) -> p c e", e=96),
                   W_xproj_h.rearrange("(c p) e -> p c e", p=128), "wstg0")
        s.wait("vector", wxld)
        wxcv = s.emit("vector", lambda h: h.tensor_copy(out=wxp_bf[:], in_=wstgC[:, 0:16 * 96]))

        s.wait("tensor", wxcv)
        s.wait("tensor", winT_done)
        ev_we = None
        for cc in range(NC):
            s.wait("tensor", ev_we)
            for dc in range(16):
                mt = s.emit("tensor", lambda h, cc=cc, dc=dc: h.matmul(
                    out=p_we[:], lhsT=winT[:, dc * DM + cc * 128: dc * DM + (cc + 1) * 128],
                    rhs=wxp_bf[:, dc * 96:(dc + 1) * 96],
                    start=(dc == 0), stop=(dc == 15)), inc=(1 if dc == 15 else 0))
            s.wait("scalar", mt)
            s.wait("scalar", ev_we)
            ev_we = s.emit("scalar", lambda h, cc=cc: h.activation(
                out=w_eff[:, cc * 96:(cc + 1) * 96], in_=p_we[:], func=AF.Copy))
        weff_done = ev_we

        s.wait("sync", wxcv)
        wdld = dma("sync", wstgC[0:64, 0:2048], W_dt_h, "wstg1")
        s.wait("vector", wdld)
        wdcv = s.emit("vector", lambda h: h.tensor_copy(out=w_dt[:], in_=wstgC[0:64, 0:2048]))

        smallr_guard = None
        for (name, hsrc, dstt) in (("bdt", b_dt_h, bdt_all), ("dvec", Dvec_h, D_all)):
            s.wait("sync", smallr_guard)
            ldr = dma("sync", smallr[:], hsrc.rearrange("(a p) -> a p", p=128), "b_" + name)
            s.wait("tensor", ldr)
            s.wait("tensor", s.token("scalar"))
            tt = s.emit("tensor", lambda h: h.transpose(
                out=p_we[:, 0:16], in_=smallr[:], identity=ident_f[0:16, 0:16]))
            s.wait("scalar", tt)
            s.emit("scalar", lambda h, d=dstt: h.activation(out=d[:], in_=p_we[:, 0:16],
                                                            func=AF.Copy))
            smallr_guard = s.token("tensor")

        s.wait("sync", wdcv)
        ald = dma("sync", wstgC[:, 0:256].rearrange("p (c s) -> p c s", s=DS),
                  A_log_h.rearrange("(c p) s -> p c s", p=128), "wstg0")
        s.wait("scalar", ald)
        ae = s.emit("scalar", lambda h: h.activation(out=A_all[:], in_=wstgC[:, 0:256],
                                                     func=AF.Exp))
        s.wait("vector", ae)
        a_done = s.emit("vector", lambda h: h.tensor_scalar(
            out=A_all[:], in0=A_all[:], scalar1=-1.0, scalar2=None, op0=ALU.mult))

        hz = s.emit("vector", lambda h: h.memset(h_state[:], 0.0))

        s.barrier()
        es_c1.close()

        es_c2 = ExitStack()
        SC2 = lambda nm, shp, dt=F32: es_c2.enter_context(nc.sbuf_tensor(nm, shp, dt))
        PC2 = lambda nm, shp, dt=F32: es_c2.enter_context(nc.psum_tensor(nm, shp, dt))
        x2r = SC2("x2r", [128, 2 * DM], F32)
        x2Tc = SC2("x2Tc", [128, NC * TC], BF16)
        dt_ch = SC2("dt_ch", [128, 2 * NDH * TC], F32)
        xp_ch = SC2("xp_ch", [128, NDH * TC], F32)
        dxp_ch = SC2("dxp_ch", [128, NDH * TC], F32)
        y_ch = SC2("y_ch", [128, NDH * TC], F32)
        a_sl = SC2("a_sl", [128, NDH * TC], F32)
        zs_ch = SC2("zs_ch", [128, 2 * NDH * TC], F32)
        b_sl0 = SC2("b_sl0", [128, NDH * TC], F32)
        b_sl1 = SC2("b_sl1", [128, NDH * TC], F32)
        ao_ch = SC2("ao_ch", [128, (T // 128 // 2) * DM], F32)
        ybf = SC2("ybf", [128, NDH * TC], BF16)
        bcb = SC2("bcb", [128, 4 * TC], F32)
        wipz = SC2("wipz", [128, NC * NDH * 128], BF16)
        wob4 = SC2("wob4", [128, NDH * DM], BF16)
        brow_c = SC2("brow_c", [32, TC], F32)
        x3f = SC2("x3f", [128, DM], F32)
        p_big = PC2("p_bigC", [128, 2048], F32)
        p_xd = p_big[0:96, 1024:2048]
        p_ao2 = PC2("p_aoC", [128, 2 * DM], F32)

        x2_scr_t = x2_scr.rearrange("(c p) d -> c p d", p=128)
        x3_scr_t = x3_scr.rearrange("(c p) d -> c p d", p=128)

        wout_wr = [None] * 16

        def wout_pieces():
            prev_cv = [None, None]
            prev_wr = [None, None]
            for dc in range(16):
                r = dc % 2
                buf = wo_stg[:, r * DM:(r + 1) * DM]
                s.wait("sync", prev_cv[r])
                ldt = dma("sync", buf, W_out_t[dc], "wold%d" % r)
                s.wait("vector", ldt)
                s.wait("vector", prev_wr[r])
                cv = s.emit("vector", lambda h, b=buf, r=r: h.tensor_copy(
                    out=wo_bf[:, r * DM:(r + 1) * DM], in_=b))
                prev_cv[r] = cv
                s.wait("sync", cv)
                wout_wr[dc] = dma("sync", wout_scr_t[dc], wo_bf[:, r * DM:(r + 1) * DM],
                                  "wow%d" % r)
                prev_wr[r] = wout_wr[dc]
                yield

        wo_gen = wout_pieces()

        WPW = NDH * 128
        quarters = [(k, dh) for k in range(NK) for dh in range(NQ)]
        NQT = len(quarters)
        G = dict(x2r_guard=[None, None], x2T_guard=None, dlt_guard=None,
                 bc_rd=[None, None, None, None], bcw_tok=None, pbig_guard=None,
                 xd_guard=None, wip_guard=None, wo_guard=None,
                 ao_guard=None, ao_par=[None, None], x3guards=[], prev_ln3=None,
                 dtp_rd_s=[None, None], dtp_rd_v=[None, None],
                 zs_rd=[None, None], y_guard=None, dxp_guard=None, xp_rd=None,
                 pbig_free=None, ab0=None, ab1=[None, None], hs_guard=None,
                 dltev={}, xdbl={})
        PQ = {}

        def dt2(qp):
            return dt_ch[:, qp * NDH * TC:(qp + 1) * NDH * TC]

        def zs2(qp):
            return zs_ch[:, qp * NDH * TC:(qp + 1) * NDH * TC]

        def chunk_stage(k, pbf):
            G['x2T_guard'] = s.token("tensor")
            s.wait("tensor", pbf)
            ev_t2 = None
            for sub in range(NSUB):
                r = sub % 2
                s.wait("sync", G['x2r_guard'][r])
                s.wait("sync", s.token("x2w"))
                ldx2 = dma("sync", x2r[:, r * DM:(r + 1) * DM], x2_scr_t[k * NSUB + sub],
                           "x2r%d" % r)
                s.wait("tensor", ldx2)
                for cc in range(NC):
                    k4 = cc % 4
                    if k4 == 0:
                        s.wait("tensor", ev_t2 or G['pbig_guard'])
                    tt = s.emit("tensor", lambda h, r=r, cc=cc, k4=k4: h.transpose(
                        out=p_big[:, k4 * 128:(k4 + 1) * 128],
                        in_=x2r[:, r * DM + cc * 128: r * DM + (cc + 1) * 128],
                        identity=ident_f[:]), inc=(1 if k4 == 3 else 0))
                    if k4 == 3:
                        s.wait("scalar", tt)
                        s.wait("scalar", G['x2T_guard'])
                        cc0 = cc - 3
                        dst = x2Tc[:].rearrange("p (c t) -> p c t", t=TC)[
                            :, cc0:cc0 + 4, sub * 128:(sub + 1) * 128]
                        ev_t2 = s.emit("scalar", lambda h, d=dst: h.activation(
                            out=d, in_=p_big[:, 0:512].rearrange("p (c t) -> p c t", t=128),
                            func=AF.Copy))
                G['x2r_guard'][r] = s.token("tensor")
            G['pbig_guard'] = ev_t2
            s.wait("tensor", ev_t2)
            s.wait("tensor", weff_done)
            s.wait("tensor", G['xd_guard'])
            for hv in range(TC // 512):
                for cc in range(NC):
                    mt = s.emit("tensor", lambda h, cc=cc, hv=hv: h.matmul(
                        out=p_xd[:, hv * 512:(hv + 1) * 512],
                        lhsT=w_eff[:, cc * 96:(cc + 1) * 96],
                        rhs=x2Tc[:, cc * TC + hv * 512: cc * TC + hv * 512 + 512],
                        start=(cc == 0), stop=(cc == NC - 1)),
                        inc=(1 if cc == NC - 1 and hv == TC // 512 - 1 else 0))
            s.wait("scalar", mt)
            s.wait("scalar", G['dlt_guard'])
            dltev = s.emit("scalar", lambda h, k=k: h.activation(
                out=dltT[:, k * TC:(k + 1) * TC], in_=p_xd[0:64, :], func=AF.Copy))
            s.wait("scalar", G['bcw_tok'])
            bcev = s.emit("scalar", lambda h: h.activation(
                out=brow_c[:], in_=p_xd[64:96, :], func=AF.Copy))
            G['dlt_guard'] = bcev
            G['xd_guard'] = bcev
            G['dltev'][k] = dltev
            G['xdbl'][k] = bcev
            s.wait("sync", bcev)
            G['bcw_tok'] = dma("sync", bc_scr[:, k * TC:(k + 1) * TC], brow_c[:], "bcw")

        def produce_prescan(qi):
            k, dh = quarters[qi]
            PQ[qi] = {}
            s.wait("sync", G['wip_guard'])
            s.wait("sync", s.token("winw0"))
            s.wait("sync", s.token("winw1"))
            wipld = [dma("sync", wipz[:, cc * WPW:(cc + 1) * WPW],
                         win_scr_t[cc][:, dh * WPW:(dh + 1) * WPW], "wip%d" % cc)
                     for cc in range(NC)]
            PQ[qi]['wipld'] = wipld
            s.wait("tensor", wdcv)
            s.wait("tensor", G['dltev'][k])
            s.wait("tensor", G['xdbl'][k])
            s.wait("tensor", G['pbig_guard'])
            s.wait("tensor", G['pbig_free'])
            for j4 in range(NDH):
                dc = dh * NDH + j4
                for hv in range(TC // 512):
                    mt = s.emit("tensor", lambda h, dc=dc, j4=j4, k=k, hv=hv: h.matmul(
                        out=p_big[:, j4 * TC + hv * 512: j4 * TC + hv * 512 + 512],
                        lhsT=w_dt[:, dc * 128:(dc + 1) * 128],
                        rhs=dltT[:, k * TC + hv * 512: k * TC + hv * 512 + 512],
                        start=True, stop=True),
                        inc=(1 if j4 == NDH - 1 and hv == TC // 512 - 1 else 0))
            PQ[qi]['mt_delta'] = mt

        def emit_exp(qi):
            k, dh = quarters[qi]
            qp = qi % 2
            s.wait("scalar", PQ[qi]['mt_delta'])
            s.wait("scalar", G['dtp_rd_s'][qp])
            tok = None
            for j4 in range(NDH):
                dc = dh * NDH + j4
                tok = s.emit("scalar", lambda h, j4=j4, dc=dc, d=dt2(qp): h.activation(
                    out=d[:, j4 * TC:(j4 + 1) * TC], in_=p_big[:, j4 * TC:(j4 + 1) * TC],
                    func=AF.Exp, bias=bdt_all[:, dc:dc + 1]))
            PQ[qi]['exp'] = tok

        def emit_clip(qi):
            qp = qi % 2
            s.wait("vector", PQ[qi]['exp'])
            s.wait("vector", G['dtp_rd_v'][qp])
            PQ[qi]['clip'] = s.emit("vector", lambda h, d=dt2(qp): h.tensor_scalar(
                out=d, in0=d, scalar1=0.10517091807564763, scalar2=0.0010005001667083846,
                op0=ALU.min, op1=ALU.max))

        def emit_ln(qi):
            qp = qi % 2
            s.wait("scalar", PQ[qi]['clip'])
            one_c = constv(1.0)
            PQ[qi]['dt_done'] = s.emit("scalar", lambda h, d=dt2(qp), one_c=one_c: h.activation(
                out=d, in_=d, func=AF.Ln, bias=one_c))

        def emit_xpmms(qi):
            s.wait("tensor", PQ[qi]['exp'])
            for cc in range(NC):
                s.wait("tensor", PQ[qi]['wipld'][cc])
            for j4 in range(NDH):
                for hv in range(TC // 512):
                    for cc in range(NC):
                        mt = s.emit("tensor", lambda h, j4=j4, cc=cc, hv=hv: h.matmul(
                            out=p_big[:, j4 * TC + hv * 512: j4 * TC + hv * 512 + 512],
                            lhsT=wipz[:, cc * WPW + j4 * 128: cc * WPW + (j4 + 1) * 128],
                            rhs=x2Tc[:, cc * TC + hv * 512: cc * TC + hv * 512 + 512],
                            start=(cc == 0), stop=(cc == NC - 1)),
                            inc=(1 if cc == NC - 1 and j4 == NDH - 1 and hv == TC // 512 - 1 else 0))
            PQ[qi]['mt_xp'] = mt
            PQ[qi]['wip_xp'] = s.token("tensor")

        def emit_xpev(qi):
            s.wait("scalar", PQ[qi]['mt_xp'])
            s.wait("scalar", G['xp_rd'])
            PQ[qi]['xp_done'] = s.emit("scalar", lambda h: h.activation(
                out=xp_ch[:], in_=p_big[:], func=AF.Copy))

        def emit_zmms(qi):
            k, dh = quarters[qi]
            s.wait("sync", PQ[qi]['wip_xp'])
            wipldz = [dma("sync", wipz[:, cc * WPW:(cc + 1) * WPW],
                          win_scr_t[cc][:, 2048 + dh * WPW: 2048 + (dh + 1) * WPW],
                          "wip%d" % cc) for cc in range(NC)]
            s.wait("tensor", PQ[qi]['xp_done'])
            for cc in range(NC):
                s.wait("tensor", wipldz[cc])
            for j4 in range(NDH):
                for hv in range(TC // 512):
                    for cc in range(NC):
                        mt = s.emit("tensor", lambda h, j4=j4, cc=cc, hv=hv: h.matmul(
                            out=p_big[:, j4 * TC + hv * 512: j4 * TC + hv * 512 + 512],
                            lhsT=wipz[:, cc * WPW + j4 * 128: cc * WPW + (j4 + 1) * 128],
                            rhs=x2Tc[:, cc * TC + hv * 512: cc * TC + hv * 512 + 512],
                            start=(cc == 0), stop=(cc == NC - 1)),
                            inc=(1 if cc == NC - 1 and j4 == NDH - 1 and hv == TC // 512 - 1 else 0))
            PQ[qi]['mt_z'] = mt
            G['wip_guard'] = s.token("tensor")

        def emit_zev(qi):
            qp = qi % 2
            s.wait("scalar", PQ[qi]['mt_z'])
            s.wait("scalar", G['zs_rd'][qp])
            PQ[qi]['zev'] = s.emit("scalar", lambda h, d=zs2(qp): h.activation(
                out=d, in_=p_big[:], func=AF.Silu))
            G['pbig_free'] = PQ[qi]['zev']

        HOOKS = {0: emit_exp, 1: emit_clip, 2: emit_ln, 3: emit_xpmms,
                 5: emit_xpev, 6: emit_zmms, 8: emit_zev}

        def consume(qi, nxt):
            k, dh = quarters[qi]
            qp = qi % 2
            if k == 0:
                for _ in range(2):
                    next(wo_gen, None)
            s.wait("vector", PQ[qi]['xp_done'])
            s.wait("vector", PQ[qi]['dt_done'])
            s.wait("vector", G['y_guard'])
            yv = None
            for j in range(NDH):
                dc = dh * NDH + j
                yv = s.emit("vector", lambda h, j=j, dc=dc: h.tensor_scalar(
                    out=y_ch[:, j * TC:(j + 1) * TC], in0=xp_ch[:, j * TC:(j + 1) * TC],
                    scalar1=D_all[:, dc:dc + 1], scalar2=None, op0=ALU.mult))
            s.wait("vector", G['dxp_guard'])
            dxpv = s.emit("vector", lambda h, d=dt2(qp): h.tensor_tensor(
                out=dxp_ch[:], in0=d, in1=xp_ch[:], op=ALU.mult))
            G['xp_rd'] = dxpv
            G['dtp_rd_v'][qp] = dxpv

            s.wait("vector", a_done)
            s.wait("vector", hz)
            acc_prev = yv
            bcast_tok = [None, None, None, None]

            def issue_bcast(s_):
                slot = s_ % 2
                s.wait("sync", G['bc_rd'][slot])
                s.wait("sync", G['bc_rd'][2 + slot])
                s.wait("sync", G['bcw_tok'])
                bcast_tok[slot] = dma(
                    "sync", bcb[:, slot * TC:(slot + 1) * TC],
                    bc_scr[s_:s_ + 1, k * TC:(k + 1) * TC].partition_broadcast(128),
                    "bcr%d" % slot)
                bcast_tok[2 + slot] = dma(
                    "sync", bcb[:, (2 + slot) * TC:(3 + slot) * TC],
                    bc_scr[16 + s_:17 + s_, k * TC:(k + 1) * TC].partition_broadcast(128),
                    "bcc%d" % slot)

            issue_bcast(0)
            amk = [None]
            for s_ in range(16):
                slot = s_ % 2
                bld = bcast_tok[slot]
                cld = bcast_tok[2 + slot]
                if s_ < 15:
                    issue_bcast(s_ + 1)
                s.wait("scalar", G['ab0'])
                s.wait("scalar", PQ[qi]['dt_done'])
                s.wait("scalar", a_done)
                amk = []
                for j in range(NDH):
                    dc = dh * NDH + j
                    amk.append(s.emit("scalar", lambda h, j=j, dc=dc, s_=s_, d=dt2(qp): h.activation(
                        out=a_sl[:, j * TC:(j + 1) * TC], in_=d[:, j * TC:(j + 1) * TC],
                        func=AF.Copy, scale=A_all[:, dc * DS + s_: dc * DS + s_ + 1],
                        bias=1.0)))
                bs = b_sl0 if (s_ % 2 == 0) else b_sl1
                s.wait("vector", bld)
                s.wait("vector", dxpv)
                s.wait("vector", G['ab1'][s_ % 2])
                bmk = s.emit("vector", lambda h, slot=slot, bs=bs: h.tensor_tensor(
                    out=bs[:].rearrange("p (j t) -> p j t", t=TC),
                    in0=dxp_ch[:].rearrange("p (j t) -> p j t", t=TC),
                    in1=bcb[:, slot * TC:(slot + 1) * TC].unsqueeze(1)
                        .broadcast_to([128, NDH, TC]), op=ALU.mult))
                G['bc_rd'][slot] = bmk
                s.wait("vector", G['hs_guard'])
                s.wait("vector", amk[-1])
                s.wait("vector", bmk)
                scn = []
                for j in range(NDH):
                    dc = dh * NDH + j
                    scn.append(s.emit("vector", lambda h, j=j, dc=dc, s_=s_, bs=bs: h.tensor_tensor_scan(
                        out=bs[:, j * TC:(j + 1) * TC], data0=a_sl[:, j * TC:(j + 1) * TC],
                        data1=bs[:, j * TC:(j + 1) * TC],
                        initial=h_state[:, dc * DS + s_: dc * DS + s_ + 1],
                        op0=ALU.mult, op1=ALU.add)))
                G['ab0'] = scn[-1]
                s.wait("vector", scn[-1])
                G['hs_guard'] = s.emit("vector", lambda h, dh=dh, s_=s_, bs=bs: h.tensor_copy(
                    out=h_state[:].rearrange("p (d s) -> p d s", s=DS)[
                        :, dh * NDH:(dh + 1) * NDH, s_:s_ + 1],
                    in_=bs[:].rearrange("p (j t) -> p j t", t=TC)[:, :, TC - 1:TC]))
                s.wait("vector", cld)
                s.wait("vector", G['hs_guard'])
                cm = s.emit("vector", lambda h, slot=slot, bs=bs: h.tensor_tensor(
                    out=bs[:].rearrange("p (j t) -> p j t", t=TC),
                    in0=bs[:].rearrange("p (j t) -> p j t", t=TC),
                    in1=bcb[:, (2 + slot) * TC:(3 + slot) * TC].unsqueeze(1)
                        .broadcast_to([128, NDH, TC]), op=ALU.mult))
                s.wait("vector", acc_prev)
                s.wait("vector", cm)
                acc_prev = s.emit("vector", lambda h, bs=bs: h.tensor_tensor(
                    out=y_ch[:], in0=y_ch[:], in1=bs[:], op=ALU.add))
                G['bc_rd'][2 + slot] = acc_prev
                G['ab1'][s_ % 2] = acc_prev
                if nxt is not None and s_ in HOOKS:
                    HOOKS[s_](nxt)
            y_done = acc_prev
            G['dtp_rd_s'][qp] = amk[-1]

            s.wait("vector", PQ[qi]['zev'])
            s.wait("vector", y_done)
            ymul = s.emit("vector", lambda h, d=zs2(qp): h.tensor_tensor(
                out=y_ch[:], in0=y_ch[:], in1=d, op=ALU.mult))
            G['zs_rd'][qp] = ymul
            s.wait("vector", ymul)
            ycv = s.emit("vector", lambda h: h.tensor_copy(out=ybf[:], in_=y_ch[:]))
            G['y_guard'] = ycv
            G['dxp_guard'] = ycv

            s.wait("tensor", ycv)
            for j in range(NDH):
                s.wait("sync", wout_wr[dh * NDH + j])
            s.wait("sync", G['wo_guard'])
            wld4 = [dma("sync", wob4[:, j * DM:(j + 1) * DM],
                        wout_scr_t[dh * NDH + j], "wo%d" % j) for j in range(NDH)]
            for sub in range(NSUB):
                pa = p_ao2[:, (sub % 2) * DM:(sub % 2 + 1) * DM]
                s.wait("tensor", G['ao_par'][sub % 2])
                for j in range(NDH):
                    s.wait("tensor", wld4[j])
                    for db in range(2):
                        mt = s.emit("tensor", lambda h, j=j, sub=sub, db=db, pa=pa: h.matmul(
                            out=pa[:, db * 512:(db + 1) * 512],
                            lhsT=ybf[:, j * TC + sub * 128: j * TC + (sub + 1) * 128],
                            rhs=wob4[:, j * DM + db * 512: j * DM + (db + 1) * 512],
                            start=(j == 0), stop=(j == NDH - 1)),
                            inc=(1 if j == NDH - 1 and db == 1 else 0))
                if dh == 0:
                    s.wait("scalar", mt)
                    s.wait("scalar", G['ao_guard'])
                    G['ao_guard'] = s.emit("scalar", lambda h, sub=sub, pa=pa: h.activation(
                        out=ao_ch[:, sub * DM:(sub + 1) * DM], in_=pa[:], func=AF.Copy))
                    G['ao_par'][sub % 2] = G['ao_guard']
                elif dh < NQ - 1:
                    s.wait("vector", mt)
                    s.wait("vector", G['ao_guard'])
                    G['ao_guard'] = s.emit("vector", lambda h, sub=sub, pa=pa: h.tensor_tensor(
                        out=ao_ch[:, sub * DM:(sub + 1) * DM], in0=pa[:],
                        in1=ao_ch[:, sub * DM:(sub + 1) * DM], op=ALU.add))
                    G['ao_par'][sub % 2] = G['ao_guard']
                else:
                    s.wait("vector", mt)
                    s.wait("vector", G['ao_guard'])
                    aadd = s.emit("vector", lambda h, sub=sub, pa=pa: h.tensor_tensor(
                        out=ao_ch[:, sub * DM:(sub + 1) * DM], in0=pa[:],
                        in1=ao_ch[:, sub * DM:(sub + 1) * DM], op=ALU.add))
                    G['ao_par'][sub % 2] = aadd
                    r = sub % 2
                    s.wait("sync", G['x2r_guard'][r])
                    ldr2 = dma("sync", x2r[:, r * DM:(r + 1) * DM],
                               x2_scr_t[k * NSUB + sub], "x2b%d" % r)
                    s.wait("vector", ldr2)
                    s.wait("vector", aadd)
                    radd = s.emit("vector", lambda h, sub=sub, r=r: h.tensor_tensor(
                        out=ao_ch[:, sub * DM:(sub + 1) * DM],
                        in0=ao_ch[:, sub * DM:(sub + 1) * DM],
                        in1=x2r[:, r * DM:(r + 1) * DM], op=ALU.add))
                    G['x2r_guard'][r] = radd
                    s.wait("scalar", radd)
                    s.wait("vector", radd)
                    for g_ in G['x3guards']:
                        s.wait("scalar", g_); s.wait("vector", g_)
                    s.wait("vector", G['prev_ln3'])
                    vt3, st3 = layernorm(ao_ch[:, sub * DM:(sub + 1) * DM], x3f[:], None)
                    G['prev_ln3'] = st3
                    G['ao_guard'] = st3
                    s.wait("sync", st3)
                    w3 = dma("sync", x3_scr_t[k * NSUB + sub], x3f[:], "x3w")
                    G['x3guards'] = [w3]
                    if stop_after == "x3":
                        G['x3guards'].append(dma("sync", out_t[k * NSUB + sub], x3f[:], "outw"))
            G['wo_guard'] = s.token("tensor")

        chunk_stage(0, None)
        produce_prescan(0)
        for fn in (emit_exp, emit_clip, emit_ln, emit_xpmms, emit_xpev,
                   emit_zmms, emit_zev):
            fn(0)
        for qi in range(NQT):
            if qi + 1 < NQT:
                if quarters[qi + 1][1] == 0:
                    chunk_stage(quarters[qi + 1][0], G['pbig_free'])
                produce_prescan(qi + 1)
                consume(qi, qi + 1)
            else:
                consume(qi, None)

        s.barrier()
        es_c2.close()
        es_c.close()

    if stop_after == "full":
        es_d = ExitStack()
        SBd = lambda nm, shp, dt=F32: es_d.enter_context(nc.sbuf_tensor(nm, shp, dt))
        PSd = lambda nm, shp, dt=F32: es_d.enter_context(nc.psum_tensor(nm, shp, dt))
        LADDER_KS = [64, 32, 16, 8, 4, 2, 1]
        for k_ in LADDER_KS:
            constv(-SIGBIG * (k_ - 0.5))
        fx_all = SBd("fx_all", [128, NT], F32)
        gmax_all = SBd("gmax_all", [128, NT], F32)
        fg_all = SBd("fg_all", [128, NT], F32)
        fu_all = SBd("fu_all", [128, NT], F32)
        f2_all = SBd("f2_all", [128, NT], F32)
        wsc = SBd("wsc", [128, 9], F32)
        wsc2 = SBd("wsc2", [128, 4], F32)
        totsb = SBd("totsb", [1, 4], F32)
        ln_sq_holder["cur"] = SBd("ln_sqD", [128, DM], F32)
        wdT = SBd("wdT", [128, 32 * DM], BF16)
        wstgD2 = SBd("wstgD2", [128, 2 * 2048], F32)
        wqb2 = SBd("wqb2", [128, 2048], BF16)
        ternB = SBd("ternB", [128, 2048], F32)
        es_dx = ExitStack()
        xqT = es_dx.enter_context(nc.sbuf_tensor("xqT", [128, NC * T], BF16))
        gu_scr_t = gu_scr.rearrange("(n p) w -> n p w", p=128)

        es_d0 = ExitStack()
        SD0 = lambda nm, shp, dt=F32: es_d0.enter_context(nc.sbuf_tensor(nm, shp, dt))
        PD0 = lambda nm, shp, dt=F32: es_d0.enter_context(nc.psum_tensor(nm, shp, dt))
        wrow = SD0("wrowD", [128, 2 * 2048], F32)
        wpart = SD0("wpartD", [128, 2], F32)
        p_d0 = PD0("p_d0", [128, 8], F32)

        onec = constv(1.0)
        w_gate_t0 = w_gate_h.rearrange("(a p) i -> a p i", p=128)
        w_up_t0 = w_up_h.rearrange("(a p) i -> a p i", p=128)
        w_down_t0 = w_down_h.rearrange("(a p) i -> a p i", p=128)
        w_srcs = [
            ("g", lambda pc: w_gate_t0[pc], 32, DM),
            ("u", lambda pc: w_up_t0[pc], 32, DM),
            ("d", lambda pc: w_down_t0[pc // 2][:, (pc % 2) * 2048:(pc % 2 + 1) * 2048],
             16, 2048)]
        red_tok_box = [None]
        ld_guard = [None, None]

        def d0_pieces():
            red_tok = None
            for wi, (tag, wsrc, npieces, wid) in enumerate(w_srcs):
                acc_t = red_tok
                for pc in range(npieces):
                    r = pc % 2
                    buf = wrow[:, r * 2048: r * 2048 + wid]
                    s.wait("sync", ld_guard[r])
                    ldw = dma("sync", buf, wsrc(pc), "w0D%d" % r)
                    s.wait("vector", ldw)
                    s.wait("vector", acc_t)
                    rt = s.emit("vector", lambda h, b=buf: h.tensor_reduce(
                        out=wpart[:, 0:1], in_=b, axis=AXX, op=ALU.add,
                        apply_absolute_value=True))
                    ld_guard[r] = rt
                    s.wait("vector", rt)
                    if pc == 0:
                        s.wait("vector", red_tok)
                        acc_t = s.emit("vector", lambda h: h.tensor_copy(
                            out=wpart[:, 1:2], in_=wpart[:, 0:1]))
                    else:
                        acc_t = s.emit("vector", lambda h: h.tensor_tensor(
                            out=wpart[:, 1:2], in0=wpart[:, 1:2], in1=wpart[:, 0:1],
                            op=ALU.add))
                    yield
                s.wait("tensor", acc_t)
                s.wait("tensor", red_tok)
                pt = s.emit("tensor", lambda h, wi=wi: h.matmul(
                    out=p_d0[0:1, wi:wi + 1], lhsT=wpart[:, 1:2], rhs=onec,
                    start=True, stop=True))
                s.wait("scalar", pt)
                red_tok = s.emit("scalar", lambda h, wi=wi: h.activation(
                    out=totsb[0:1, wi:wi + 1], in_=p_d0[0:1, wi:wi + 1], func=AF.Copy))
                red_tok_box[0] = red_tok

        d0_gen = d0_pieces()
        def ternary(dst_ap, src_ap, wi, dep_toks, scr=None):
            tau = wsc[:, wi * 3 + 1: wi * 3 + 2]
            ntau = wsc[:, wi * 3 + 2: wi * 3 + 3]
            scr = scr if scr is not None else tern_scr
            for t_ in dep_toks:
                s.wait("vector", t_)
            r1 = s.emit("vector", lambda h, scr=scr: h.tensor_scalar(
                out=scr[:, 0:src_ap.shape[-1]], in0=src_ap, scalar1=ntau,
                scalar2=-1.0, op0=ALU.is_ge, op1=ALU.add))
            s.wait("vector", r1)
            return s.emit("vector", lambda h, scr=scr: h.scalar_tensor_tensor(
                out=dst_ap, in0=src_ap, scalar=tau,
                in1=scr[:, 0:src_ap.shape[-1]], op0=ALU.is_gt, op1=ALU.add))

        def ladder(u_abs_ap, d_ap, stp_ap, width, dep):
            s.wait("vector", dep)
            t = dep
            for li, k_ in enumerate(LADDER_KS):
                srcd = u_abs_ap if li == 0 else d_ap
                s.wait("vector", t)
                st = s.emit("vector", lambda h, srcd=srcd, k_=k_: h.tensor_scalar(
                    out=stp_ap, in0=srcd, scalar1=float(k_) - 0.5, scalar2=float(k_),
                    op0=ALU.is_ge, op1=ALU.mult))
                s.wait("vector", st)
                t = s.emit("vector", lambda h, srcd=srcd, k_=k_: h.scalar_tensor_tensor(
                    out=d_ap, in0=stp_ap, scalar=-1.0, in1=srcd,
                    op0=ALU.mult, op1=ALU.add))
            s.wait("vector", t)
            return s.emit("vector", lambda h: h.tensor_tensor(
                out=u_abs_ap, in0=u_abs_ap, in1=d_ap, op=ALU.subtract))

        es_d2 = ExitStack()
        SD2 = lambda nm, shp, dt=F32: es_d2.enter_context(nc.sbuf_tensor(nm, shp, dt))
        PD2 = lambda nm, shp, dt=F32: es_d2.enter_context(nc.psum_tensor(nm, shp, dt))
        x3c = SD2("x3cD", [128, 4 * DM], F32)
        qa = SD2("qaD", [128, DM], F32)
        qd = SD2("qdD", [128, DM], F32)
        qs = SD2("qsD", [128, DM], F32)
        qbf = SD2("qbfD", [128, DM], BF16)
        qcol = SD2("qcolD", [128, 4], F32)
        p_trD = PD2("p_trD", [128, 512], BF16)

        x3ld_g = [None, None, None, None]
        lad_prev = None
        tr_prev = None
        for tch in range(NT):
            r = tch % 4
            buf = x3c[:, r * DM:(r + 1) * DM]
            s.wait("sync", x3ld_g[r])
            s.wait("sync", s.token("x3w"))
            ldx3 = dma("sync", buf, x3_scr_t[tch], "x3r%d" % r)
            s.wait("vector", ldx3)
            s.wait("vector", lad_prev)
            mx = s.emit("vector", lambda h, b=buf: h.reduce_max(
                out=qcol[:, 0:1], in_=b, axis=AXX, apply_absolute_value=True))
            s.wait("vector", mx)
            fxv = s.emit("vector", lambda h, tch=tch: h.tensor_scalar(
                out=fx_all[:, tch:tch + 1], in0=qcol[:, 0:1],
                scalar1=1.0e-5, scalar2=1.0 / 127.0, op0=ALU.max, op1=ALU.mult))
            s.wait("vector", fxv)
            sxv = s.emit("vector", lambda h, tch=tch: h.reciprocal(
                out=qcol[:, 1:2], in_=fx_all[:, tch:tch + 1]))
            s.wait("scalar", sxv)
            ab = s.emit("scalar", lambda h, b=buf: h.activation(
                out=qa[:], in_=b, func=AF.Abs, scale=qcol[:, 1:2]))
            sg = s.emit("scalar", lambda h, b=buf: h.activation(
                out=qs[:], in_=b, func=AF.Sign))
            lt = ladder(qa[:], qd[:], ln_sq_holder["cur"][:, 0:DM], DM, ab)
            s.wait("vector", lt)
            s.wait("vector", sg)
            s.wait("vector", tr_prev)
            qv = s.emit("vector", lambda h: h.tensor_tensor(
                out=qbf[:], in0=qa[:], in1=qs[:], op=ALU.mult))
            lad_prev = qv
            x3ld_g[r] = qv
            s.wait("tensor", qv)
            ev_q = None
            for g in range(2):
                s.wait("tensor", ev_q or tr_prev)
                for k4 in range(4):
                    cc = g * 4 + k4
                    tt = s.emit("tensor", lambda h, cc=cc, k4=k4: h.transpose(
                        out=p_trD[:, k4 * 128:(k4 + 1) * 128],
                        in_=qbf[:, cc * 128:(cc + 1) * 128],
                        identity=ident_bf[:]), inc=(1 if k4 == 3 else 0))
                s.wait("scalar", tt)
                s.wait("scalar", tr_prev)
                dst = xqT[:].rearrange("p (c t) -> p c t", t=T)[:, g * 4:(g + 1) * 4,
                                                               tch * 128:(tch + 1) * 128]
                ev_q = s.emit("scalar", lambda h, d=dst: h.activation(
                    out=d, in_=p_trD[:].rearrange("p (c t) -> p c t", t=128), func=AF.Copy))
            tr_prev = s.token("tensor")
            for _ in range(5):
                if next(d0_gen, "end") == "end":
                    break
        xqT_done = s.token("scalar")
        s.barrier()
        es_d2.close()

        for _ in d0_gen:
            pass
        red_tok = red_tok_box[0]
        s.wait("tensor", red_tok)
        bt = s.emit("tensor", lambda h: h.matmul(
            out=p_d0[:, 0:3], lhsT=ones_f[0:1, :], rhs=totsb[0:1, 0:3],
            start=True, stop=True))
        s.wait("vector", bt)
        totb = s.emit("vector", lambda h: h.tensor_copy(out=wsc2[:, 0:3], in_=p_d0[:, 0:3]))
        mc_toks = []
        for wi, nelem in ((0, INNER * DM), (1, INNER * DM), (2, DM * INNER)):
            s.wait("vector", totb if wi == 0 else mc_toks[-1])
            t_a = s.emit("vector", lambda h, wi=wi, nelem=nelem: h.tensor_scalar(
                out=wsc[:, wi * 3: wi * 3 + 1], in0=wsc2[:, wi:wi + 1],
                scalar1=1.0 / nelem, scalar2=1.0e-5, op0=ALU.mult, op1=ALU.max))
            s.wait("vector", t_a)
            t_b = s.emit("vector", lambda h, wi=wi: h.tensor_scalar(
                out=wsc[:, wi * 3 + 1: wi * 3 + 2], in0=wsc[:, wi * 3: wi * 3 + 1],
                scalar1=0.5, scalar2=None, op0=ALU.mult))
            s.wait("vector", t_b)
            mc_toks.append(s.emit("vector", lambda h, wi=wi: h.tensor_scalar(
                out=wsc[:, wi * 3 + 2: wi * 3 + 3], in0=wsc[:, wi * 3 + 1: wi * 3 + 2],
                scalar1=-1.0, scalar2=None, op0=ALU.mult)))
        wsc_done = mc_toks[-1]
        w_down_t = w_down_h.rearrange("(a p) i -> a p i", p=128)

        def wd_pieces():
            ldg = [None, None]
            wq2_guard = None
            ev_wd = None
            for pc in range(8):
                for hv in range(2):
                    r2 = (pc * 2 + hv) % 2
                    buf = wstgD2[:, r2 * 2048:(r2 + 1) * 2048]
                    s.wait("sync", ldg[r2])
                    ldw = dma("sync", buf, w_down_t[pc][:, hv * 2048:(hv + 1) * 2048],
                              "wdd%d" % r2)
                    s.wait("vector", ldw)
                    tq = ternary(wqb2[:], buf, 2, [wq2_guard], scr=ternB)
                    ldg[r2] = tq
                    s.wait("tensor", tq)
                    for g in range(4):
                        s.wait("tensor", ev_wd)
                        for k4 in range(4):
                            jj = g * 4 + k4
                            tt = s.emit("tensor", lambda h, jj=jj, k4=k4: h.transpose(
                                out=p_trB2w[:, k4 * 128:(k4 + 1) * 128],
                                in_=wqb2[:, jj * 128:(jj + 1) * 128],
                                identity=ident_bf[:]), inc=(1 if k4 == 3 else 0))
                        s.wait("scalar", tt)
                        s.wait("scalar", ev_wd)
                        ic0 = hv * 16 + g * 4
                        dst = wdT[:].rearrange("p (c d) -> p c d", d=DM)[
                            :, ic0:ic0 + 4, pc * 128:(pc + 1) * 128]
                        ev_wd = s.emit("scalar", lambda h, d=dst: h.activation(
                            out=d, in_=p_trB2w[:].rearrange("p (c d) -> p c d", d=128),
                            func=AF.Copy))
                    wq2_guard = ev_wd
                    yield

        wd_gen = wd_pieces()

        es_da = ExitStack()
        SDa = lambda nm, shp, dt=F32: es_da.enter_context(nc.sbuf_tensor(nm, shp, dt))
        PDa = lambda nm, shp, dt=F32: es_da.enter_context(nc.psum_tensor(nm, shp, dt))
        wstgD = SDa("wstgD", [128, 4 * DM], F32)
        tern_scr = SDa("ternA", [128, DM], F32)
        wqb = SDa("wqbA", [128, DM], BF16)
        wgT2 = SDa("wgT", [128, 2 * NC * 512], BF16)
        wuT2 = SDa("wuT", [128, 2 * NC * 512], BF16)
        sgb = SDa("sgbA", [128, 2 * 512], F32)
        gub = SDa("gubA", [128, 2 * 512], BF16)
        qcolD = SDa("qcolA", [128, 4], F32)
        p_gA = PDa("p_gA", [128, 2 * 512], F32)
        p_uA = PDa("p_uA", [128, 2 * 512], F32)
        p_trA2 = PDa("p_trA2", [128, 512], BF16)
        p_trB2w = PDa("p_trB2w", [128, 512], BF16)

        w_gate_t = w_gate_h.rearrange("(a p) i -> a p i", p=128)
        w_up_t = w_up_h.rearrange("(a p) i -> a p i", p=128)

        s.wait("vector", wsc_done)
        s.wait("vector", xqT_done)
        fgt = s.emit("vector", lambda h: h.tensor_tensor(
            out=fg_all[:], in0=fx_all[:], in1=wsc[:, 0:1].broadcast_to([128, NT]),
            op=ALU.mult))
        fut = s.emit("vector", lambda h: h.tensor_tensor(
            out=fu_all[:], in0=fx_all[:], in1=wsc[:, 3:4].broadcast_to([128, NT]),
            op=ALU.mult))

        gm_init = [None] * NT
        ld_g = [None] * 4
        ld_ctr = [0]
        wq_guard = None
        gu_par = [None, None]
        st_par = [None, None]
        mmA_par = [None, None]
        for ob in range(8):
            obp = ob % 2
            wgT = wgT2[:, obp * NC * 512:(obp + 1) * NC * 512]
            wuT = wuT2[:, obp * NC * 512:(obp + 1) * NC * 512]
            mmA_guard = mmA_par[obp]
            for (wt, wdst, wi) in ((w_gate_t, wgT, 0), (w_up_t, wuT, 1)):
                ev_wq = None
                for osub in range(4):
                    pc = ob * 4 + osub
                    r = ld_ctr[0] % 4
                    ld_ctr[0] += 1
                    buf = wstgD[:, r * DM:(r + 1) * DM]
                    s.wait("sync", ld_g[r])
                    ldw = dma("sync", buf, wt[pc], "wD%d" % r)
                    s.wait("vector", ldw)
                    tq = ternary(wqb[:], buf, wi, [wq_guard])
                    ld_g[r] = tq
                    wq_guard = tq
                    s.wait("tensor", tq)
                    for g in range(2):
                        s.wait("tensor", ev_wq or mmA_guard)
                        for k4 in range(4):
                            ic = g * 4 + k4
                            tt = s.emit("tensor", lambda h, ic=ic, k4=k4: h.transpose(
                                out=p_trA2[:, k4 * 128:(k4 + 1) * 128],
                                in_=wqb[:, ic * 128:(ic + 1) * 128],
                                identity=ident_bf[:]), inc=(1 if k4 == 3 else 0))
                        s.wait("scalar", tt)
                        s.wait("scalar", ev_wq)
                        dst = wdst[:].rearrange("p (c o) -> p c o", o=512)[
                            :, g * 4:(g + 1) * 4, osub * 128:(osub + 1) * 128]
                        ev_wq = s.emit("scalar", lambda h, d=dst: h.activation(
                            out=d, in_=p_trA2[:].rearrange("p (c o) -> p c o", o=128),
                            func=AF.Copy))
                    wq_guard = ev_wq
            wT_done = s.token("scalar")
            for _ in range(2):
                next(wd_gen, None)

            s.wait("tensor", wT_done)
            s.wait("tensor", fgt); s.wait("tensor", fut)
            for tch in range(NT):
                par = tch % 2
                pg = p_gA[:, par * 512:(par + 1) * 512]
                pu = p_uA[:, par * 512:(par + 1) * 512]
                sg_ = sgb[:, par * 512:(par + 1) * 512]
                gu_ = gub[:, par * 512:(par + 1) * 512]
                s.wait("tensor", gu_par[par])
                for ic in range(NC):
                    s.emit("tensor", lambda h, ic=ic, tch=tch, pg=pg, wgT=wgT: h.matmul(
                        out=pg, lhsT=xqT[:, ic * T + tch * 128: ic * T + (tch + 1) * 128],
                        rhs=wgT[:, ic * 512:(ic + 1) * 512],
                        start=(ic == 0), stop=(ic == NC - 1)))
                for ic in range(NC):
                    mt = s.emit("tensor", lambda h, ic=ic, tch=tch, pu=pu, wuT=wuT: h.matmul(
                        out=pu, lhsT=xqT[:, ic * T + tch * 128: ic * T + (tch + 1) * 128],
                        rhs=wuT[:, ic * 512:(ic + 1) * 512],
                        start=(ic == 0), stop=(ic == NC - 1)), inc=(1 if ic == NC - 1 else 0))
                s.wait("scalar", mt)
                s.wait("scalar", gu_par[par])
                sgev = s.emit("scalar", lambda h, tch=tch, pg=pg, sg_=sg_: h.activation(
                    out=sg_, in_=pg, func=AF.Silu, scale=fg_all[:, tch:tch + 1]))
                s.wait("vector", sgev)
                s.wait("vector", st_par[par])
                guv = s.emit("vector", lambda h, tch=tch, pu=pu, sg_=sg_, gu_=gu_: h.scalar_tensor_tensor(
                    out=gu_, in0=pu, scalar=fu_all[:, tch:tch + 1], in1=sg_,
                    op0=ALU.mult, op1=ALU.mult))
                gu_par[par] = guv
                s.wait("vector", guv)
                rm = s.emit("vector", lambda h, par=par, gu_=gu_: h.reduce_max(
                    out=qcolD[:, par:par + 1], in_=gu_, axis=AXX, apply_absolute_value=True))
                s.wait("vector", rm)
                if gm_init[tch] is None:
                    gm_init[tch] = s.emit("vector", lambda h, tch=tch, par=par: h.tensor_copy(
                        out=gmax_all[:, tch:tch + 1], in_=qcolD[:, par:par + 1]))
                else:
                    gm_init[tch] = s.emit("vector", lambda h, tch=tch, par=par: h.tensor_tensor(
                        out=gmax_all[:, tch:tch + 1], in0=gmax_all[:, tch:tch + 1],
                        in1=qcolD[:, par:par + 1], op=ALU.max))
                s.wait("sync", guv)
                st_par[par] = dma("sync", gu_scr_t[tch * 8 + ob], gu_, "guw%d" % par)
            mmA_guard = s.token("tensor")
        gmax_done = s.token("vector")
        s.barrier()
        es_da.close()
        es_d0.close()
        es_dx.close()

        es_db = ExitStack()
        SDb = lambda nm, shp, dt=F32: es_db.enter_context(nc.sbuf_tensor(nm, shp, dt))
        PDb = lambda nm, shp, dt=F32: es_db.enter_context(nc.psum_tensor(nm, shp, dt))
        tern_scr2 = SDb("ternB2", [128, 2048], F32)
        guh = SDb("guhB", [128, 2048], BF16)
        qa2 = SDb("qa2B", [128, 2 * 2048], F32)
        qd2 = SDb("qd2B", [128, 2048], F32)
        qs2 = SDb("qs2B", [128, 2048], BF16)
        qbf2 = SDb("qbf2B", [128, 2 * 4096], BF16)
        xq2T = SDb("xq2T", [128, 2 * 32 * 128], BF16)
        x3c2 = SDb("x3c2", [128, 2 * DM], F32)
        outf = SDb("outfD", [128, 2 * DM], F32)
        qcolD2 = SDb("qcolD2", [128, 4], F32)
        ln_sq_holder["cur"] = SDb("ln_sqDB", [128, DM], F32)
        p_dn = PDb("p_dnB", [128, 2 * DM], F32)
        p_trB2 = PDb("p_trB2", [128, 512], BF16)

        tern_scr = tern_scr2
        for _ in wd_gen:
            pass
        wdT_done = s.token("scalar")
        tr_ps_guard = wdT_done
        s.barrier()

        s.wait("vector", gmax_done)
        f2a = s.emit("vector", lambda h: h.tensor_scalar(
            out=f2_all[:], in0=gmax_all[:], scalar1=1.0e-5, scalar2=1.0 / 127.0,
            op0=ALU.max, op1=ALU.mult))
        s.wait("vector", f2a)
        f2b = s.emit("vector", lambda h: h.tensor_tensor(
            out=f2_all[:], in0=f2_all[:], in1=wsc[:, 6:7].broadcast_to([128, NT]),
            op=ALU.mult))

        guh_guard = None
        qs_guard = None
        lad2_prev = [None, None]
        tr2_prev = [None, None]
        x3ld2_g = [None, None]
        out_guards = {0: [], 1: []}
        prev_ln4 = None
        dn_guard = [None, None]
        pend_tail = None
        for tch in range(NT):
            par = tch % 2
            guh_ = guh[:]
            qa2_ = qa2[:, par * 2048:(par + 1) * 2048]
            qd2_ = qd2[:]
            qs2_ = qs2[:]
            qbf2_ = qbf2[:, par * 4096:(par + 1) * 4096]
            xq2T_ = xq2T[:, par * 4096:(par + 1) * 4096]
            x3c2_ = x3c2[:, par * DM:(par + 1) * DM]
            outf_ = outf[:, par * DM:(par + 1) * DM]
            pdn_ = p_dn[:, par * DM:(par + 1) * DM]
            stp_ = tern_scr2[:]
            s.wait("vector", lad2_prev[par])
            s.wait("vector", lad2_prev[1 - par])
            s.wait("vector", gmax_done)
            s2v = s.emit("vector", lambda h, tch=tch, par=par: h.tensor_scalar(
                out=qcolD2[:, 2 * par: 2 * par + 1], in0=gmax_all[:, tch:tch + 1],
                scalar1=1.0e-5, scalar2=1.0 / 127.0, op0=ALU.max, op1=ALU.mult))
            s.wait("vector", s2v)
            s2r = s.emit("vector", lambda h, par=par: h.reciprocal(
                out=qcolD2[:, 2 * par + 1: 2 * par + 2], in_=qcolD2[:, 2 * par: 2 * par + 1]))
            for hf in range(2):
                s.wait("sync", guh_guard)
                s.wait("sync", s.token("guw0"))
                s.wait("sync", s.token("guw1"))
                glds = [dma("sync", guh_[:, b4 * 512:(b4 + 1) * 512],
                            gu_scr_t[tch * 8 + hf * 4 + b4], "gur%d" % b4)
                        for b4 in range(4)]
                for g_ in glds:
                    s.wait("scalar", g_)
                s.wait("scalar", s2r)
                s.wait("scalar", lad2_prev[par])
                ab2 = s.emit("scalar", lambda h, guh_=guh_, par=par, qa2_=qa2_: h.activation(
                    out=qa2_, in_=guh_, func=AF.Abs, scale=qcolD2[:, 2 * par + 1: 2 * par + 2]))
                s.wait("scalar", qs_guard)
                sg2 = s.emit("scalar", lambda h, guh_=guh_, qs2_=qs2_: h.activation(
                    out=qs2_, in_=guh_, func=AF.Sign))
                guh_guard = sg2
                lt2 = ladder(qa2_, qd2_, stp_, 2048, ab2)
                s.wait("vector", lt2)
                s.wait("vector", sg2)
                s.wait("vector", tr2_prev[par])
                qv2 = s.emit("vector", lambda h, hf=hf, qbf2_=qbf2_, qa2_=qa2_, qs2_=qs2_: h.tensor_tensor(
                    out=qbf2_[:, hf * 2048:(hf + 1) * 2048], in0=qa2_, in1=qs2_,
                    op=ALU.mult))
                lad2_prev[par] = qv2
                qs_guard = qv2
            s.wait("tensor", lad2_prev[par])
            ev_q2 = None
            for g in range(8):
                s.wait("tensor", ev_q2 or tr2_prev[par])
                s.wait("tensor", tr_ps_guard)
                for k4 in range(4):
                    oc = g * 4 + k4
                    tt = s.emit("tensor", lambda h, oc=oc, k4=k4, qbf2_=qbf2_: h.transpose(
                        out=p_trB2[:, k4 * 128:(k4 + 1) * 128],
                        in_=qbf2_[:, oc * 128:(oc + 1) * 128],
                        identity=ident_bf[:]), inc=(1 if k4 == 3 else 0))
                s.wait("scalar", tt)
                s.wait("scalar", ev_q2)
                dst = xq2T_.rearrange("p (c t) -> p c t", t=128)[:, g * 4:(g + 1) * 4, :]
                ev_q2 = s.emit("scalar", lambda h, d=dst: h.activation(
                    out=d, in_=p_trB2[:].rearrange("p (c t) -> p c t", t=128), func=AF.Copy))
                tr_ps_guard = ev_q2
            tr2_prev[par] = s.token("tensor")
            xq2_done = ev_q2

            s.wait("tensor", xq2_done)
            s.wait("tensor", wdT_done)
            s.wait("tensor", dn_guard[par])
            for oc in range(32):
                for db in range(2):
                    mt = s.emit("tensor", lambda h, oc=oc, db=db, xq2T_=xq2T_, pdn_=pdn_: h.matmul(
                        out=pdn_[:, db * 512:(db + 1) * 512],
                        lhsT=xq2T_[:, oc * 128:(oc + 1) * 128],
                        rhs=wdT[:, oc * DM + db * 512: oc * DM + (db + 1) * 512],
                        start=(oc == 0), stop=(oc == 31)),
                        inc=(1 if oc == 31 and db == 1 else 0))
            s.wait("sync", x3ld2_g[par])
            s.wait("sync", s.token("x3w"))
            ldx3b = dma("sync", x3c2_, x3_scr_t[tch], "x3b%d" % par)

            def dn_tail(tch=tch, par=par, mt=mt, ldx3b=ldx3b, pdn_=pdn_,
                        outf_=outf_, x3c2_=x3c2_):
                nonlocal prev_ln4
                s.wait("vector", mt)
                s.wait("vector", f2b)
                s.wait("vector", prev_ln4)
                for g_ in out_guards[par]:
                    s.wait("vector", g_); s.wait("scalar", g_)
                dq = s.emit("vector", lambda h: h.tensor_scalar(
                    out=outf_, in0=pdn_, scalar1=f2_all[:, tch:tch + 1], scalar2=None,
                    op0=ALU.mult))
                dn_guard[par] = dq
                s.wait("vector", ldx3b)
                s.wait("vector", dq)
                radd = s.emit("vector", lambda h: h.tensor_tensor(
                    out=outf_, in0=outf_, in1=x3c2_, op=ALU.add))
                x3ld2_g[par] = radd
                s.wait("scalar", radd)
                s.wait("vector", radd)
                vt4, st4 = layernorm(outf_, outf_, None)
                prev_ln4 = st4
                s.wait("sync", st4)
                out_guards[par] = [dma("sync", out_t[tch], outf_, "outw%d" % par)]

            if pend_tail is not None:
                pend_tail()
            pend_tail = dn_tail
        pend_tail()

        s.barrier()
        es_db.close()
        es_d.close()

    block = es.enter_context(nc.Block())
    for e, attr in (("sync", block.sync), ("tensor", block.tensor), ("vector", block.vector),
                    ("scalar", block.scalar), ("gpsimd", block.gpsimd)):
        def make_body(ops):
            def body(hh):
                for f in ops:
                    f(hh)
            return body
        attr(make_body(s.q[e]))
    es.close()
    mybir.codegen_inst_isa_subclasses(nc)
    return nc


_NC_CACHE = {}


def _get_nc(stop_after):
    if stop_after not in _NC_CACHE:
        _NC_CACHE[stop_after] = build_program(stop_after)
    return _NC_CACHE[stop_after]


def kernel(**inputs):
    nc = _get_nc(STOP_AFTER)

    f32 = lambda a: np.ascontiguousarray(np.asarray(a, dtype=np.float32))
    common = {
        "W_attn": f32(inputs["L0_attn_W_attn"]),
        "b_attn": f32(inputs["L0_attn_b_attn"]),
        "W_proj": f32(inputs["L0_attn_W_proj"]),
        "b_proj": f32(inputs["L0_attn_b_proj"]),
        "Wg": f32(inputs["L0_ffn_Wg"]),
        "bg": f32(inputs["L0_ffn_bg"]),
        "W1": f32(inputs["L0_ffn_W1"]),
        "b1": f32(inputs["L0_ffn_b1"]),
        "W2": f32(inputs["L0_ffn_W2"]),
        "b2": f32(inputs["L0_ffn_b2"]),
        "W_in": f32(inputs["L1_attn_W_in"]),
        "W_xproj": f32(inputs["L1_attn_W_xproj"]),
        "W_dt": f32(inputs["L1_attn_W_dt"]),
        "b_dt": f32(inputs["L1_attn_b_dt"]),
        "A_log": f32(inputs["L1_attn_A_log"]),
        "Dvec": f32(inputs["L1_attn_D"]),
        "W_out": f32(inputs["L1_attn_W_out"]),
        "w_gate": f32(inputs["L1_ffn_w_gate"]),
        "w_up": f32(inputs["L1_ffn_w_up"]),
        "w_down": f32(inputs["L1_ffn_w_down"]),
    }
    p = np.arange(128).reshape(128, 1)
    jj = np.arange(T).reshape(1, T)
    alibi_base = np.where(p <= jj, (p - jj).astype(np.float32), np.float32(NEGBIG))
    common["alibi_base"] = alibi_base.astype(np.float32)
    common["ident_bf"] = np.eye(128, dtype=ml_dtypes.bfloat16)
    common["ident_f"] = np.eye(128, dtype=np.float32)
    common["ones_bf"] = np.ones((1, 128), dtype=ml_dtypes.bfloat16)
    common["ones_f"] = np.ones((1, 128), dtype=np.float32)

    x_full = f32(inputs["L0_attn_x"])
    in_maps = []
    for b in range(8):
        m = dict(common)
        m["x0"] = np.ascontiguousarray(x_full[b])
        in_maps.append(m)

    res = run_bass_kernel_spmd(nc, in_maps, core_ids=list(range(8)))
    out = np.stack([np.asarray(r["out"], dtype=np.float32) for r in res.results], axis=0)
    return out
